# version 02
# baseline (speedup 1.0000x reference)
import math
import numpy as np
from contextlib import ExitStack

import concourse.bass as bass
import concourse.mybir as mybir
from concourse.bass_utils import run_bass_kernel_spmd

F32 = mybir.dt.float32
BF16 = mybir.dt.bfloat16
U32 = mybir.dt.uint32
AF = mybir.ActivationFunctionType
ALU = mybir.AluOpType
AX = mybir.AxisListType

SEQ = 2048
DM = 2048
NT = 16
ND = 16
H = 16
KVH = 4
NEXP = 16
TOPK = 1024
NCORES = 8
CEXP = 1.0 / math.sqrt(128.0)
LAM_INIT = 0.2
NEG_BIG = -1.0e30
BIGV = 1.0e6
SEARCH_ITERS = 34
RMS_SC = 0.8 * math.sqrt(128.0)


def make_consts():
    c = {}
    c["ident"] = np.eye(128, dtype=np.float32)
    j = np.arange(0, 256, 2, dtype=np.float32)
    theta = (np.float32(1.0) / (np.float32(10000.0) ** (j / np.float32(256.0)))).astype(np.float32)
    pos = np.arange(SEQ, dtype=np.float32)
    ang = pos[:, None] * theta[None, :]
    cosv = np.cos(ang).astype(np.float32)
    sinv = np.sin(ang).astype(np.float32)
    CF = np.empty((SEQ, 256), dtype=np.float32)
    SF = np.empty((SEQ, 256), dtype=np.float32)
    CF[:, 0::2] = cosv
    CF[:, 1::2] = cosv
    SF[:, 0::2] = -sinv
    SF[:, 1::2] = sinv
    c["ropec"] = CF
    c["ropes"] = SF
    p = np.arange(128)[:, None]
    cc = np.arange(128)[None, :]
    c["emask"] = np.where(cc <= p, 0.0, NEG_BIG).astype(np.float32)
    kk = np.arange(128)[:, None]
    qq = np.arange(128)[None, :]
    c["maskt"] = (qq >= kk).astype(np.float32)
    c["maskcnt"] = (127.0 - np.arange(128, dtype=np.float32)).reshape(128, 1)
    pidx = np.arange(128)
    c["mgrp"] = (pidx[:, None] % 16 == pidx[None, :] % 16).astype(np.float32)
    c["repl16"] = (np.arange(16)[:, None] == pidx[None, :] % 16).astype(np.float32)
    c["ones_row"] = np.ones((1, 128), dtype=np.float32)
    mc = np.zeros((128, 4), dtype=np.float32)
    mc[:, 0] = 1e-5
    mc[:, 1] = LAM_INIT
    mc[:, 2] = math.log(0.8 * math.sqrt(128.0))
    mc[:, 3] = 2.0
    c["cmisc"] = mc
    return c


class Ctr:
    def __init__(self, ctx, nc, name):
        self.h = ctx.enter_context(nc.semaphore(name))
        self.n = 0


def build_program(dbg=False, stop=5, sub=3):
    import os as _os0
    KSKIP = set((_os0.environ.get("KSKIP") or "").split(","))
    import os as _os
    nc = bass.Bass("TRN2", target_bir_lowering=False,
                   detect_race_conditions=bool(_os.environ.get("KSTRICT")))
    ctx = ExitStack()

    dri = lambda n, sh: nc.dram_tensor(n, sh, F32, kind="ExternalInput").ap()
    x_in = dri("x_in", [SEQ, DM])
    wq = dri("wq", [DM, 4096])
    wk = dri("wk", [DM, 1024])
    wv = dri("wv", [DM, 512])
    wo = dri("wo", [DM, DM])
    lq1 = dri("lq1", [16, 128])
    lq2 = dri("lq2", [16, 128])
    lk1 = dri("lk1", [16, 128])
    lk2 = dri("lk2", [16, 128])
    rmsw = dri("rmsw", [128, 1])
    rw = dri("rw", [DM, 16])
    rb = dri("rb", [1, 16])
    ew = dri("ew", [DM, DM])
    eb = dri("eb", [1, DM])
    ident = dri("ident", [128, 128])
    ropec = dri("ropec", [SEQ, 256])
    ropes = dri("ropes", [SEQ, 256])
    emask = dri("emask", [128, 128])
    maskt = dri("maskt", [128, 128])
    maskcnt = dri("maskcnt", [128, 1])
    mgrp = dri("mgrp", [128, 128])
    repl16 = dri("repl16", [16, 128])
    ones_row = dri("ones_row", [1, 128])
    selr = dri("selr", [128, 1])
    cmisc = dri("cmisc", [128, 4])

    y_out = nc.dram_tensor("y_out", [SEQ, DM], F32, kind="ExternalOutput").ap()
    ik = dict(kind="ExternalOutput") if dbg else {}
    x1r = nc.dram_tensor("x1r", [SEQ, DM], F32, **ik).ap()
    x2r = nc.dram_tensor("x2r", [SEQ, DM], F32, **ik).ap()
    attr = nc.dram_tensor("attr", [DM, SEQ], BF16, **ik).ap()
    cc_in = [nc.dram_tensor(f"cc_in{k}", [16, SEQ], F32) for k in range(2)]
    cc_out = [nc.dram_tensor(f"cc_out{k}", [128, SEQ], F32) for k in range(2)]

    sbt = lambda n, sh, dt: ctx.enter_context(nc.sbuf_tensor(n, sh, dt))
    pst = lambda n, sh, dt: ctx.enter_context(nc.psum_tensor(n, sh, dt))

    BIG = sbt("BIG", [128, ND * SEQ], BF16)
    WBUF = sbt("WBUF", [128, ND * DM], BF16)
    KT0 = 0
    WSLOT = 16384
    VS0 = 24576
    SCR = sbt("SCR", [128, 8192], BF16)
    qTh = sbt("qTh", [128, 2 * SEQ], BF16)
    xt = [sbt(f"xt{i}", [128, DM], F32) for i in range(2)]
    zt = sbt("zt", [128, DM], F32)
    xn = sbt("xn", [128, DM], BF16)
    e0 = sbt("e0", [128, SEQ], BF16)
    e1 = sbt("e1", [128, SEQ], BF16)
    sall = qTh[:, :].bitcast(F32)
    snapt = sbt("snapt", [128, SEQ], F32)
    qrw = sbt("qrw", [128, 1024], BF16)
    qrt = sbt("qrt", [128, 1024], BF16)
    uneg = qrw
    dchk = qrt
    attsc = sbt("attsc", [128, 640], BF16)
    qro = attsc[:, 0:512]
    ST = snapt[0:16, :]
    lqk = [snapt[0:16, i * 128:(i + 1) * 128] for i in range(4)]
    sown = sbt("sown", [128, 64], F32)
    stat = sbt("stat", [128, 64], F32)
    srch = sbt("srch", [128, 32], F32)
    srchu = sbt("srchu", [128, 1], U32)
    srow = sbt("srow", [1, 16], F32)
    wcol = sbt("wcol", [128, 16], F32)
    sstat = sbt("sstat", [128, 64], F32)
    lamc = sbt("lamc", [128, 16], F32)
    lsc = sbt("lsc", [16, 8], F32)
    c_id_bf = sbt("c_id_bf", [128, 128], BF16)
    c_id_f = sbt("c_id_f", [128, 128], F32)
    c_emask = sbt("c_emask", [128, 128], BF16)
    c_maskt = sbt("c_maskt", [128, 128], BF16)
    c_mcnt = sbt("c_mcnt", [128, 1], F32)
    c_mgrp = sbt("c_mgrp", [128, 128], F32)
    c_repl = sbt("c_repl", [16, 128], F32)
    c_ones = sbt("c_ones", [1, 128], F32)
    c_selr = sbt("c_selr", [128, 1], F32)
    c_rb = sbt("c_rb", [1, 16], F32)
    c_ebbf = e1[0:1, 0:DM]
    jnk_bf = snapt[:, 0:1024].bitcast(BF16)
    jnk_bf2 = snapt[:, 1024:2048].bitcast(BF16)
    c_rmsw = sbt("c_rmsw", [128, 1], F32)
    c_misc = sbt("c_misc", [128, 4], F32)
    rwsb = sbt("rwsb", [128, ND * 16], BF16)
    c_onesb = c_maskt[0:1, :]
    c_rope_c = SCR[:, 0:NT * 256]
    c_rope_s = SCR[:, NT * 256:2 * NT * 256]

    pE0 = pst("pE0", [128, 1024], F32)
    pE1 = pst("pE1", [128, 1024], F32)
    pA = pst("pA", [128, 512], F32)
    pB = pst("pB", [128, 512], F32)
    pTR = pst("pTR", [128, 2048], BF16)

    pS_cnt = pE0[:, 0:1]
    pS_thx = pE0[:, 1:2]
    pS_lam = pE0[:, 2:18]
    pS_w = pE0[:, 18:34]
    pS_st = [pE0[0:16, 34:162], pE0[0:16, 162:290]]
    pS_r0 = pE0[0:1, 290:418]
    pS_r16 = pE0[0:16, 418:419]
    pS_lr = pE0[0:1, 420:436]

    dma = {n: Ctr(ctx, nc, n) for n in (
        "smc", "sx0", "sx1", "ssl0", "ssl1", "gmc", "geb", "gst", "gsall",
        "grope", "gwk", "gwv", "gwq0", "gwq1", "gzs", "gat0", "gat1")}
    SX = [dma["sx0"], dma["sx1"]]
    SSL = [dma["ssl0"], dma["ssl1"]]
    GWQ = [dma["gwq0"], dma["gwq1"]]
    GAT = [dma["gat0"], dma["gat1"]]
    ccs = Ctr(ctx, nc, "ccs")
    tmm = Ctr(ctx, nc, "tmm")
    pml = Ctr(ctx, nc, "pml")
    vml = Ctr(ctx, nc, "vml")
    sml = Ctr(ctx, nc, "sml")

    prog = {k: [] for k in ("sync", "gpsimd", "tensor", "vector", "scalar")}
    S_ = prog["sync"].append
    G_ = prog["gpsimd"].append
    T_ = prog["tensor"].append
    V_ = prog["vector"].append
    C_ = prog["scalar"].append

    guard = {}

    def need(e, g):
        if g is None:
            return
        if isinstance(g, list):
            for gg in g:
                need(e, gg)
        else:
            e.wait_ge(g[0], g[1])

    def gget(key):
        return guard.get(key)

    state_tr = [0]

    def _c0(e):
        e.dma_start(out=c_id_f[:], in_=ident).then_inc(dma["smc"].h, 16)
        e.dma_start(out=c_mcnt[:], in_=maskcnt).then_inc(dma["smc"].h, 16)
        e.dma_start(out=c_mgrp[:], in_=mgrp).then_inc(dma["smc"].h, 16)
        e.dma_start(out=c_repl[:], in_=repl16).then_inc(dma["smc"].h, 16)
        e.dma_start(out=c_ones[:], in_=ones_row).then_inc(dma["smc"].h, 16)
        e.dma_start(out=c_rb[:], in_=rb).then_inc(dma["smc"].h, 16)
        e.dma_start(out=c_selr[:], in_=selr).then_inc(dma["smc"].h, 16)
        e.dma_start(out=c_rmsw[:], in_=rmsw).then_inc(dma["smc"].h, 16)
        e.dma_start(out=c_misc[:], in_=cmisc).then_inc(dma["smc"].h, 16)
    S_(_c0)
    dma["smc"].n += 9 * 16
    SL_CONST = dma["smc"].n

    def _c1(e):
        e.dma_start(out=c_id_bf[:], in_=ident).then_inc(dma["gmc"].h, 16)
        e.dma_start(out=c_emask[:], in_=emask).then_inc(dma["gmc"].h, 16)
        e.dma_start(out=c_maskt[:], in_=maskt).then_inc(dma["gmc"].h, 16)
        e.dma_start(out=rwsb[:].rearrange("p (t c) -> p t c", c=16),
                        in_=rw.rearrange("(t p) c -> p t c", p=128)).then_inc(dma["gmc"].h, 16)
        e.dma_start(out=WBUF[:].rearrange("p (t c) -> p t c", c=2048),
                        in_=ew.rearrange("(t p) c -> p t c", p=128)).then_inc(dma["gmc"].h, 16)
    G_(_c1)
    dma["gmc"].n += 5 * 16
    GL_CONST = dma["gmc"].n

    def ln_stats(xtile, waits, par):
        b = par * 16

        def f(e, xtile=xtile, waits=list(waits) + [gget(("stats", par))], b=b):
            for g in waits:
                need(e, g)
            e.activation(out=jnk_bf, in_=xtile[:], func=AF.Identity,
                         accum_out=stat[:, b + 0:b + 1])
            e.activation(out=jnk_bf2, in_=xtile[:], func=AF.Square,
                         accum_out=stat[:, b + 1:b + 2])
            e.copy(out=stat[:, b + 10:b + 11],
                   in_=stat[:, b + 1:b + 2]).then_inc(sml.h, 1)
        C_(f)
        sml.n += 1
        return sml.n

    def ln_normalize(xtile, s_stats, par):
        b = par * 16

        def fA(e, s_stats=s_stats, b=b, g=gget(("statA", par))):
            e.wait_ge(sml.h, s_stats)
            need(e, g)
            e.activation(out=stat[:, b + 2:b + 3], in_=stat[:, b + 0:b + 1],
                         func=AF.Identity, scale=1.0 / DM)
            e.activation(out=stat[:, b + 3:b + 4], in_=stat[:, b + 1:b + 2],
                         func=AF.Identity, scale=1.0 / DM).then_inc(sml.h, 1)
        C_(fA)
        sml.n += 1
        s_A = sml.n

        def fB(e, s_A=s_A, b=b, g=gget(("statB", par))):
            e.wait_ge(sml.h, s_A)
            need(e, g)
            e.tensor_scalar(out=stat[:, b + 4:b + 5], in0=stat[:, b + 2:b + 3],
                            scalar1=stat[:, b + 2:b + 3],
                            scalar2=stat[:, b + 3:b + 4],
                            op0=ALU.mult, op1=ALU.subtract).then_inc(vml.h, 1)
        V_(fB)
        vml.n += 1
        v_B = vml.n
        guard[("stats", par)] = (vml.h, v_B)

        def fC(e, v_B=v_B, b=b):
            e.wait_ge(vml.h, v_B)
            e.activation(out=stat[:, b + 5:b + 6], in_=stat[:, b + 4:b + 5],
                         func=AF.Ln, scale=-1.0,
                         bias=c_misc[:, 0:1]).then_inc(sml.h, 1)
        C_(fC)
        sml.n += 1
        s_C = sml.n

        def fD(e, s_C=s_C, b=b):
            e.wait_ge(sml.h, s_C)
            e.tensor_copy(out=stat[:, b + 7:b + 8],
                          in_=stat[:, b + 5:b + 6]).then_inc(vml.h, 1)
        V_(fD)
        vml.n += 1
        v_D = vml.n

        def fE(e, v_D=v_D, b=b):
            e.wait_ge(vml.h, v_D)
            e.activation(out=stat[:, b + 6:b + 7], in_=stat[:, b + 7:b + 8],
                         func=AF.Exp, scale=-0.5).then_inc(sml.h, 1)
        C_(fE)
        sml.n += 1
        s_E = sml.n
        guard[("statB", par)] = (sml.h, s_E)

        def fF(e, xtile=xtile, s_E=s_E, b=b, g=gget("xn")):
            e.wait_ge(sml.h, s_E)
            need(e, g)
            e.tensor_scalar(out=xn[:], in0=xtile[:], scalar1=stat[:, b + 2:b + 3],
                            scalar2=stat[:, b + 6:b + 7], op0=ALU.subtract,
                            op1=ALU.mult).then_inc(vml.h, 1)
        V_(fF)
        vml.n += 1
        guard[("statA", par)] = (vml.h, vml.n)
        return vml.n

    def transpose_to_big(i, v_xn):
        for bq in range(4):
            half = state_tr[0] % 2
            state_tr[0] += 1

            def ft(e, bq=bq, half=half, v_xn=v_xn, g=gget(("pTR", half))):
                e.wait_ge(vml.h, v_xn)
                need(e, g)
                for j in range(4):
                    dt = bq * 4 + j
                    ins = e.transpose(pTR[:, half * 1024 + j * 128:half * 1024 + (j + 1) * 128],
                                      xn[:, dt * 128:(dt + 1) * 128], c_id_bf[:])
                ins.then_inc(tmm.h, 1)
            T_(ft)
            tmm.n += 1
            t_tr = tmm.n

            def fe(e, bq=bq, half=half, t_tr=t_tr, i=i):
                e.wait_ge(tmm.h, t_tr)
                out_ap = BIG[:, :].rearrange("p (t q) -> p t q", q=SEQ)[
                    :, bq * 4:(bq + 1) * 4, i * 128:(i + 1) * 128]
                e.activation(out=out_ap, in_=pTR[:, half * 1024:half * 1024 + 512],
                             func=AF.Copy).then_inc(sml.h, 1)
            C_(fe)
            sml.n += 1
            guard[("pTR", half)] = (sml.h, sml.n)
        guard["xn"] = (tmm.h, tmm.n)
        return sml.n

    def barrier():
        t1, v1, s1, p1 = tmm.n, vml.n, sml.n, pml.n
        dsnap = [(d.h, d.n) for d in dma.values() if d.n > 0]

        def wt(e, t1=t1, v1=v1, s1=s1, p1=p1, dsnap=list(dsnap)):
            e.wait_ge(tmm.h, t1)
            e.wait_ge(vml.h, v1)
            e.wait_ge(sml.h, s1)
            if p1 > 0:
                e.wait_ge(pml.h, p1)
            for h, n in dsnap:
                e.wait_ge(h, n)
        for EM in (T_, V_, C_, G_, S_):
            EM(wt)
        guard.clear()

    def ffn_phase(call, src_res, dst_res, fuse_next_ln, big_tile_s, xt_free):
        cin, cout = cc_in[call], cc_out[call]

        def feb(e, call=call):
            e.dma_start(out=c_ebbf, in_=eb).then_inc(dma["geb"].h, 16)
            if call == 1:
                e.dma_start(out=WBUF[:].rearrange("p (t c) -> p t c", c=2048),
                            in_=ew.rearrange("(t p) c -> p t c", p=128)
                            ).then_inc(dma["geb"].h, 16)
        G_(feb)
        dma["geb"].n += 16 * (1 + (1 if call == 1 else 0))
        g_eb = dma["geb"].n

        st_prev = {}
        for i in range(NT):
            pR = pA if i % 2 == 0 else pB
            par = i % 2
            sb = par * 16

            def fr(e, i=i, pR=pR, bs=big_tile_s[i], g=gget(id(pR))):
                e.wait_ge(sml.h, bs)
                if i == 0:
                    e.wait_ge(dma["gmc"].h, GL_CONST)
                    e.wait_ge(dma["smc"].h, SL_CONST)
                need(e, g)
                for dt in range(ND):
                    e.matmul(pR[:, 0:16],
                             BIG[:, dt * SEQ + i * 128:dt * SEQ + (i + 1) * 128],
                             rwsb[:, dt * 16:(dt + 1) * 16],
                             start=(dt == 0), stop=False)
                e.matmul(pR[:, 0:16], c_ones[:], c_rb[:], start=False,
                         stop=True).then_inc(tmm.h, 1)
            T_(fr)
            tmm.n += 1
            t_lg = tmm.n

            def fx(e, pR=pR, t_lg=t_lg, sb=sb, par=par,
                   g=[gget(("rtacc", par))]):
                e.wait_ge(tmm.h, t_lg)
                need(e, g)
                e.activation(out=sown[:, sb:sb + 16], in_=pR[:, 0:16], func=AF.Exp,
                             accum_out=stat[:, 32 + par:33 + par])
                e.copy(out=stat[:, 40 + par:41 + par],
                       in_=stat[:, 32 + par:33 + par]).then_inc(sml.h, 1)
            C_(fx)
            sml.n += 1
            s_exp = sml.n
            guard[id(pR)] = (sml.h, sml.n)

            def fn(e, s_exp=s_exp, par=par):
                e.wait_ge(sml.h, s_exp)
                e.reciprocal(out=stat[:, 34 + par:35 + par],
                             in_=stat[:, 32 + par:33 + par]).then_inc(vml.h, 1)
            V_(fn)
            vml.n += 1
            v_sm = vml.n
            guard[("rtacc", par)] = (vml.h, v_sm)

            def fnm(e, v_sm=v_sm, sb=sb, par=par, g=gget(("sownn", par))):
                e.wait_ge(vml.h, v_sm)
                need(e, g)
                e.activation(out=sown[:, 32 + sb:32 + sb + 16],
                             in_=sown[:, sb:sb + 16], func=AF.Identity,
                             scale=stat[:, 34 + par:35 + par]).then_inc(sml.h, 1)
            C_(fnm)
            sml.n += 1
            s_nm = sml.n

            def fst(e, par=par, sb=sb, s_nm=s_nm, g=gget("bankS")):
                e.wait_ge(sml.h, s_nm)
                need(e, g)
                e.transpose(pS_st[par], sown[:, 32 + sb:32 + sb + 16],
                            c_id_f[:]).then_inc(tmm.h, 1)
            T_(fst)
            tmm.n += 1
            t_st = tmm.n

            def fse(e, i=i, par=par, t_st=t_st):
                e.wait_ge(tmm.h, t_st)
                e.activation(out=ST[0:16, i * 128:(i + 1) * 128], in_=pS_st[par],
                             func=AF.Copy).then_inc(sml.h, 1)
            C_(fse)
            sml.n += 1
            guard[("sownn", par)] = (tmm.h, t_st)
            guard["bankS"] = (sml.h, sml.n)
        st_all = sml.n

        def fcc(e, st_all=st_all, cin=cin, cout=cout, gs0=dma["gst"].n, cc0=ccs.n):
            e.wait_ge(sml.h, st_all)
            e.dma_start(out=cin[:, :], in_=ST[0:16, :]).then_inc(dma["gst"].h, 16)
            e.wait_ge(dma["gst"].h, gs0 + 16)
            e.collective_compute(
                "AllGather", ALU.bypass,
                replica_groups=[list(range(NCORES))],
                ins=[cin.ap()], outs=[cout.ap()],
            ).then_inc(ccs.h, 1)
            e.wait_ge(ccs.h, cc0 + 1)
            e.dma_start(out=sall[:], in_=cout[:, :]).then_inc(dma["gsall"].h, 16)
        G_(fcc)
        dma["gst"].n += 16
        ccs.n += 1
        dma["gsall"].n += 16
        g_sall = dma["gsall"].n

        def fs0(e, g_sall=g_sall):
            e.wait_ge(dma["gsall"].h, g_sall)
            e.memset(srch[:, :], 0.0)
            e.memset(srch[:, 2:3], 1.0)
            e.memset(srch[:, 4:5], 0.5).then_inc(vml.h, 1)
        V_(fs0)
        vml.n += 1

        for it in range(SEARCH_ITERS):
            par = it % 2
            nxt = 1 - par
            v_mid = vml.n

            def fmb(e, v_mid=v_mid, par=par):
                e.wait_ge(vml.h, v_mid)
                e.copy(out=srch[:, 12:13], in_=srch[:, 4 + par:5 + par]).then_inc(sml.h, 1)
            C_(fmb)
            sml.n += 1
            s_mid = sml.n

            def fsa(e, s_mid=s_mid):
                e.wait_ge(sml.h, s_mid)
                e.tensor_scalar(out=e0[:], in0=sall[:], scalar1=srch[:, 12:13],
                                scalar2=0.0, op0=ALU.is_ge, op1=ALU.add,
                                accum_out=srch[:, 13:14])
                e.tensor_copy(out=srch[:, 14:15], in_=srch[:, 13:14]).then_inc(vml.h, 1)
            V_(fsa)
            vml.n += 1
            v_cnt = vml.n

            def fsb(e, v_cnt=v_cnt, g=gget("bankS")):
                e.wait_ge(vml.h, v_cnt)
                need(e, g)
                e.matmul(pS_cnt, c_mgrp[:], srch[:, 13:14], start=True,
                         stop=True).then_inc(tmm.h, 1)
            T_(fsb)
            tmm.n += 1
            t_cnt = tmm.n

            def fsA(e, t_cnt=t_cnt, par=par):
                e.wait_ge(tmm.h, t_cnt)
                e.tensor_scalar(out=srch[:, 8:9], in0=pS_cnt, scalar1=float(TOPK),
                                scalar2=None, op0=ALU.is_ge)
                e.tensor_tensor(out=srch[:, 6:7], in0=srch[:, 4 + par:5 + par],
                                in1=srch[:, 0 + par:1 + par], op=ALU.subtract)
                e.tensor_tensor(out=srch[:, 7:8], in0=srch[:, 2 + par:3 + par],
                                in1=srch[:, 4 + par:5 + par],
                                op=ALU.subtract).then_inc(vml.h, 1)
            V_(fsA)
            vml.n += 1
            v_A = vml.n
            guard["bankS"] = (vml.h, v_A)

            def fsB(e, v_A=v_A):
                e.wait_ge(vml.h, v_A)
                e.copy(out=srch[:, 9:10], in_=srch[:, 8:9])
                e.copy(out=srch[:, 10:11], in_=srch[:, 6:7])
                e.copy(out=srch[:, 11:12], in_=srch[:, 7:8]).then_inc(sml.h, 1)
            C_(fsB)
            sml.n += 1
            s_B = sml.n

            def fsC(e, s_B=s_B, par=par, nxt=nxt):
                e.wait_ge(sml.h, s_B)
                e.scalar_tensor_tensor(out=srch[:, 0 + nxt:1 + nxt],
                                       in0=srch[:, 10:11], scalar=srch[:, 9:10],
                                       in1=srch[:, 0 + par:1 + par],
                                       op0=ALU.mult, op1=ALU.add)
                e.scalar_tensor_tensor(out=srch[:, 2 + nxt:3 + nxt],
                                       in0=srch[:, 11:12], scalar=srch[:, 9:10],
                                       in1=srch[:, 4 + par:5 + par],
                                       op0=ALU.mult, op1=ALU.add).then_inc(vml.h, 1)
            V_(fsC)
            vml.n += 1
            v_C = vml.n

            def fsD(e, v_C=v_C, nxt=nxt):
                e.wait_ge(vml.h, v_C)
                e.activation(out=srch[:, 10:11], in_=srch[:, 0 + nxt:1 + nxt],
                             func=AF.Identity, scale=0.5)
                e.activation(out=srch[:, 11:12], in_=srch[:, 2 + nxt:3 + nxt],
                             func=AF.Identity, scale=0.5).then_inc(sml.h, 1)
            C_(fsD)
            sml.n += 1
            s_D = sml.n

            def fsE(e, s_D=s_D, nxt=nxt):
                e.wait_ge(sml.h, s_D)
                e.tensor_tensor(out=srch[:, 4 + nxt:5 + nxt], in0=srch[:, 10:11],
                                in1=srch[:, 11:12], op=ALU.add).then_inc(vml.h, 1)
            V_(fsE)
            vml.n += 1
        LOF = SEARCH_ITERS % 2

        v_lo = vml.n

        def flb(e, v_lo=v_lo, LOF=LOF):
            e.wait_ge(vml.h, v_lo)
            e.copy(out=srch[:, 15:16], in_=srch[:, LOF:LOF + 1]).then_inc(sml.h, 1)
        C_(flb)
        sml.n += 1
        s_lo = sml.n

        def fsn1(e, s_lo=s_lo):
            e.wait_ge(sml.h, s_lo)
            e.scalar_tensor_tensor(out=e0[:], in0=sall[:], scalar=srch[:, 15:16],
                                   in1=sall[:], op0=ALU.is_ge,
                                   op1=ALU.bypass).then_inc(vml.h, 1)
        V_(fsn1)
        vml.n += 1
        v_n1 = vml.n

        def fsn2(e, v_n1=v_n1):
            e.wait_ge(vml.h, v_n1)
            e.activation(out=xn[:], in_=e0[:], func=AF.Identity, scale=-2.0,
                         bias=c_misc[:, 3:4]).then_inc(sml.h, 1)
        C_(fsn2)
        sml.n += 1
        s_n2 = sml.n

        def fsn3(e, s_n2=s_n2):
            e.wait_ge(sml.h, s_n2)
            e.tensor_tensor(out=snapt[:], in0=sall[:], in1=xn[:], op=ALU.add)
            e.tensor_reduce(out=srch[:, 6:7], in_=snapt[:], axis=AX.X,
                            op=ALU.min).then_inc(vml.h, 1)
        V_(fsn3)
        vml.n += 1
        v_cm = vml.n

        def fst2(e, v_cm=v_cm, g=gget("bankS")):
            e.wait_ge(vml.h, v_cm)
            need(e, g)
            e.transpose(pS_r0, srch[:, 6:7], c_id_f[:]).then_inc(tmm.h, 1)
        T_(fst2)
        tmm.n += 1
        t_cm = tmm.n

        def fmx(e, t_cm=t_cm):
            e.wait_ge(tmm.h, t_cm)
            e.tensor_reduce(out=srow[:], in_=pS_r0.rearrange("p (r e) -> p e r", r=8),
                            axis=AX.X, op=ALU.min).then_inc(vml.h, 1)
        V_(fmx)
        vml.n += 1
        guard["bankS"] = (vml.h, vml.n)
        v_th = vml.n

        def fth(e, v_th=v_th, g=gget("bankS")):
            e.wait_ge(vml.h, v_th)
            need(e, g)
            e.transpose(pS_r16, srow[:], c_id_f[0:1, 0:1]).then_inc(tmm.h, 1)
        T_(fth)
        tmm.n += 1
        t_th = tmm.n

        def fthc(e, t_th=t_th):
            e.wait_ge(tmm.h, t_th)
            e.tensor_copy(out=srch[0:16, 7:8], in_=pS_r16).then_inc(vml.h, 1)
        V_(fthc)
        vml.n += 1
        guard["bankS"] = (vml.h, vml.n)
        v_thc = vml.n

        def fthx(e, v_thc=v_thc, g=gget("bankS")):
            e.wait_ge(vml.h, v_thc)
            need(e, g)
            e.matmul(pS_thx, c_repl[:], srch[0:16, 7:8], start=True,
                     stop=True).then_inc(tmm.h, 1)
        T_(fthx)
        tmm.n += 1
        t_thx = tmm.n

        def fw0(e, t_thx=t_thx):
            e.wait_ge(tmm.h, t_thx)
            e.activation(out=srch[:, 8:9], in_=pS_thx, func=AF.Copy).then_inc(sml.h, 1)
        C_(fw0)
        sml.n += 1
        s_th = sml.n
        guard["bankS"] = (sml.h, sml.n)

        def fw1(e, s_th=s_th):
            e.wait_ge(sml.h, s_th)
            e.scalar_tensor_tensor(out=snapt[:], in0=sall[:], scalar=srch[:, 8:9],
                                   in1=sall[:], op0=ALU.is_ge,
                                   op1=ALU.mult).then_inc(vml.h, 1)
        V_(fw1)
        vml.n += 1
        v_msk = vml.n

        def fw2(e, v_msk=v_msk, g=gget("bankS")):
            e.wait_ge(vml.h, v_msk)
            need(e, g)
            for i in range(NT):
                ins = e.matmul(pS_w[:, i:i + 1], snapt[:, i * 128:(i + 1) * 128],
                               c_selr[:], start=True, stop=True)
            ins.then_inc(tmm.h, 1)
        T_(fw2)
        tmm.n += 1
        t_w = tmm.n

        def fw3(e, t_w=t_w):
            e.wait_ge(tmm.h, t_w)
            e.activation(out=wcol[:], in_=pS_w, func=AF.Identity,
                         scale=0.5).then_inc(sml.h, 1)
        C_(fw3)
        sml.n += 1
        guard["bankS"] = (sml.h, sml.n)
        v_w = None
        s_w = sml.n

        y_evac = {}
        apply_v = dict(xt_free)
        store_g = {}
        big_next = [None] * NT
        zt_guard = [gget("zt"), gget("zt_store")]

        def emit_y(i):
            slot = (i % 4) * 2048
            for cch in range(4):
                pY = pA if (i * 4 + cch) % 2 == 0 else pB

                def fy(e, i=i, cch=cch, pY=pY, bs=big_tile_s[i], g=gget(id(pY)),
                       geb=(g_eb if i == 0 and cch == 0 else None)):
                    e.wait_ge(sml.h, bs)
                    if geb is not None:
                        e.wait_ge(dma["geb"].h, geb)
                    need(e, g)
                    for dt in range(ND):
                        e.matmul(pY[:],
                                 BIG[:, dt * SEQ + i * 128:dt * SEQ + (i + 1) * 128],
                                 WBUF[:, dt * DM + cch * 512:dt * DM + (cch + 1) * 512],
                                 start=(dt == 0), stop=False)
                    e.matmul(pY[:], c_onesb, c_ebbf[:, cch * 512:(cch + 1) * 512],
                             start=False, stop=True).then_inc(tmm.h, 1)
                T_(fy)
                tmm.n += 1
                t_y = tmm.n

                def fyev(e, i=i, cch=cch, pY=pY, t_y=t_y, slot=slot,
                         w=apply_v.get(i - 4)):
                    e.wait_ge(tmm.h, t_y)
                    if w is not None:
                        e.wait_ge(vml.h, w)
                    e.activation(out=SCR[:, slot + cch * 512:slot + (cch + 1) * 512],
                                 in_=pY[:], func=AF.Copy).then_inc(sml.h, 1)
                C_(fyev)
                sml.n += 1
                guard[id(pY)] = (sml.h, sml.n)
            y_evac[i] = sml.n

        def emit_apply(i):
            slot = (i % 4) * 2048
            xb = xt[i % 2]

            def fxl(e, i=i, xb=xb, w=apply_v.get(i - 2), sh=SX[i % 2].h):
                if w is not None:
                    e.wait_ge(vml.h, w)
                e.dma_start(out=xb[:], in_=src_res[i * 128:(i + 1) * 128, :]
                            ).then_inc(sh, 16)
            S_(fxl)
            SX[i % 2].n += 16
            sl_x = (SX[i % 2].h, SX[i % 2].n)

            def fap(e, i=i, xb=xb, slot=slot, sl_x=sl_x, sy=max(y_evac[i], s_w),
                    zg=list(zt_guard)):
                e.wait_ge(sml.h, sy)
                e.wait_ge(sl_x[0], sl_x[1])
                for g in zg:
                    need(e, g)
                e.scalar_tensor_tensor(out=zt[:], in0=SCR[:, slot:slot + 2048],
                                       scalar=wcol[:, i:i + 1], in1=xb[:],
                                       op0=ALU.mult, op1=ALU.add).then_inc(vml.h, 1)
            V_(fap)
            vml.n += 1
            v_z = vml.n
            apply_v[i] = v_z

            def fzs(e, i=i, v_z=v_z):
                e.wait_ge(vml.h, v_z)
                e.dma_start(out=dst_res[i * 128:(i + 1) * 128, :], in_=zt[:]
                            ).then_inc(dma["gzs"].h, 16)
            G_(fzs)
            dma["gzs"].n += 16
            g_z = dma["gzs"].n
            store_g[i] = g_z

            if fuse_next_ln:
                s_st = ln_stats(zt, [(vml.h, v_z)], i % 2)
                v_xn = ln_normalize(zt, s_st, i % 2)
                s_big = transpose_to_big(i, v_xn)
                big_next[i] = s_big
                zt_guard[0] = (vml.h, v_xn)
                zt_guard[1] = (dma["gzs"].h, g_z)
            else:
                zt_guard[0] = (dma["gzs"].h, g_z)
                zt_guard[1] = None

        for i in range(NT):
            emit_y(i)
            if i >= 3:
                emit_apply(i - 3)
        for i in range(NT - 3, NT):
            emit_apply(i)
        guard["zt"] = zt_guard[0]
        guard["zt_store"] = zt_guard[1]
        return big_next

    def initial_ln():
        big_tile_s = [None] * NT
        xn_done = {}
        for i in range(NT):
            xb = xt[i % 2]

            def fl(e, i=i, xb=xb, w=xn_done.get(i - 2), sh=SX[i % 2].h):
                if w is not None:
                    e.wait_ge(vml.h, w)
                e.dma_start(out=xb[:], in_=x_in[i * 128:(i + 1) * 128, :]
                            ).then_inc(sh, 16)
            S_(fl)
            SX[i % 2].n += 16
            s_st = ln_stats(xb, [(SX[i % 2].h, SX[i % 2].n)], i % 2)
            if sub < 2:
                xn_done[i] = vml.n
                big_tile_s[i] = sml.n
                continue
            v_xn = ln_normalize(xb, s_st, i % 2)
            xn_done[i] = v_xn
            if sub < 3:
                big_tile_s[i] = sml.n
                continue
            big_tile_s[i] = transpose_to_big(i, v_xn)
        xt_free = {-2: xn_done[NT - 2], -1: xn_done[NT - 1]}
        return big_tile_s, xt_free

    def kq_rope_group(i, pP, colw, dst_col_fn, t_mm, rope_state):
        ngr = colw // 256

        def fr1(e, i=i, ngr=ngr, pP=pP, t_mm=t_mm, gr=rope_state.get("g"),
                w=rope_state.get("v2")):
            e.wait_ge(tmm.h, t_mm)
            e.wait_ge(dma["grope"].h, gr)
            if w is not None:
                e.wait_ge(pml.h, w)
            CF = c_rope_c[:, i * 256:(i + 1) * 256]
            SF3 = c_rope_s[:, i * 256:(i + 1) * 256].rearrange(
                "p (j t) -> p j t", t=2)
            for g2 in range(ngr):
                b = g2 * 256
                pg3 = pP[:, b:b + 256].rearrange("p (j t) -> p j t", t=2)
                q3 = qrt[:, 512 + b:512 + b + 256].rearrange("p (j t) -> p j t", t=2)
                e.tensor_tensor(out=qrt[:, b:b + 256], in0=pP[:, b:b + 256],
                                in1=CF, op=ALU.mult)
                e.tensor_tensor(out=q3[:, :, 0:1], in0=pg3[:, :, 1:2],
                                in1=SF3[:, :, 0:1], op=ALU.mult)
                ins = e.tensor_tensor(out=q3[:, :, 1:2], in0=pg3[:, :, 0:1],
                                      in1=SF3[:, :, 1:2], op=ALU.mult)
            ins.then_inc(vml.h, 1)
        V_(fr1)
        vml.n += 1
        v_r1 = vml.n

        def fr2(e, colw=colw, v_r1=v_r1, w=rope_state.get("t")):
            e.wait_ge(vml.h, v_r1)
            if w is not None:
                e.wait_ge(tmm.h, w)
            e.tensor_tensor(out=qro[:, 0:colw], in0=qrt[:, 0:colw],
                            in1=qrt[:, 512:512 + colw],
                            op=ALU.add).then_inc(pml.h, 1)
        G_(fr2)
        pml.n += 1
        p_r2 = pml.n
        rope_state["v2"] = p_r2

        ntr = colw // 128
        half = state_tr[0] % 2
        state_tr[0] += 1

        def ftr(e, ntr=ntr, half=half, p_r2=p_r2, g=gget(("pTR", half))):
            e.wait_ge(pml.h, p_r2)
            need(e, g)
            for j in range(ntr):
                ins = e.transpose(pTR[:, half * 1024 + j * 128:half * 1024 + (j + 1) * 128],
                                  qro[:, j * 128:(j + 1) * 128], c_id_bf[:])
            ins.then_inc(tmm.h, 1)
        T_(ftr)
        tmm.n += 1
        t_tr = tmm.n

        def fte(e, ntr=ntr, half=half, t_tr=t_tr, dst=dst_col_fn):
            e.wait_ge(tmm.h, t_tr)
            for j in range(ntr):
                e.activation(out=dst(j), in_=pTR[:, half * 1024 + j * 128:
                                               half * 1024 + (j + 1) * 128],
                             func=AF.Copy).then_inc(sml.h, 1)
        C_(fte)
        sml.n += ntr
        guard[("pTR", half)] = (sml.h, sml.n)
        return sml.n, v_r1

    def attention_phase(nheads=H, nqi=NT):
        rope_state = {}
        state_u = {}
        state_d = {}
        state_att = {}
        state_att2 = {}

        def flq(e):
            e.dma_start(out=lqk[0], in_=lq1).then_inc(dma["smc"].h, 16)
            e.dma_start(out=lqk[1], in_=lq2).then_inc(dma["smc"].h, 16)
            e.dma_start(out=lqk[2], in_=lk1).then_inc(dma["smc"].h, 16)
            e.dma_start(out=lqk[3], in_=lk2).then_inc(dma["smc"].h, 16)
        S_(flq)
        dma["smc"].n += 64
        sl_lqk = dma["smc"].n

        def fl1(e, sl_lqk=sl_lqk):
            e.wait_ge(dma["smc"].h, sl_lqk)
            e.scalar_tensor_tensor(out=lqk[0], in0=lqk[0], scalar=1.0,
                                   in1=lqk[2], op0=ALU.mult, op1=ALU.mult,
                                   accum_out=lsc[:, 0:1])
            e.scalar_tensor_tensor(out=lqk[1], in0=lqk[1], scalar=1.0,
                                   in1=lqk[3], op0=ALU.mult, op1=ALU.mult,
                                   accum_out=lsc[:, 1:2])
            e.tensor_copy(out=lsc[:, 6:7], in_=lsc[:, 1:2]).then_inc(vml.h, 1)
        V_(fl1)
        vml.n += 1
        v_l1 = vml.n

        def fl2(e, v_l1=v_l1):
            e.wait_ge(vml.h, v_l1)
            e.activation(out=lsc[:, 2:4], in_=lsc[:, 0:2], func=AF.Exp).then_inc(sml.h, 1)
        C_(fl2)
        sml.n += 1
        s_l2 = sml.n

        def fl3(e, s_l2=s_l2):
            e.wait_ge(sml.h, s_l2)
            e.tensor_tensor(out=lsc[:, 4:5], in0=lsc[:, 2:3], in1=lsc[:, 3:4],
                            op=ALU.subtract).then_inc(vml.h, 1)
        V_(fl3)
        vml.n += 1
        v_l3a = vml.n

        def fl3b(e, v_l3a=v_l3a):
            e.wait_ge(vml.h, v_l3a)
            e.activation(out=lsc[:, 5:6], in_=lsc[:, 4:5], func=AF.Identity,
                         bias=c_misc[0:16, 1:2]).then_inc(sml.h, 1)
        C_(fl3b)
        sml.n += 1
        v_l3 = sml.n

        def fl4(e, v_l3=v_l3):
            e.wait_ge(sml.h, v_l3)
            e.transpose(pS_lr, lsc[:, 5:6], c_id_f[0:16, 0:16]).then_inc(tmm.h, 1)
        T_(fl4)
        tmm.n += 1
        t_l4 = tmm.n

        def fl5(e, t_l4=t_l4):
            e.wait_ge(tmm.h, t_l4)
            e.tensor_copy(out=srow[:], in_=pS_lr).then_inc(vml.h, 1)
        V_(fl5)
        vml.n += 1
        v_l5 = vml.n

        def fl6(e, v_l5=v_l5):
            e.wait_ge(vml.h, v_l5)
            e.matmul(pS_lam, c_ones[:], srow[:], start=True,
                     stop=True).then_inc(tmm.h, 1)
        T_(fl6)
        tmm.n += 1
        t_l6 = tmm.n

        def fl7(e, t_l6=t_l6):
            e.wait_ge(tmm.h, t_l6)
            e.tensor_copy(out=lamc[:], in_=pS_lam).then_inc(vml.h, 1)
        V_(fl7)
        vml.n += 1
        v_lam = vml.n

        def frt(e):
            e.dma_start(out=SCR[:, 0:NT * 256].rearrange("p (t c) -> p t c", c=256),
                        in_=ropec.rearrange("(t p) c -> p t c", p=128)
                        ).then_inc(dma["grope"].h, 16)
            e.dma_start(out=SCR[:, NT * 256:2 * NT * 256].rearrange("p (t c) -> p t c", c=256),
                        in_=ropes.rearrange("(t p) c -> p t c", p=128)
                        ).then_inc(dma["grope"].h, 16)
        G_(frt)
        dma["grope"].n += 32
        rope_state["g"] = dma["grope"].n

        def fwk(e):
            e.dma_start(out=WBUF[:, WSLOT:WSLOT + ND * 1024].rearrange("p (t c) -> p t c", c=1024),
                        in_=wk.rearrange("(t p) c -> p t c", p=128)
                        ).then_inc(dma["gwk"].h, 16)
        G_(fwk)
        dma["gwk"].n += 16
        g_wk = dma["gwk"].n

        for i in range(NT):
            for cch in range(2):
                pP = pA if (i * 2 + cch) % 2 == 0 else pB

                def fkm(e, i=i, cch=cch, pP=pP, g_wk=g_wk, g=gget(id(pP))):
                    e.wait_ge(dma["gwk"].h, g_wk)
                    need(e, g)
                    for dt in range(ND):
                        ins = e.matmul(pP[:],
                                       BIG[:, dt * SEQ + i * 128:dt * SEQ + (i + 1) * 128],
                                       WBUF[:, WSLOT + dt * 1024 + cch * 512:
                                            WSLOT + dt * 1024 + (cch + 1) * 512],
                                       start=(dt == 0), stop=(dt == ND - 1))
                    ins.then_inc(tmm.h, 1)
                T_(fkm)
                tmm.n += 1
                t_mm = tmm.n

                def dstf(j, i=i, cch=cch):
                    kc = cch * 4 + j
                    return WBUF[:, KT0 + kc * SEQ + i * 128:KT0 + kc * SEQ + (i + 1) * 128]
                s_done, v_ev = kq_rope_group(i, pP, 512, dstf, t_mm, rope_state)
                guard[id(pP)] = (vml.h, v_ev)
        kT_ready = sml.n

        def fwv(e, kT_ready=kT_ready):
            e.wait_ge(sml.h, kT_ready)
            e.dma_start(out=WBUF[:, WSLOT:WSLOT + ND * 512].rearrange("p (t c) -> p t c", c=512),
                        in_=wv.rearrange("(t p) c -> p t c", p=128)
                        ).then_inc(dma["gwv"].h, 16)
        G_(fwv)
        dma["gwv"].n += 16
        g_wv = dma["gwv"].n

        for i in range(NT):
            pP = pA if i % 2 == 0 else pB

            def fvm(e, i=i, pP=pP, g_wv=g_wv, g=gget(id(pP))):
                e.wait_ge(dma["gwv"].h, g_wv)
                need(e, g)
                for dt in range(ND):
                    ins = e.matmul(pP[:], BIG[:, dt * SEQ + i * 128:dt * SEQ + (i + 1) * 128],
                                   WBUF[:, WSLOT + dt * 512:WSLOT + (dt + 1) * 512],
                                   start=(dt == 0), stop=(dt == ND - 1))
                ins.then_inc(tmm.h, 1)
            T_(fvm)
            tmm.n += 1
            t_v = tmm.n

            def fve(e, i=i, pP=pP, t_v=t_v):
                e.wait_ge(tmm.h, t_v)
                e.activation(out=WBUF[:, VS0 + i * 512:VS0 + (i + 1) * 512],
                             in_=pP[:], func=AF.Copy).then_inc(sml.h, 1)
            C_(fve)
            sml.n += 1
            guard[id(pP)] = (sml.h, sml.n)
        v_ready = sml.n

        WQS = [WSLOT, WSLOT + ND * 256]
        wq_user = {}
        for h in range(nheads):
            slot = WQS[h % 2]
            kv = h // 4

            def fwq(e, h=h, slot=slot, w=wq_user.get(h - 2), vr=v_ready):
                if h < 2:
                    e.wait_ge(sml.h, vr)
                if w is not None:
                    e.wait_ge(tmm.h, w)
                e.dma_start(out=WBUF[:, slot:slot + ND * 256].rearrange(
                                "p (t c) -> p t c", c=256),
                            in_=wq[:, h * 256:(h + 1) * 256].rearrange(
                                "(t p) c -> p t c", p=128)).then_inc(GWQ[h % 2].h, 16)
            G_(fwq)
            GWQ[h % 2].n += 16
            g_wq = (GWQ[h % 2].h, GWQ[h % 2].n)

            for i in range(NT):
                pP = pA if i % 2 == 0 else pB

                def fqm(e, i=i, pP=pP, slot=slot, g_wq=g_wq, g=gget(id(pP)),
                        qg=gget("qTh") if i == 0 else None):
                    e.wait_ge(g_wq[0], g_wq[1])
                    need(e, g)
                    need(e, qg)
                    for dt in range(ND):
                        ins = e.matmul(pP[:, 0:256],
                                       BIG[:, dt * SEQ + i * 128:dt * SEQ + (i + 1) * 128],
                                       WBUF[:, slot + dt * 256:slot + (dt + 1) * 256],
                                       start=(dt == 0), stop=(dt == ND - 1))
                    ins.then_inc(tmm.h, 1)
                T_(fqm)
                tmm.n += 1
                t_mm = tmm.n

                def dstq(j, i=i):
                    return qTh[:, j * SEQ + i * 128:j * SEQ + (i + 1) * 128]
                s_done, v_ev = kq_rope_group(i, pP, 256, dstq, t_mm, rope_state)
                guard[id(pP)] = (vml.h, v_ev)
            wq_user[h] = tmm.n
            q_ready = sml.n

            for qi in range(nqi):
                klen = (qi + 1) * 128
                nch = (klen + 511) // 512
                vvv = qi % 4

                for kc in range(nch):
                    wch = min(512, klen - kc * 512)
                    isdiag = (kc == nch - 1)
                    hb = (kc % 2) * 512
                    hE0 = pE0[:, hb:hb + wch]
                    hE1 = pE1[:, hb:hb + wch]

                    def fem(e, h=h, qi=qi, kc=kc, wch=wch, isdiag=isdiag,
                            hE0=hE0, hE1=hE1, vvv=vvv, kv=kv, hb=hb,
                            g0=gget(("pE0", kc % 2)), g1=gget(("pE1", kc % 2)),
                            kr=(q_ready if (qi == 0 and kc == 0) else None),
                            lr=(v_lam if (h == 0 and qi == 0) else None)):
                        if kr is not None:
                            e.wait_ge(sml.h, kr)
                        if lr is not None:
                            e.wait_ge(vml.h, lr)
                        need(e, g0)
                        need(e, g1)
                        for a, hE in ((0, hE0), (1, hE1)):
                            kcg = kv * 2 + a
                            ins = e.matmul(hE,
                                           qTh[:, a * SEQ + qi * 128:a * SEQ + (qi + 1) * 128],
                                           WBUF[:, KT0 + kcg * SEQ + kc * 512:
                                                KT0 + kcg * SEQ + kc * 512 + wch],
                                           start=True, stop=not isdiag)
                            if isdiag:
                                ins = e.matmul(hE[:, wch - 128:wch], c_id_bf[:],
                                               c_emask[:], start=False, stop=True)
                        ins.then_inc(tmm.h, 1)
                    T_(fem)
                    tmm.n += 1
                    t_e = tmm.n

                    def fex(e, kc=kc, wch=wch, hE0=hE0, hE1=hE1, t_e=t_e,
                            g=gget("sstat_e") if kc == 0 else None):
                        e.wait_ge(tmm.h, t_e)
                        need(e, g)
                        e.activation(out=e0[:, kc * 512:kc * 512 + wch], in_=hE0,
                                     func=AF.Exp, scale=CEXP,
                                     accum_out=sstat[:, 32 + kc:33 + kc])
                        e.activation(out=e1[:, kc * 512:kc * 512 + wch], in_=hE1,
                                     func=AF.Exp, scale=CEXP,
                                     accum_out=sstat[:, 36 + kc:37 + kc])
                        e.copy(out=sstat[:, 56:57],
                               in_=sstat[:, 36 + kc:37 + kc]).then_inc(sml.h, 1)
                    C_(fex)
                    sml.n += 1
                    guard[("pE0", kc % 2)] = (sml.h, sml.n)
                    guard[("pE1", kc % 2)] = (sml.h, sml.n)
                s_exps = sml.n

                def fg1(e, nch=nch, s_exps=s_exps):
                    e.wait_ge(sml.h, s_exps)
                    if nch > 1:
                        e.tensor_reduce(out=sstat[:, 40:41], in_=sstat[:, 32:32 + nch],
                                        axis=AX.X, op=ALU.add)
                        e.tensor_reduce(out=sstat[:, 41:42], in_=sstat[:, 36:36 + nch],
                                        axis=AX.X, op=ALU.add)
                    else:
                        e.tensor_copy(out=sstat[:, 40:41], in_=sstat[:, 32:33])
                        e.tensor_copy(out=sstat[:, 41:42], in_=sstat[:, 36:37])
                    e.nop().then_inc(vml.h, 1) if False else None
                    e.tensor_copy(out=sstat[:, 57:58], in_=sstat[:, 41:42]).then_inc(vml.h, 1)
                V_(fg1)
                vml.n += 1
                v_g1 = vml.n
                guard["sstat_e"] = (vml.h, v_g1)

                def fg2(e, v_g1=v_g1):
                    e.wait_ge(vml.h, v_g1)
                    e.copy(out=sstat[:, 42:43], in_=sstat[:, 40:41])
                    e.copy(out=sstat[:, 43:44], in_=sstat[:, 41:42]).then_inc(sml.h, 1)
                C_(fg2)
                sml.n += 1
                s_g2 = sml.n

                def fg3(e, s_g2=s_g2):
                    e.wait_ge(sml.h, s_g2)
                    e.reciprocal(out=sstat[:, 44:45], in_=sstat[:, 43:44])
                    e.reciprocal(out=sstat[:, 45:46], in_=sstat[:, 42:43]).then_inc(vml.h, 1)
                V_(fg3)
                vml.n += 1
                v_g3 = vml.n

                def fg4(e, v_g3=v_g3):
                    e.wait_ge(vml.h, v_g3)
                    e.copy(out=sstat[:, 46:47], in_=sstat[:, 44:45])
                    e.activation(out=sstat[:, 55:56], in_=sstat[:, 45:46],
                                 func=AF.Identity, scale=-1.0).then_inc(sml.h, 1)
                C_(fg4)
                sml.n += 1
                s_g4 = sml.n

                def fg5(e, h=h, s_g4=s_g4):
                    e.wait_ge(sml.h, s_g4)
                    e.tensor_scalar(out=sstat[:, 53:54], in0=lamc[:, h:h + 1],
                                    scalar1=sstat[:, 42:43],
                                    scalar2=sstat[:, 46:47],
                                    op0=ALU.mult, op1=ALU.mult)
                    e.tensor_copy(out=sstat[:, 56:57],
                                  in_=sstat[:, 55:56]).then_inc(vml.h, 1)
                V_(fg5)
                vml.n += 1
                v_g5 = vml.n

                def fg6(e, v_g5=v_g5):
                    e.wait_ge(vml.h, v_g5)
                    e.copy(out=sstat[:, 54:55], in_=sstat[:, 53:54]).then_inc(sml.h, 1)
                C_(fg6)
                sml.n += 1
                s_gb = sml.n

                for kc in (range(nch) if "comb" not in KSKIP else []):
                    wch = min(512, klen - kc * 512)
                    ub = (kc % 2) * 512
                    isdiag = (kc == nch - 1)

                    def fcb(e, kc=kc, wch=wch, ub=ub, s_gb=s_gb,
                            w=state_u.get(kc % 2)):
                        e.wait_ge(sml.h, max(s_gb, w if w is not None else 0))
                        e.scalar_tensor_tensor(out=uneg[:, ub:ub + wch],
                                               in0=e1[:, kc * 512:kc * 512 + wch],
                                               scalar=sstat[:, 54:55],
                                               in1=e0[:, kc * 512:kc * 512 + wch],
                                               op0=ALU.mult,
                                               op1=ALU.subtract).then_inc(vml.h, 1)
                    V_(fcb)
                    vml.n += 1
                    v_u = vml.n

                    def fed(e, kc=kc, wch=wch, ub=ub, v_u=v_u,
                            w=state_d.get(kc % 2),
                            w2=(state_d.get("vsd") if kc == 0 else None)):
                        e.wait_ge(vml.h, v_u)
                        if w is not None:
                            e.wait_ge(tmm.h, w)
                        if w2 is not None:
                            e.wait_ge(vml.h, w2)
                        e.activation(out=dchk[:, ub:ub + wch], in_=uneg[:, ub:ub + wch],
                                     func=AF.Exp, scale=sstat[:, 56:57],
                                     accum_out=sstat[:, 48 + kc:49 + kc]).then_inc(sml.h, 1)
                    C_(fed)
                    sml.n += 1
                    state_u[kc % 2] = sml.n
                    s_d = sml.n

                    if "ftd" in KSKIP:
                        continue
                    ntr = wch // 128
                    half = state_tr[0] % 2
                    state_tr[0] += 1

                    def ftd(e, ntr=ntr, ub=ub, half=half, s_d=s_d,
                            g=gget(("pTR", half))):
                        e.wait_ge(sml.h, s_d)
                        need(e, g)
                        for j in range(ntr):
                            ins = e.transpose(pTR[:, half * 1024 + j * 128:
                                                  half * 1024 + (j + 1) * 128],
                                              dchk[:, ub + j * 128:ub + (j + 1) * 128],
                                              c_id_bf[:])
                        ins.then_inc(tmm.h, 1)
                    T_(ftd)
                    tmm.n += 1
                    t_d = tmm.n
                    state_d[kc % 2] = t_d

                    ncopy = ntr - 1 if isdiag else ntr
                    if "fde" in KSKIP:
                        ncopy = 0
                    if ncopy > 0:
                        def fde(e, kc=kc, ncopy=ncopy, half=half, t_d=t_d):
                            e.wait_ge(tmm.h, t_d)
                            e.tensor_copy(out=xn[:, kc * 512:kc * 512 + ncopy * 128],
                                          in_=pTR[:, half * 1024:
                                                   half * 1024 + ncopy * 128]
                                          ).then_inc(vml.h, 1)
                        V_(fde)
                        vml.n += 1
                    if isdiag and "fdm" not in KSKIP:
                        def fdm(e, ntr=ntr, half=half, t_d=t_d, qi=qi):
                            e.wait_ge(tmm.h, t_d)
                            e.tensor_tensor(out=xn[:, qi * 128:(qi + 1) * 128],
                                            in0=pTR[:, half * 1024 + (ntr - 1) * 128:
                                                    half * 1024 + ntr * 128],
                                            in1=c_maskt[:], op=ALU.mult).then_inc(vml.h, 1)
                        V_(fdm)
                        vml.n += 1
                    guard[("pTR", half)] = (vml.h, vml.n)
                s_dT_s, s_dT_v = sml.n, vml.n
                if "pv" in KSKIP or "comb" in KSKIP:
                    continue

                pV = pA if qi % 2 == 0 else pB

                def fpv(e, qi=qi, kv=kv, pV=pV, s_dT_s=s_dT_s, s_dT_v=s_dT_v,
                        g=gget(id(pV)),
                        vr=(v_ready if (h == 0 and qi == 0) else None)):
                    e.wait_ge(sml.h, s_dT_s)
                    e.wait_ge(vml.h, s_dT_v)
                    if vr is not None:
                        e.wait_ge(sml.h, vr)
                    need(e, g)
                    for kt in range(qi + 1):
                        ins = e.matmul(pV[:, 0:128], xn[:, kt * 128:(kt + 1) * 128],
                                       WBUF[:, VS0 + kt * 512 + kv * 128:
                                            VS0 + kt * 512 + (kv + 1) * 128],
                                       start=(kt == 0), stop=(kt == qi))
                    ins.then_inc(tmm.h, 1)
                T_(fpv)
                tmm.n += 1
                t_pv = tmm.n

                def fpe(e, qi=qi, pV=pV, t_pv=t_pv, g=gget(("attA", qi))):
                    e.wait_ge(tmm.h, t_pv)
                    need(e, g)
                    e.activation(out=attsc[:, 256:384], in_=pV[:, 0:128],
                                 func=AF.Square, accum_out=sstat[:, qi:qi + 1])
                    e.activation(out=snapt[:, qi * 128:(qi + 1) * 128],
                                 in_=pV[:, 0:128], func=AF.Copy).then_inc(sml.h, 1)
                C_(fpe)
                sml.n += 1
                guard[id(pV)] = (sml.h, sml.n)
                s_pe = sml.n

                def fsd(e, qi=qi, nch=nch, s_pe=s_pe):
                    e.wait_ge(sml.h, s_pe)
                    if nch > 1:
                        e.tensor_reduce(out=sstat[:, 16 + qi:17 + qi],
                                        in_=sstat[:, 48:48 + nch],
                                        axis=AX.X, op=ALU.add).then_inc(vml.h, 1)
                    else:
                        e.tensor_copy(out=sstat[:, 16 + qi:17 + qi],
                                      in_=sstat[:, 48:49]).then_inc(vml.h, 1)
                V_(fsd)
                vml.n += 1
                state_d["vsd"] = vml.n
            v_head = vml.n
            s_head = sml.n

            if "pv" in KSKIP or "comb" in KSKIP or "head" in KSKIP:
                continue
            def fscA(e, v_head=v_head, s_head=s_head):
                e.wait_ge(vml.h, v_head)
                e.wait_ge(sml.h, s_head)
                e.copy(out=sstat[:, 32:48], in_=sstat[:, 16:32]).then_inc(sml.h, 1)
            C_(fscA)
            sml.n += 1
            sA = sml.n

            def fscB(e, sA=sA):
                e.wait_ge(sml.h, sA)
                e.tensor_scalar(out=sstat[:, 16:32], in0=sstat[:, 32:48],
                                scalar1=c_mcnt[:], scalar2=None,
                                op0=ALU.subtract).then_inc(vml.h, 1)
            V_(fscB)
            vml.n += 1
            vB = vml.n

            def fscC(e, vB=vB):
                e.wait_ge(vml.h, vB)
                e.copy(out=sstat[:, 32:48], in_=sstat[:, 16:32]).then_inc(sml.h, 1)
            C_(fscC)
            sml.n += 1
            sC = sml.n

            def fscD(e, sC=sC):
                e.wait_ge(sml.h, sC)
                e.tensor_tensor(out=sstat[:, 48:64], in0=sstat[:, 32:48],
                                in1=sstat[:, 32:48], op=ALU.mult).then_inc(vml.h, 1)
            V_(fscD)
            vml.n += 1
            vD = vml.n

            def fscE(e, vD=vD):
                e.wait_ge(vml.h, vD)
                e.activation(out=sstat[:, 32:48], in_=sstat[:, 48:64],
                             func=AF.Identity, scale=128.0e-8).then_inc(sml.h, 1)
            C_(fscE)
            sml.n += 1
            sE = sml.n

            def fscF(e, sE=sE):
                e.wait_ge(sml.h, sE)
                e.tensor_tensor(out=sstat[:, 48:64], in0=sstat[:, 32:48],
                                in1=sstat[:, 0:16], op=ALU.add).then_inc(vml.h, 1)
            V_(fscF)
            vml.n += 1
            vF = vml.n

            def fscG(e, vF=vF):
                e.wait_ge(vml.h, vF)
                e.activation(out=sstat[:, 32:48], in_=sstat[:, 48:64],
                             func=AF.Ln).then_inc(sml.h, 1)
            C_(fscG)
            sml.n += 1
            sG = sml.n

            def fscH(e, sG=sG):
                e.wait_ge(sml.h, sG)
                e.tensor_copy(out=sstat[:, 48:64],
                              in_=sstat[:, 32:48]).then_inc(vml.h, 1)
            V_(fscH)
            vml.n += 1
            vH = vml.n

            def fscI(e, vH=vH):
                e.wait_ge(vml.h, vH)
                e.activation(out=sstat[:, 0:16], in_=sstat[:, 48:64], func=AF.Exp,
                             scale=-0.5, bias=c_misc[:, 2:3]).then_inc(sml.h, 1)
            C_(fscI)
            sml.n += 1
            v_sc3 = sml.n
            guard["sstat_e"] = (sml.h, v_sc3)

            for qi in (range(nqi) if "evac" not in KSKIP and "pv" not in KSKIP
                       and "comb" not in KSKIP else []):
                sl = (qi % 2) * 128
                so = 384 + (qi % 2) * 128

                def fas(e, qi=qi, sl=sl, v_sc3=v_sc3, w=state_att.get(qi % 2)):
                    e.wait_ge(sml.h, v_sc3)
                    if w is not None:
                        e.wait_ge(tmm.h, w)
                    e.tensor_scalar(out=attsc[:, sl:sl + 128],
                                    in0=snapt[:, qi * 128:(qi + 1) * 128],
                                    scalar1=sstat[:, 0 + qi:1 + qi], scalar2=None,
                                    op0=ALU.mult).then_inc(vml.h, 1)
                V_(fas)
                vml.n += 1
                v_as = vml.n
                guard[("attA", qi)] = (vml.h, v_as)

                half = state_tr[0] % 2
                state_tr[0] += 1

                def fat(e, sl=sl, half=half, v_as=v_as, g=gget(("pTR", half))):
                    e.wait_ge(vml.h, v_as)
                    need(e, g)
                    e.transpose(pTR[:, half * 1024:half * 1024 + 128],
                                attsc[:, sl:sl + 128], c_id_bf[:]).then_inc(tmm.h, 1)
                T_(fat)
                tmm.n += 1
                t_at = tmm.n
                state_att[qi % 2] = t_at

                def fae(e, so=so, half=half, t_at=t_at, w=state_att2.get(qi % 2)):
                    e.wait_ge(tmm.h, t_at)
                    if w is not None:
                        e.wait_ge(w[0], w[1])
                    e.tensor_copy(out=attsc[:, so:so + 128],
                                  in_=pTR[:, half * 1024:half * 1024 + 128]
                                  ).then_inc(vml.h, 1)
                V_(fae)
                vml.n += 1
                s_ae = vml.n
                guard[("pTR", half)] = (vml.h, s_ae)

                def fad(e, h=h, qi=qi, so=so, s_ae=s_ae):
                    e.wait_ge(vml.h, s_ae)
                    e.dma_start(out=attr[h * 128:(h + 1) * 128, qi * 128:(qi + 1) * 128],
                                in_=attsc[:, so:so + 128]).then_inc(GAT[qi % 2].h, 16)
                G_(fad)
                GAT[qi % 2].n += 16
                state_att2[qi % 2] = (GAT[qi % 2].h, GAT[qi % 2].n)
            guard["qTh"] = (tmm.h, tmm.n)
            rope_state["t"] = tmm.n

    def wo_phase():
        def fwo(e):
            e.dma_start(out=WBUF[:].rearrange("p (t c) -> p t c", c=2048),
                        in_=wo.rearrange("(t p) c -> p t c", p=128)
                        ).then_inc(dma["gwk"].h, 16)
        G_(fwo)
        dma["gwk"].n += 16
        g_wo = dma["gwk"].n

        def fws(e, g_wo=g_wo):
            e.wait_ge(dma["gwk"].h, g_wo)
            for dt in range(ND):
                ins = e.tensor_scalar(out=WBUF[:, dt * DM:(dt + 1) * DM],
                                      in0=WBUF[:, dt * DM:(dt + 1) * DM],
                                      scalar1=c_rmsw[:], scalar2=None, op0=ALU.mult)
            ins.then_inc(vml.h, 1)
        V_(fws)
        vml.n += 1
        v_wo = vml.n

        apply_v = {}
        big_tile_s = [None] * NT
        zt_guard = [None, None]
        for i in range(NT):
            sbase = (i % 2) * 2048

            def fsl(e, i=i, sbase=sbase, w=apply_v.get(i - 2)):
                if w is not None:
                    e.wait_ge(vml.h, w)
                e.dma_start(out=SCR[:, sbase:sbase + 2048].rearrange(
                                "p (t c) -> p t c", c=128),
                            in_=attr[:, i * 128:(i + 1) * 128].rearrange(
                                "(t p) c -> p t c", p=128)).then_inc(SSL[i % 2].h, 16)
            S_(fsl)
            SSL[i % 2].n += 16

            xb = xt[i % 2]

            def fx1(e, i=i, xb=xb, w=apply_v.get(i - 2)):
                if w is not None:
                    e.wait_ge(vml.h, w)
                e.dma_start(out=xb[:], in_=x1r[i * 128:(i + 1) * 128, :]
                            ).then_inc(SX[i % 2].h, 16)
            S_(fx1)
            SX[i % 2].n += 16
            sl_i = [(SSL[i % 2].h, SSL[i % 2].n), (SX[i % 2].h, SX[i % 2].n)]

            for cch in range(4):
                pW = pA if (i * 4 + cch) % 2 == 0 else pB

                def fwm(e, i=i, cch=cch, pW=pW, sbase=sbase, sl_i=sl_i,
                        v_wo=v_wo, g=gget(id(pW))):
                    for hh, nn in sl_i:
                        e.wait_ge(hh, nn)
                    if i == 0 and cch == 0:
                        e.wait_ge(vml.h, v_wo)
                    need(e, g)
                    for dt in range(ND):
                        ins = e.matmul(pW[:], SCR[:, sbase + dt * 128:sbase + (dt + 1) * 128],
                                       WBUF[:, dt * DM + cch * 512:dt * DM + (cch + 1) * 512],
                                       start=(dt == 0), stop=(dt == ND - 1))
                    ins.then_inc(tmm.h, 1)
                T_(fwm)
                tmm.n += 1
                t_w = tmm.n

                def fadd(e, cch=cch, pW=pW, xb=xb, t_w=t_w,
                         zg=(list(zt_guard) if cch == 0 else None)):
                    e.wait_ge(tmm.h, t_w)
                    if zg is not None:
                        for g in zg:
                            need(e, g)
                    e.tensor_tensor(out=zt[:, cch * 512:(cch + 1) * 512],
                                    in0=pW[:], in1=xb[:, cch * 512:(cch + 1) * 512],
                                    op=ALU.add).then_inc(vml.h, 1)
                V_(fadd)
                vml.n += 1
                guard[id(pW)] = (vml.h, vml.n)
            v_z = vml.n
            apply_v[i] = v_z

            def fzs(e, i=i, v_z=v_z):
                e.wait_ge(vml.h, v_z)
                e.dma_start(out=x2r[i * 128:(i + 1) * 128, :], in_=zt[:]
                            ).then_inc(dma["gzs"].h, 16)
            G_(fzs)
            dma["gzs"].n += 16
            g_z = dma["gzs"].n

            s_st = ln_stats(zt, [(vml.h, v_z)], i % 2)
            v_xn = ln_normalize(zt, s_st, i % 2)
            big_tile_s[i] = transpose_to_big(i, v_xn)
            zt_guard = [(vml.h, v_xn), (dma["gzs"].h, g_z)]
        guard["zt"] = zt_guard[0]
        guard["zt_store"] = zt_guard[1]
        xt_free = {-2: apply_v[NT - 2], -1: apply_v[NT - 1]}
        return big_tile_s, xt_free

    big_s, xt_free = initial_ln()
    if dbg:
        d_big1 = nc.dram_tensor("d_big1", [128, ND * SEQ], BF16,
                                kind="ExternalOutput").ap()
        _bs = big_s[NT - 1]

        def fdump1(e, _bs=_bs):
            e.wait_ge(sml.h, _bs)
            e.dma_start(out=d_big1, in_=BIG[:]).then_inc(dma["smc"].h, 16)
        S_(fdump1)
        dma["smc"].n += 16
    if stop >= 2:
        ffn_phase(0, x_in, x1r, fuse_next_ln=True, big_tile_s=big_s, xt_free=xt_free)
        barrier()
    if stop >= 3:
        nh_ = H
        nq_ = NT
        if stop == 3:
            if sub == 17:
                nh_, nq_ = 1, 0
            elif sub == 18:
                nh_, nq_ = 1, 1
            elif sub == 19:
                nh_, nq_ = 1, 4
            elif sub == 20:
                nh_, nq_ = 1, 2
            elif sub == 21:
                nh_, nq_ = 1, 3
            elif sub <= 16:
                nh_, nq_ = sub, NT
        attention_phase(nheads=nh_, nqi=nq_)
        barrier()
    if stop >= 4:
        big_s, xt_free = wo_phase()
        barrier()
    if stop >= 5:
        ffn_phase(1, x2r, y_out, fuse_next_ln=False, big_tile_s=big_s, xt_free=xt_free)

    if dbg:
        ddump = {}
        for nm, src in (("d_sall", sall), ("d_srch", srch), ("d_wcol", wcol),
                        ("d_stat", stat), ("d_snapt", snapt), ("d_sown", sown)):
            dt_ = nc.dram_tensor(nm, list(src.shape), F32, kind="ExternalOutput").ap()
            ddump[nm] = (dt_, src)
        d_big = nc.dram_tensor("d_big", [128, ND * SEQ], BF16, kind="ExternalOutput").ap()
        d_ccin = nc.dram_tensor("d_ccin", [16, SEQ], F32, kind="ExternalOutput").ap()
        d_ccout = nc.dram_tensor("d_ccout", [128, SEQ], F32, kind="ExternalOutput").ap()
        t9, v9, s9 = tmm.n, vml.n, sml.n
        dsn = [(d.h, d.n) for d in dma.values() if d.n > 0]

        def fdump(e, t9=t9, v9=v9, s9=s9, dsn=list(dsn)):
            e.wait_ge(tmm.h, t9)
            e.wait_ge(vml.h, v9)
            e.wait_ge(sml.h, s9)
            for h, n in dsn:
                e.wait_ge(h, n)
            for nm, (dt_, src) in ddump.items():
                if nm == "d_srch":
                    e.dma_start(out=dt_[:, 0:12], in_=src[:, 0:12]).then_inc(dma["smc"].h, 16)
                elif nm == "d_stat":
                    e.dma_start(out=dt_[:, 0:8], in_=src[:, 0:8]).then_inc(dma["smc"].h, 16)
                else:
                    e.dma_start(out=dt_, in_=src[:, :]).then_inc(dma["smc"].h, 16)
            e.dma_start(out=d_big, in_=BIG[:]).then_inc(dma["smc"].h, 16)
            e.dma_start(out=d_ccin, in_=cc_in[0].ap()).then_inc(dma["smc"].h, 16)
            e.dma_start(out=d_ccout, in_=cc_out[0].ap()).then_inc(dma["smc"].h, 16)
        S_(fdump)
        dma["smc"].n += 16 * 9

    dsnap_fin = [(d.h, d.n) for d in dma.values() if d.n > 0]

    def ffin(e, dsnap=list(dsnap_fin)):
        for h, n in dsnap:
            e.wait_ge(h, n)
    G_(ffin)

    with nc.Block() as blk:
        @blk.sync
        def _(e):
            for f in prog["sync"]:
                f(e)

        @blk.gpsimd
        def _(e):
            for f in prog["gpsimd"]:
                f(e)

        @blk.tensor
        def _(e):
            for f in prog["tensor"]:
                f(e)

        @blk.vector
        def _(e):
            for f in prog["vector"]:
                f(e)

        @blk.scalar
        def _(e):
            for f in prog["scalar"]:
                f(e)

    ctx.close()
    return nc


_CONSTS = make_consts()


def kernel(L0_attn_x, L0_attn_Wq, L0_attn_Wk, L0_attn_Wv, L0_attn_Wo,
           L0_attn_lq1, L0_attn_lq2, L0_attn_lk1, L0_attn_lk2, L0_attn_rms_w,
           L0_ffn_router_w, L0_ffn_router_b, L0_ffn_expert_w, L0_ffn_expert_b):
    nc = build_program()
    x = np.ascontiguousarray(np.asarray(L0_attn_x, dtype=np.float32))
    base = dict(
        wq=np.ascontiguousarray(np.asarray(L0_attn_Wq, np.float32)),
        wk=np.ascontiguousarray(np.asarray(L0_attn_Wk, np.float32)),
        wv=np.ascontiguousarray(np.asarray(L0_attn_Wv, np.float32)),
        wo=np.ascontiguousarray(np.asarray(L0_attn_Wo, np.float32)),
        lq1=np.ascontiguousarray(np.asarray(L0_attn_lq1, np.float32)),
        lq2=np.ascontiguousarray(np.asarray(L0_attn_lq2, np.float32)),
        lk1=np.ascontiguousarray(np.asarray(L0_attn_lk1, np.float32)),
        lk2=np.ascontiguousarray(np.asarray(L0_attn_lk2, np.float32)),
        rmsw=np.ascontiguousarray(np.asarray(L0_attn_rms_w, np.float32).reshape(128, 1)),
        rw=np.ascontiguousarray(np.asarray(L0_ffn_router_w, np.float32)),
        rb=np.ascontiguousarray(np.asarray(L0_ffn_router_b, np.float32).reshape(1, 16)),
        ew=np.ascontiguousarray(np.asarray(L0_ffn_expert_w, np.float32)),
        eb=np.ascontiguousarray(np.asarray(L0_ffn_expert_b, np.float32).reshape(1, DM)),
        **_CONSTS,
    )
    pidx = np.arange(128)
    in_maps = []
    for r in range(NCORES):
        m = dict(base)
        m["x_in"] = np.ascontiguousarray(x[r])
        m["selr"] = (pidx // 16 == r).astype(np.float32).reshape(128, 1)
        in_maps.append(m)
    res = run_bass_kernel_spmd(nc, in_maps, core_ids=list(range(NCORES)))
    return np.stack([res.results[r]["y_out"] for r in range(NCORES)], axis=0)


# version 03
# speedup vs baseline: 1.0109x; 1.0109x over previous
import math
import numpy as np
from contextlib import ExitStack

import concourse.bass as bass
import concourse.mybir as mybir
from concourse.bass_utils import run_bass_kernel_spmd

F32 = mybir.dt.float32
BF16 = mybir.dt.bfloat16
U32 = mybir.dt.uint32
AF = mybir.ActivationFunctionType
ALU = mybir.AluOpType
AX = mybir.AxisListType

SEQ = 2048
DM = 2048
NT = 16
ND = 16
H = 16
KVH = 4
NEXP = 16
TOPK = 1024
NCORES = 8
CEXP = 1.0 / math.sqrt(128.0)
LAM_INIT = 0.2
NEG_BIG = -1.0e30
BIGV = 1.0e6
SEARCH_ITERS = 34
RMS_SC = 0.8 * math.sqrt(128.0)


def make_consts():
    c = {}
    c["ident"] = np.eye(128, dtype=np.float32)
    j = np.arange(0, 256, 2, dtype=np.float32)
    theta = (np.float32(1.0) / (np.float32(10000.0) ** (j / np.float32(256.0)))).astype(np.float32)
    pos = np.arange(SEQ, dtype=np.float32)
    ang = pos[:, None] * theta[None, :]
    cosv = np.cos(ang).astype(np.float32)
    sinv = np.sin(ang).astype(np.float32)
    CF = np.empty((SEQ, 256), dtype=np.float32)
    SF = np.empty((SEQ, 256), dtype=np.float32)
    CF[:, 0::2] = cosv
    CF[:, 1::2] = cosv
    SF[:, 0::2] = -sinv
    SF[:, 1::2] = sinv
    c["ropec"] = CF
    c["ropes"] = SF
    p = np.arange(128)[:, None]
    cc = np.arange(128)[None, :]
    c["emask"] = np.where(cc <= p, 0.0, NEG_BIG).astype(np.float32)
    kk = np.arange(128)[:, None]
    qq = np.arange(128)[None, :]
    c["maskt"] = (qq >= kk).astype(np.float32)
    c["maskcnt"] = (127.0 - np.arange(128, dtype=np.float32)).reshape(128, 1)
    pidx = np.arange(128)
    c["mgrp"] = (pidx[:, None] % 16 == pidx[None, :] % 16).astype(np.float32)
    c["repl16"] = (np.arange(16)[:, None] == pidx[None, :] % 16).astype(np.float32)
    c["ones_row"] = np.ones((1, 128), dtype=np.float32)
    mc = np.zeros((128, 4), dtype=np.float32)
    mc[:, 0] = 1e-5
    mc[:, 1] = LAM_INIT
    mc[:, 2] = math.log(0.8 * math.sqrt(128.0))
    mc[:, 3] = 2.0
    c["cmisc"] = mc
    return c


class Ctr:
    def __init__(self, ctx, nc, name):
        self.h = ctx.enter_context(nc.semaphore(name))
        self.n = 0


def build_program(dbg=False, stop=5, sub=3):
    import os as _os0
    KSKIP = set((_os0.environ.get("KSKIP") or "").split(","))
    import os as _os
    nc = bass.Bass("TRN2", target_bir_lowering=False,
                   detect_race_conditions=bool(_os.environ.get("KSTRICT")))
    ctx = ExitStack()

    dri = lambda n, sh: nc.dram_tensor(n, sh, F32, kind="ExternalInput").ap()
    x_in = dri("x_in", [SEQ, DM])
    wq = dri("wq", [DM, 4096])
    wk = dri("wk", [DM, 1024])
    wv = dri("wv", [DM, 512])
    wo = dri("wo", [DM, DM])
    lq1 = dri("lq1", [16, 128])
    lq2 = dri("lq2", [16, 128])
    lk1 = dri("lk1", [16, 128])
    lk2 = dri("lk2", [16, 128])
    rmsw = dri("rmsw", [128, 1])
    rw = dri("rw", [DM, 16])
    rb = dri("rb", [1, 16])
    ew = dri("ew", [DM, DM])
    eb = dri("eb", [1, DM])
    ident = dri("ident", [128, 128])
    ropec = dri("ropec", [SEQ, 256])
    ropes = dri("ropes", [SEQ, 256])
    emask = dri("emask", [128, 128])
    maskt = dri("maskt", [128, 128])
    maskcnt = dri("maskcnt", [128, 1])
    mgrp = dri("mgrp", [128, 128])
    repl16 = dri("repl16", [16, 128])
    ones_row = dri("ones_row", [1, 128])
    selr = dri("selr", [128, 1])
    cmisc = dri("cmisc", [128, 4])

    y_out = nc.dram_tensor("y_out", [SEQ, DM], F32, kind="ExternalOutput").ap()
    ik = dict(kind="ExternalOutput") if dbg else {}
    x1r = nc.dram_tensor("x1r", [SEQ, DM], F32, **ik).ap()
    x2r = nc.dram_tensor("x2r", [SEQ, DM], F32, **ik).ap()
    attr = nc.dram_tensor("attr", [DM, SEQ], BF16, **ik).ap()
    cc_in = [nc.dram_tensor(f"cc_in{k}", [16, SEQ], F32) for k in range(2)]
    cc_out = [nc.dram_tensor(f"cc_out{k}", [128, SEQ], F32) for k in range(2)]

    sbt = lambda n, sh, dt: ctx.enter_context(nc.sbuf_tensor(n, sh, dt))
    pst = lambda n, sh, dt: ctx.enter_context(nc.psum_tensor(n, sh, dt))

    BIG = sbt("BIG", [128, ND * SEQ], BF16)
    WBUF = sbt("WBUF", [128, ND * DM], BF16)
    KT0 = 0
    WSLOT = 16384
    VS0 = 24576
    SCR = sbt("SCR", [128, 8192], BF16)
    qTh = sbt("qTh", [128, 2 * SEQ], BF16)
    xt = [sbt(f"xt{i}", [128, DM], F32) for i in range(2)]
    e0b = xt[0][:, :].bitcast(BF16)[:, 0:SEQ]
    e1b = xt[0][:, :].bitcast(BF16)[:, SEQ:2 * SEQ]
    zt = sbt("zt", [128, DM], F32)
    xn = sbt("xn", [128, DM], BF16)
    e0 = sbt("e0", [128, SEQ], BF16)
    e1 = sbt("e1", [128, SEQ], BF16)
    sall = qTh[:, :].bitcast(F32)
    snapt = sbt("snapt", [128, SEQ], F32)
    qrw = sbt("qrw", [128, 1024], BF16)
    qrt = sbt("qrt", [128, 1024], BF16)
    uneg = qrw
    dchk = qrt
    attsc = sbt("attsc", [128, 640], BF16)
    qro = attsc[:, 0:512]
    ST = snapt[0:16, :]
    lqk = [snapt[0:16, i * 128:(i + 1) * 128] for i in range(4)]
    sown = sbt("sown", [128, 64], F32)
    stat = sbt("stat", [128, 64], F32)
    srch = sbt("srch", [128, 32], F32)
    srchu = sbt("srchu", [128, 1], U32)
    srow = sbt("srow", [1, 16], F32)
    wcol = sbt("wcol", [128, 16], F32)
    sstat = sbt("sstat", [128, 64], F32)
    lamc = sbt("lamc", [128, 16], F32)
    lsc = sbt("lsc", [16, 8], F32)
    c_id_bf = sbt("c_id_bf", [128, 128], BF16)
    c_id_f = sbt("c_id_f", [128, 128], F32)
    c_emask = sbt("c_emask", [128, 128], BF16)
    c_maskt = sbt("c_maskt", [128, 128], BF16)
    c_mcnt = sbt("c_mcnt", [128, 1], F32)
    c_mgrp = sbt("c_mgrp", [128, 128], F32)
    c_repl = sbt("c_repl", [16, 128], F32)
    c_ones = sbt("c_ones", [1, 128], F32)
    c_selr = sbt("c_selr", [128, 1], F32)
    c_rb = sbt("c_rb", [1, 16], F32)
    c_ebbf = e1[0:1, 0:DM]
    jnk_bf = snapt[:, 0:1024].bitcast(BF16)
    jnk_bf2 = snapt[:, 1024:2048].bitcast(BF16)
    c_rmsw = sbt("c_rmsw", [128, 1], F32)
    c_misc = sbt("c_misc", [128, 4], F32)
    rwsb = sbt("rwsb", [128, ND * 16], BF16)
    c_onesb = c_maskt[0:1, :]
    c_rope_c = SCR[:, 0:NT * 256]
    c_rope_s = SCR[:, NT * 256:2 * NT * 256]

    pE0 = pst("pE0", [128, 1024], F32)
    pE1 = pst("pE1", [128, 1024], F32)
    pA = pst("pA", [128, 512], F32)
    pB = pst("pB", [128, 512], F32)
    pTR = pst("pTR", [128, 2048], BF16)

    pS_cnt = pE0[:, 0:1]
    pS_thx = pE0[:, 1:2]
    pS_lam = pE0[:, 2:18]
    pS_w = pE0[:, 18:34]
    pS_st = [pE0[0:16, 34:162], pE0[0:16, 162:290]]
    pS_r0 = pE0[0:1, 290:418]
    pS_r16 = pE0[0:16, 418:419]
    pS_lr = pE0[0:1, 420:436]

    dma = {n: Ctr(ctx, nc, n) for n in (
        "smc", "sx0", "sx1", "ssl0", "ssl1", "gmc", "geb", "gst", "gsall",
        "grope", "gwk", "gwv", "gwq0", "gwq1", "gzs", "gat0", "gat1")}
    SX = [dma["sx0"], dma["sx1"]]
    SSL = [dma["ssl0"], dma["ssl1"]]
    GWQ = [dma["gwq0"], dma["gwq1"]]
    GAT = [dma["gat0"], dma["gat1"]]
    ccs = Ctr(ctx, nc, "ccs")
    tmm = Ctr(ctx, nc, "tmm")
    pml = Ctr(ctx, nc, "pml")
    vml = Ctr(ctx, nc, "vml")
    sml = Ctr(ctx, nc, "sml")

    prog = {k: [] for k in ("sync", "gpsimd", "tensor", "vector", "scalar")}
    S_ = prog["sync"].append
    G_ = prog["gpsimd"].append
    T_ = prog["tensor"].append
    V_ = prog["vector"].append
    C_ = prog["scalar"].append

    guard = {}

    def need(e, g):
        if g is None:
            return
        if isinstance(g, list):
            for gg in g:
                need(e, gg)
        else:
            e.wait_ge(g[0], g[1])

    def gget(key):
        return guard.get(key)

    state_tr = [0]

    def _c0(e):
        e.dma_start(out=c_id_f[:], in_=ident).then_inc(dma["smc"].h, 16)
        e.dma_start(out=c_mcnt[:], in_=maskcnt).then_inc(dma["smc"].h, 16)
        e.dma_start(out=c_mgrp[:], in_=mgrp).then_inc(dma["smc"].h, 16)
        e.dma_start(out=c_repl[:], in_=repl16).then_inc(dma["smc"].h, 16)
        e.dma_start(out=c_ones[:], in_=ones_row).then_inc(dma["smc"].h, 16)
        e.dma_start(out=c_rb[:], in_=rb).then_inc(dma["smc"].h, 16)
        e.dma_start(out=c_selr[:], in_=selr).then_inc(dma["smc"].h, 16)
        e.dma_start(out=c_rmsw[:], in_=rmsw).then_inc(dma["smc"].h, 16)
        e.dma_start(out=c_misc[:], in_=cmisc).then_inc(dma["smc"].h, 16)
    S_(_c0)
    dma["smc"].n += 9 * 16
    SL_CONST = dma["smc"].n

    def _c1(e):
        e.dma_start(out=c_id_bf[:], in_=ident).then_inc(dma["gmc"].h, 16)
        e.dma_start(out=c_emask[:], in_=emask).then_inc(dma["gmc"].h, 16)
        e.dma_start(out=c_maskt[:], in_=maskt).then_inc(dma["gmc"].h, 16)
        e.dma_start(out=rwsb[:].rearrange("p (t c) -> p t c", c=16),
                        in_=rw.rearrange("(t p) c -> p t c", p=128)).then_inc(dma["gmc"].h, 16)
        e.dma_start(out=WBUF[:].rearrange("p (t c) -> p t c", c=2048),
                        in_=ew.rearrange("(t p) c -> p t c", p=128)).then_inc(dma["gmc"].h, 16)
    G_(_c1)
    dma["gmc"].n += 5 * 16
    GL_CONST = dma["gmc"].n

    def ln_stats(xtile, waits, par):
        b = par * 16

        def f(e, xtile=xtile, waits=list(waits) + [gget(("stats", par))], b=b):
            for g in waits:
                need(e, g)
            e.activation(out=jnk_bf, in_=xtile[:], func=AF.Identity,
                         accum_out=stat[:, b + 0:b + 1])
            e.activation(out=jnk_bf2, in_=xtile[:], func=AF.Square,
                         accum_out=stat[:, b + 1:b + 2])
            e.copy(out=stat[:, b + 10:b + 11],
                   in_=stat[:, b + 1:b + 2]).then_inc(sml.h, 1)
        C_(f)
        sml.n += 1
        return sml.n

    def ln_normalize(xtile, s_stats, par):
        b = par * 16

        def fA(e, s_stats=s_stats, b=b, g=gget(("statA", par))):
            e.wait_ge(sml.h, s_stats)
            need(e, g)
            e.activation(out=stat[:, b + 2:b + 3], in_=stat[:, b + 0:b + 1],
                         func=AF.Identity, scale=1.0 / DM)
            e.activation(out=stat[:, b + 3:b + 4], in_=stat[:, b + 1:b + 2],
                         func=AF.Identity, scale=1.0 / DM).then_inc(sml.h, 1)
        C_(fA)
        sml.n += 1
        s_A = sml.n

        def fB(e, s_A=s_A, b=b, g=gget(("statB", par))):
            e.wait_ge(sml.h, s_A)
            need(e, g)
            e.tensor_scalar(out=stat[:, b + 4:b + 5], in0=stat[:, b + 2:b + 3],
                            scalar1=stat[:, b + 2:b + 3],
                            scalar2=stat[:, b + 3:b + 4],
                            op0=ALU.mult, op1=ALU.subtract).then_inc(vml.h, 1)
        V_(fB)
        vml.n += 1
        v_B = vml.n
        guard[("stats", par)] = (vml.h, v_B)

        def fC(e, v_B=v_B, b=b):
            e.wait_ge(vml.h, v_B)
            e.activation(out=stat[:, b + 5:b + 6], in_=stat[:, b + 4:b + 5],
                         func=AF.Ln, scale=-1.0,
                         bias=c_misc[:, 0:1]).then_inc(sml.h, 1)
        C_(fC)
        sml.n += 1
        s_C = sml.n

        def fD(e, s_C=s_C, b=b):
            e.wait_ge(sml.h, s_C)
            e.tensor_copy(out=stat[:, b + 7:b + 8],
                          in_=stat[:, b + 5:b + 6]).then_inc(vml.h, 1)
        V_(fD)
        vml.n += 1
        v_D = vml.n

        def fE(e, v_D=v_D, b=b):
            e.wait_ge(vml.h, v_D)
            e.activation(out=stat[:, b + 6:b + 7], in_=stat[:, b + 7:b + 8],
                         func=AF.Exp, scale=-0.5).then_inc(sml.h, 1)
        C_(fE)
        sml.n += 1
        s_E = sml.n
        guard[("statB", par)] = (sml.h, s_E)

        def fF(e, xtile=xtile, s_E=s_E, b=b, g=gget("xn")):
            e.wait_ge(sml.h, s_E)
            need(e, g)
            e.tensor_scalar(out=xn[:], in0=xtile[:], scalar1=stat[:, b + 2:b + 3],
                            scalar2=stat[:, b + 6:b + 7], op0=ALU.subtract,
                            op1=ALU.mult).then_inc(vml.h, 1)
        V_(fF)
        vml.n += 1
        guard[("statA", par)] = (vml.h, vml.n)
        return vml.n

    def transpose_to_big(i, v_xn):
        for bq in range(4):
            half = state_tr[0] % 2
            state_tr[0] += 1

            def ft(e, bq=bq, half=half, v_xn=v_xn, g=gget(("pTR", half))):
                e.wait_ge(vml.h, v_xn)
                need(e, g)
                for j in range(4):
                    dt = bq * 4 + j
                    ins = e.transpose(pTR[:, half * 1024 + j * 128:half * 1024 + (j + 1) * 128],
                                      xn[:, dt * 128:(dt + 1) * 128], c_id_bf[:])
                ins.then_inc(tmm.h, 1)
            T_(ft)
            tmm.n += 1
            t_tr = tmm.n

            def fe(e, bq=bq, half=half, t_tr=t_tr, i=i):
                e.wait_ge(tmm.h, t_tr)
                out_ap = BIG[:, :].rearrange("p (t q) -> p t q", q=SEQ)[
                    :, bq * 4:(bq + 1) * 4, i * 128:(i + 1) * 128]
                e.activation(out=out_ap, in_=pTR[:, half * 1024:half * 1024 + 512],
                             func=AF.Copy).then_inc(sml.h, 1)
            C_(fe)
            sml.n += 1
            guard[("pTR", half)] = (sml.h, sml.n)
        guard["xn"] = (tmm.h, tmm.n)
        return sml.n

    def barrier():
        t1, v1, s1, p1 = tmm.n, vml.n, sml.n, pml.n
        dsnap = [(d.h, d.n) for d in dma.values() if d.n > 0]

        def wt(e, t1=t1, v1=v1, s1=s1, p1=p1, dsnap=list(dsnap)):
            e.wait_ge(tmm.h, t1)
            e.wait_ge(vml.h, v1)
            e.wait_ge(sml.h, s1)
            if p1 > 0:
                e.wait_ge(pml.h, p1)
            for h, n in dsnap:
                e.wait_ge(h, n)
        for EM in (T_, V_, C_, G_, S_):
            EM(wt)
        guard.clear()

    def ffn_phase(call, src_res, dst_res, fuse_next_ln, big_tile_s, xt_free):
        cin, cout = cc_in[call], cc_out[call]

        def feb(e, call=call):
            e.dma_start(out=c_ebbf, in_=eb).then_inc(dma["geb"].h, 16)
            if call == 1:
                e.dma_start(out=WBUF[:].rearrange("p (t c) -> p t c", c=2048),
                            in_=ew.rearrange("(t p) c -> p t c", p=128)
                            ).then_inc(dma["geb"].h, 16)
        G_(feb)
        dma["geb"].n += 16 * (1 + (1 if call == 1 else 0))
        g_eb = dma["geb"].n

        st_prev = {}
        for i in range(NT):
            pR = pA if i % 2 == 0 else pB
            par = i % 2
            sb = par * 16

            def fr(e, i=i, pR=pR, bs=big_tile_s[i], g=gget(id(pR))):
                e.wait_ge(sml.h, bs)
                if i == 0:
                    e.wait_ge(dma["gmc"].h, GL_CONST)
                    e.wait_ge(dma["smc"].h, SL_CONST)
                need(e, g)
                for dt in range(ND):
                    e.matmul(pR[:, 0:16],
                             BIG[:, dt * SEQ + i * 128:dt * SEQ + (i + 1) * 128],
                             rwsb[:, dt * 16:(dt + 1) * 16],
                             start=(dt == 0), stop=False)
                e.matmul(pR[:, 0:16], c_ones[:], c_rb[:], start=False,
                         stop=True).then_inc(tmm.h, 1)
            T_(fr)
            tmm.n += 1
            t_lg = tmm.n

            def fx(e, pR=pR, t_lg=t_lg, sb=sb, par=par,
                   g=[gget(("rtacc", par))]):
                e.wait_ge(tmm.h, t_lg)
                need(e, g)
                e.activation(out=sown[:, sb:sb + 16], in_=pR[:, 0:16], func=AF.Exp,
                             accum_out=stat[:, 32 + par:33 + par])
                e.copy(out=stat[:, 40 + par:41 + par],
                       in_=stat[:, 32 + par:33 + par]).then_inc(sml.h, 1)
            C_(fx)
            sml.n += 1
            s_exp = sml.n
            guard[id(pR)] = (sml.h, sml.n)

            def fn(e, s_exp=s_exp, par=par):
                e.wait_ge(sml.h, s_exp)
                e.reciprocal(out=stat[:, 34 + par:35 + par],
                             in_=stat[:, 32 + par:33 + par]).then_inc(vml.h, 1)
            V_(fn)
            vml.n += 1
            v_sm = vml.n
            guard[("rtacc", par)] = (vml.h, v_sm)

            def fnm(e, v_sm=v_sm, sb=sb, par=par, g=gget(("sownn", par))):
                e.wait_ge(vml.h, v_sm)
                need(e, g)
                e.activation(out=sown[:, 32 + sb:32 + sb + 16],
                             in_=sown[:, sb:sb + 16], func=AF.Identity,
                             scale=stat[:, 34 + par:35 + par]).then_inc(sml.h, 1)
            C_(fnm)
            sml.n += 1
            s_nm = sml.n

            def fst(e, par=par, sb=sb, s_nm=s_nm, g=gget("bankS")):
                e.wait_ge(sml.h, s_nm)
                need(e, g)
                e.transpose(pS_st[par], sown[:, 32 + sb:32 + sb + 16],
                            c_id_f[:]).then_inc(tmm.h, 1)
            T_(fst)
            tmm.n += 1
            t_st = tmm.n

            def fse(e, i=i, par=par, t_st=t_st):
                e.wait_ge(tmm.h, t_st)
                e.activation(out=ST[0:16, i * 128:(i + 1) * 128], in_=pS_st[par],
                             func=AF.Copy).then_inc(sml.h, 1)
            C_(fse)
            sml.n += 1
            guard[("sownn", par)] = (tmm.h, t_st)
            guard["bankS"] = (sml.h, sml.n)
        st_all = sml.n

        def fcc(e, st_all=st_all, cin=cin, cout=cout, gs0=dma["gst"].n, cc0=ccs.n):
            e.wait_ge(sml.h, st_all)
            e.dma_start(out=cin[:, :], in_=ST[0:16, :]).then_inc(dma["gst"].h, 16)
            e.wait_ge(dma["gst"].h, gs0 + 16)
            e.collective_compute(
                "AllGather", ALU.bypass,
                replica_groups=[list(range(NCORES))],
                ins=[cin.ap()], outs=[cout.ap()],
            ).then_inc(ccs.h, 1)
            e.wait_ge(ccs.h, cc0 + 1)
            e.dma_start(out=sall[:], in_=cout[:, :]).then_inc(dma["gsall"].h, 16)
        G_(fcc)
        dma["gst"].n += 16
        ccs.n += 1
        dma["gsall"].n += 16
        g_sall = dma["gsall"].n

        def fs0(e, g_sall=g_sall):
            e.wait_ge(dma["gsall"].h, g_sall)
            e.memset(srch[:, :], 0.0)
            e.memset(srch[:, 2:3], 1.0)
            e.memset(srch[:, 4:5], 0.5).then_inc(vml.h, 1)
        V_(fs0)
        vml.n += 1

        for it in range(SEARCH_ITERS):
            par = it % 2
            nxt = 1 - par
            v_mid = vml.n

            def fmb(e, v_mid=v_mid, par=par):
                e.wait_ge(vml.h, v_mid)
                e.copy(out=srch[:, 12:13], in_=srch[:, 4 + par:5 + par]).then_inc(sml.h, 1)
            C_(fmb)
            sml.n += 1
            s_mid = sml.n

            def fsa(e, s_mid=s_mid):
                e.wait_ge(sml.h, s_mid)
                e.tensor_scalar(out=e0[:], in0=sall[:], scalar1=srch[:, 12:13],
                                scalar2=0.0, op0=ALU.is_ge, op1=ALU.add,
                                accum_out=srch[:, 13:14])
                e.tensor_copy(out=srch[:, 14:15], in_=srch[:, 13:14]).then_inc(vml.h, 1)
            V_(fsa)
            vml.n += 1
            v_cnt = vml.n

            def fsb(e, v_cnt=v_cnt, g=gget("bankS")):
                e.wait_ge(vml.h, v_cnt)
                need(e, g)
                e.matmul(pS_cnt, c_mgrp[:], srch[:, 13:14], start=True,
                         stop=True).then_inc(tmm.h, 1)
            T_(fsb)
            tmm.n += 1
            t_cnt = tmm.n

            def fsA(e, t_cnt=t_cnt, par=par):
                e.wait_ge(tmm.h, t_cnt)
                e.tensor_scalar(out=srch[:, 8:9], in0=pS_cnt, scalar1=float(TOPK),
                                scalar2=None, op0=ALU.is_ge)
                e.tensor_tensor(out=srch[:, 6:7], in0=srch[:, 4 + par:5 + par],
                                in1=srch[:, 0 + par:1 + par], op=ALU.subtract)
                e.tensor_tensor(out=srch[:, 7:8], in0=srch[:, 2 + par:3 + par],
                                in1=srch[:, 4 + par:5 + par],
                                op=ALU.subtract).then_inc(vml.h, 1)
            V_(fsA)
            vml.n += 1
            v_A = vml.n
            guard["bankS"] = (vml.h, v_A)

            def fsB(e, v_A=v_A):
                e.wait_ge(vml.h, v_A)
                e.copy(out=srch[:, 9:10], in_=srch[:, 8:9])
                e.copy(out=srch[:, 10:11], in_=srch[:, 6:7])
                e.copy(out=srch[:, 11:12], in_=srch[:, 7:8]).then_inc(sml.h, 1)
            C_(fsB)
            sml.n += 1
            s_B = sml.n

            def fsC(e, s_B=s_B, par=par, nxt=nxt):
                e.wait_ge(sml.h, s_B)
                e.scalar_tensor_tensor(out=srch[:, 0 + nxt:1 + nxt],
                                       in0=srch[:, 10:11], scalar=srch[:, 9:10],
                                       in1=srch[:, 0 + par:1 + par],
                                       op0=ALU.mult, op1=ALU.add)
                e.scalar_tensor_tensor(out=srch[:, 2 + nxt:3 + nxt],
                                       in0=srch[:, 11:12], scalar=srch[:, 9:10],
                                       in1=srch[:, 4 + par:5 + par],
                                       op0=ALU.mult, op1=ALU.add).then_inc(vml.h, 1)
            V_(fsC)
            vml.n += 1
            v_C = vml.n

            def fsD(e, v_C=v_C, nxt=nxt):
                e.wait_ge(vml.h, v_C)
                e.activation(out=srch[:, 10:11], in_=srch[:, 0 + nxt:1 + nxt],
                             func=AF.Identity, scale=0.5)
                e.activation(out=srch[:, 11:12], in_=srch[:, 2 + nxt:3 + nxt],
                             func=AF.Identity, scale=0.5).then_inc(sml.h, 1)
            C_(fsD)
            sml.n += 1
            s_D = sml.n

            def fsE(e, s_D=s_D, nxt=nxt):
                e.wait_ge(sml.h, s_D)
                e.tensor_tensor(out=srch[:, 4 + nxt:5 + nxt], in0=srch[:, 10:11],
                                in1=srch[:, 11:12], op=ALU.add).then_inc(vml.h, 1)
            V_(fsE)
            vml.n += 1
        LOF = SEARCH_ITERS % 2

        v_lo = vml.n

        def flb(e, v_lo=v_lo, LOF=LOF):
            e.wait_ge(vml.h, v_lo)
            e.copy(out=srch[:, 15:16], in_=srch[:, LOF:LOF + 1]).then_inc(sml.h, 1)
        C_(flb)
        sml.n += 1
        s_lo = sml.n

        def fsn1(e, s_lo=s_lo):
            e.wait_ge(sml.h, s_lo)
            e.scalar_tensor_tensor(out=e0[:], in0=sall[:], scalar=srch[:, 15:16],
                                   in1=sall[:], op0=ALU.is_ge,
                                   op1=ALU.bypass).then_inc(vml.h, 1)
        V_(fsn1)
        vml.n += 1
        v_n1 = vml.n

        def fsn2(e, v_n1=v_n1):
            e.wait_ge(vml.h, v_n1)
            e.activation(out=xn[:], in_=e0[:], func=AF.Identity, scale=-2.0,
                         bias=c_misc[:, 3:4]).then_inc(sml.h, 1)
        C_(fsn2)
        sml.n += 1
        s_n2 = sml.n

        def fsn3(e, s_n2=s_n2):
            e.wait_ge(sml.h, s_n2)
            e.tensor_tensor(out=snapt[:], in0=sall[:], in1=xn[:], op=ALU.add)
            e.tensor_reduce(out=srch[:, 6:7], in_=snapt[:], axis=AX.X,
                            op=ALU.min).then_inc(vml.h, 1)
        V_(fsn3)
        vml.n += 1
        v_cm = vml.n

        def fst2(e, v_cm=v_cm, g=gget("bankS")):
            e.wait_ge(vml.h, v_cm)
            need(e, g)
            e.transpose(pS_r0, srch[:, 6:7], c_id_f[:]).then_inc(tmm.h, 1)
        T_(fst2)
        tmm.n += 1
        t_cm = tmm.n

        def fmx(e, t_cm=t_cm):
            e.wait_ge(tmm.h, t_cm)
            e.tensor_reduce(out=srow[:], in_=pS_r0.rearrange("p (r e) -> p e r", r=8),
                            axis=AX.X, op=ALU.min).then_inc(vml.h, 1)
        V_(fmx)
        vml.n += 1
        guard["bankS"] = (vml.h, vml.n)
        v_th = vml.n

        def fth(e, v_th=v_th, g=gget("bankS")):
            e.wait_ge(vml.h, v_th)
            need(e, g)
            e.transpose(pS_r16, srow[:], c_id_f[0:1, 0:1]).then_inc(tmm.h, 1)
        T_(fth)
        tmm.n += 1
        t_th = tmm.n

        def fthc(e, t_th=t_th):
            e.wait_ge(tmm.h, t_th)
            e.tensor_copy(out=srch[0:16, 7:8], in_=pS_r16).then_inc(vml.h, 1)
        V_(fthc)
        vml.n += 1
        guard["bankS"] = (vml.h, vml.n)
        v_thc = vml.n

        def fthx(e, v_thc=v_thc, g=gget("bankS")):
            e.wait_ge(vml.h, v_thc)
            need(e, g)
            e.matmul(pS_thx, c_repl[:], srch[0:16, 7:8], start=True,
                     stop=True).then_inc(tmm.h, 1)
        T_(fthx)
        tmm.n += 1
        t_thx = tmm.n

        def fw0(e, t_thx=t_thx):
            e.wait_ge(tmm.h, t_thx)
            e.activation(out=srch[:, 8:9], in_=pS_thx, func=AF.Copy).then_inc(sml.h, 1)
        C_(fw0)
        sml.n += 1
        s_th = sml.n
        guard["bankS"] = (sml.h, sml.n)

        def fw1(e, s_th=s_th):
            e.wait_ge(sml.h, s_th)
            e.scalar_tensor_tensor(out=snapt[:], in0=sall[:], scalar=srch[:, 8:9],
                                   in1=sall[:], op0=ALU.is_ge,
                                   op1=ALU.mult).then_inc(vml.h, 1)
        V_(fw1)
        vml.n += 1
        v_msk = vml.n

        def fw2(e, v_msk=v_msk, g=gget("bankS")):
            e.wait_ge(vml.h, v_msk)
            need(e, g)
            for i in range(NT):
                ins = e.matmul(pS_w[:, i:i + 1], snapt[:, i * 128:(i + 1) * 128],
                               c_selr[:], start=True, stop=True)
            ins.then_inc(tmm.h, 1)
        T_(fw2)
        tmm.n += 1
        t_w = tmm.n

        def fw3(e, t_w=t_w):
            e.wait_ge(tmm.h, t_w)
            e.activation(out=wcol[:], in_=pS_w, func=AF.Identity,
                         scale=0.5).then_inc(sml.h, 1)
        C_(fw3)
        sml.n += 1
        guard["bankS"] = (sml.h, sml.n)
        v_w = None
        s_w = sml.n

        y_evac = {}
        apply_v = dict(xt_free)
        store_g = {}
        big_next = [None] * NT
        zt_guard = [gget("zt"), gget("zt_store")]

        def emit_y(i):
            slot = (i % 4) * 2048
            for cch in range(4):
                pY = pA if (i * 4 + cch) % 2 == 0 else pB

                def fy(e, i=i, cch=cch, pY=pY, bs=big_tile_s[i], g=gget(id(pY)),
                       geb=(g_eb if i == 0 and cch == 0 else None)):
                    e.wait_ge(sml.h, bs)
                    if geb is not None:
                        e.wait_ge(dma["geb"].h, geb)
                    need(e, g)
                    for dt in range(ND):
                        e.matmul(pY[:],
                                 BIG[:, dt * SEQ + i * 128:dt * SEQ + (i + 1) * 128],
                                 WBUF[:, dt * DM + cch * 512:dt * DM + (cch + 1) * 512],
                                 start=(dt == 0), stop=False)
                    e.matmul(pY[:], c_onesb, c_ebbf[:, cch * 512:(cch + 1) * 512],
                             start=False, stop=True).then_inc(tmm.h, 1)
                T_(fy)
                tmm.n += 1
                t_y = tmm.n

                def fyev(e, i=i, cch=cch, pY=pY, t_y=t_y, slot=slot,
                         w=apply_v.get(i - 4)):
                    e.wait_ge(tmm.h, t_y)
                    if w is not None:
                        e.wait_ge(vml.h, w)
                    e.activation(out=SCR[:, slot + cch * 512:slot + (cch + 1) * 512],
                                 in_=pY[:], func=AF.Copy).then_inc(sml.h, 1)
                C_(fyev)
                sml.n += 1
                guard[id(pY)] = (sml.h, sml.n)
            y_evac[i] = sml.n

        def emit_apply(i):
            slot = (i % 4) * 2048
            xb = xt[i % 2]

            def fxl(e, i=i, xb=xb, w=apply_v.get(i - 2), sh=SX[i % 2].h):
                if w is not None:
                    e.wait_ge(vml.h, w)
                e.dma_start(out=xb[:], in_=src_res[i * 128:(i + 1) * 128, :]
                            ).then_inc(sh, 16)
            S_(fxl)
            SX[i % 2].n += 16
            sl_x = (SX[i % 2].h, SX[i % 2].n)

            def fap(e, i=i, xb=xb, slot=slot, sl_x=sl_x, sy=max(y_evac[i], s_w),
                    zg=list(zt_guard)):
                e.wait_ge(sml.h, sy)
                e.wait_ge(sl_x[0], sl_x[1])
                for g in zg:
                    need(e, g)
                e.scalar_tensor_tensor(out=zt[:], in0=SCR[:, slot:slot + 2048],
                                       scalar=wcol[:, i:i + 1], in1=xb[:],
                                       op0=ALU.mult, op1=ALU.add).then_inc(vml.h, 1)
            V_(fap)
            vml.n += 1
            v_z = vml.n
            apply_v[i] = v_z

            def fzs(e, i=i, v_z=v_z):
                e.wait_ge(vml.h, v_z)
                e.dma_start(out=dst_res[i * 128:(i + 1) * 128, :], in_=zt[:]
                            ).then_inc(dma["gzs"].h, 16)
            G_(fzs)
            dma["gzs"].n += 16
            g_z = dma["gzs"].n
            store_g[i] = g_z

            if fuse_next_ln:
                s_st = ln_stats(zt, [(vml.h, v_z)], i % 2)
                v_xn = ln_normalize(zt, s_st, i % 2)
                s_big = transpose_to_big(i, v_xn)
                big_next[i] = s_big
                zt_guard[0] = (vml.h, v_xn)
                zt_guard[1] = (dma["gzs"].h, g_z)
            else:
                zt_guard[0] = (dma["gzs"].h, g_z)
                zt_guard[1] = None

        for i in range(NT):
            emit_y(i)
            if i >= 3:
                emit_apply(i - 3)
        for i in range(NT - 3, NT):
            emit_apply(i)
        guard["zt"] = zt_guard[0]
        guard["zt_store"] = zt_guard[1]
        return big_next

    def initial_ln():
        big_tile_s = [None] * NT
        xn_done = {}
        for i in range(NT):
            xb = xt[i % 2]

            def fl(e, i=i, xb=xb, w=xn_done.get(i - 2), sh=SX[i % 2].h):
                if w is not None:
                    e.wait_ge(vml.h, w)
                e.dma_start(out=xb[:], in_=x_in[i * 128:(i + 1) * 128, :]
                            ).then_inc(sh, 16)
            S_(fl)
            SX[i % 2].n += 16
            s_st = ln_stats(xb, [(SX[i % 2].h, SX[i % 2].n)], i % 2)
            if sub < 2:
                xn_done[i] = vml.n
                big_tile_s[i] = sml.n
                continue
            v_xn = ln_normalize(xb, s_st, i % 2)
            xn_done[i] = v_xn
            if sub < 3:
                big_tile_s[i] = sml.n
                continue
            big_tile_s[i] = transpose_to_big(i, v_xn)
        xt_free = {-2: xn_done[NT - 2], -1: xn_done[NT - 1]}
        return big_tile_s, xt_free

    def kq_rope_group(i, pP, colw, dst_col_fn, t_mm, rope_state):
        ngr = colw // 256

        def fr1(e, i=i, ngr=ngr, pP=pP, t_mm=t_mm, gr=rope_state.get("g"),
                w=rope_state.get("v2")):
            e.wait_ge(tmm.h, t_mm)
            e.wait_ge(dma["grope"].h, gr)
            if w is not None:
                e.wait_ge(pml.h, w)
            CF = c_rope_c[:, i * 256:(i + 1) * 256]
            SF3 = c_rope_s[:, i * 256:(i + 1) * 256].rearrange(
                "p (j t) -> p j t", t=2)
            for g2 in range(ngr):
                b = g2 * 256
                pg3 = pP[:, b:b + 256].rearrange("p (j t) -> p j t", t=2)
                q3 = qrt[:, 512 + b:512 + b + 256].rearrange("p (j t) -> p j t", t=2)
                e.tensor_tensor(out=qrt[:, b:b + 256], in0=pP[:, b:b + 256],
                                in1=CF, op=ALU.mult)
                e.tensor_tensor(out=q3[:, :, 0:1], in0=pg3[:, :, 1:2],
                                in1=SF3[:, :, 0:1], op=ALU.mult)
                ins = e.tensor_tensor(out=q3[:, :, 1:2], in0=pg3[:, :, 0:1],
                                      in1=SF3[:, :, 1:2], op=ALU.mult)
            ins.then_inc(vml.h, 1)
        V_(fr1)
        vml.n += 1
        v_r1 = vml.n

        def fr2(e, colw=colw, v_r1=v_r1, w=rope_state.get("t")):
            e.wait_ge(vml.h, v_r1)
            if w is not None:
                e.wait_ge(tmm.h, w)
            e.tensor_tensor(out=qro[:, 0:colw], in0=qrt[:, 0:colw],
                            in1=qrt[:, 512:512 + colw],
                            op=ALU.add).then_inc(pml.h, 1)
        G_(fr2)
        pml.n += 1
        p_r2 = pml.n
        rope_state["v2"] = p_r2

        ntr = colw // 128
        half = state_tr[0] % 2
        state_tr[0] += 1

        def ftr(e, ntr=ntr, half=half, p_r2=p_r2, g=gget(("pTR", half))):
            e.wait_ge(pml.h, p_r2)
            need(e, g)
            for j in range(ntr):
                ins = e.transpose(pTR[:, half * 1024 + j * 128:half * 1024 + (j + 1) * 128],
                                  qro[:, j * 128:(j + 1) * 128], c_id_bf[:])
            ins.then_inc(tmm.h, 1)
        T_(ftr)
        tmm.n += 1
        t_tr = tmm.n

        def fte(e, ntr=ntr, half=half, t_tr=t_tr, dst=dst_col_fn):
            e.wait_ge(tmm.h, t_tr)
            for j in range(ntr):
                e.activation(out=dst(j), in_=pTR[:, half * 1024 + j * 128:
                                               half * 1024 + (j + 1) * 128],
                             func=AF.Copy).then_inc(sml.h, 1)
        C_(fte)
        sml.n += ntr
        guard[("pTR", half)] = (sml.h, sml.n)
        return sml.n, v_r1

    def attention_phase(nheads=H, nqi=NT):
        rope_state = {}
        state_u = {}
        state_d = {}
        state_att = {}
        state_att2 = {}

        def flq(e):
            e.dma_start(out=lqk[0], in_=lq1).then_inc(dma["smc"].h, 16)
            e.dma_start(out=lqk[1], in_=lq2).then_inc(dma["smc"].h, 16)
            e.dma_start(out=lqk[2], in_=lk1).then_inc(dma["smc"].h, 16)
            e.dma_start(out=lqk[3], in_=lk2).then_inc(dma["smc"].h, 16)
        S_(flq)
        dma["smc"].n += 64
        sl_lqk = dma["smc"].n

        def fl1(e, sl_lqk=sl_lqk):
            e.wait_ge(dma["smc"].h, sl_lqk)
            e.scalar_tensor_tensor(out=lqk[0], in0=lqk[0], scalar=1.0,
                                   in1=lqk[2], op0=ALU.mult, op1=ALU.mult,
                                   accum_out=lsc[:, 0:1])
            e.scalar_tensor_tensor(out=lqk[1], in0=lqk[1], scalar=1.0,
                                   in1=lqk[3], op0=ALU.mult, op1=ALU.mult,
                                   accum_out=lsc[:, 1:2])
            e.tensor_copy(out=lsc[:, 6:7], in_=lsc[:, 1:2]).then_inc(vml.h, 1)
        V_(fl1)
        vml.n += 1
        v_l1 = vml.n

        def fl2(e, v_l1=v_l1):
            e.wait_ge(vml.h, v_l1)
            e.activation(out=lsc[:, 2:4], in_=lsc[:, 0:2], func=AF.Exp).then_inc(sml.h, 1)
        C_(fl2)
        sml.n += 1
        s_l2 = sml.n

        def fl3(e, s_l2=s_l2):
            e.wait_ge(sml.h, s_l2)
            e.tensor_tensor(out=lsc[:, 4:5], in0=lsc[:, 2:3], in1=lsc[:, 3:4],
                            op=ALU.subtract).then_inc(vml.h, 1)
        V_(fl3)
        vml.n += 1
        v_l3a = vml.n

        def fl3b(e, v_l3a=v_l3a):
            e.wait_ge(vml.h, v_l3a)
            e.activation(out=lsc[:, 5:6], in_=lsc[:, 4:5], func=AF.Identity,
                         bias=c_misc[0:16, 1:2]).then_inc(sml.h, 1)
        C_(fl3b)
        sml.n += 1
        v_l3 = sml.n

        def fl4(e, v_l3=v_l3):
            e.wait_ge(sml.h, v_l3)
            e.transpose(pS_lr, lsc[:, 5:6], c_id_f[0:16, 0:16]).then_inc(tmm.h, 1)
        T_(fl4)
        tmm.n += 1
        t_l4 = tmm.n

        def fl5(e, t_l4=t_l4):
            e.wait_ge(tmm.h, t_l4)
            e.tensor_copy(out=srow[:], in_=pS_lr).then_inc(vml.h, 1)
        V_(fl5)
        vml.n += 1
        v_l5 = vml.n

        def fl6(e, v_l5=v_l5):
            e.wait_ge(vml.h, v_l5)
            e.matmul(pS_lam, c_ones[:], srow[:], start=True,
                     stop=True).then_inc(tmm.h, 1)
        T_(fl6)
        tmm.n += 1
        t_l6 = tmm.n

        def fl7(e, t_l6=t_l6):
            e.wait_ge(tmm.h, t_l6)
            e.tensor_copy(out=lamc[:], in_=pS_lam).then_inc(vml.h, 1)
        V_(fl7)
        vml.n += 1
        v_lam = vml.n

        def frt(e):
            e.dma_start(out=SCR[:, 0:NT * 256].rearrange("p (t c) -> p t c", c=256),
                        in_=ropec.rearrange("(t p) c -> p t c", p=128)
                        ).then_inc(dma["grope"].h, 16)
            e.dma_start(out=SCR[:, NT * 256:2 * NT * 256].rearrange("p (t c) -> p t c", c=256),
                        in_=ropes.rearrange("(t p) c -> p t c", p=128)
                        ).then_inc(dma["grope"].h, 16)
        G_(frt)
        dma["grope"].n += 32
        rope_state["g"] = dma["grope"].n

        def fwk(e):
            e.dma_start(out=WBUF[:, WSLOT:WSLOT + ND * 1024].rearrange("p (t c) -> p t c", c=1024),
                        in_=wk.rearrange("(t p) c -> p t c", p=128)
                        ).then_inc(dma["gwk"].h, 16)
        G_(fwk)
        dma["gwk"].n += 16
        g_wk = dma["gwk"].n

        for i in range(NT):
            for cch in range(2):
                pP = pA if (i * 2 + cch) % 2 == 0 else pB

                def fkm(e, i=i, cch=cch, pP=pP, g_wk=g_wk, g=gget(id(pP))):
                    e.wait_ge(dma["gwk"].h, g_wk)
                    need(e, g)
                    for dt in range(ND):
                        ins = e.matmul(pP[:],
                                       BIG[:, dt * SEQ + i * 128:dt * SEQ + (i + 1) * 128],
                                       WBUF[:, WSLOT + dt * 1024 + cch * 512:
                                            WSLOT + dt * 1024 + (cch + 1) * 512],
                                       start=(dt == 0), stop=(dt == ND - 1))
                    ins.then_inc(tmm.h, 1)
                T_(fkm)
                tmm.n += 1
                t_mm = tmm.n

                def dstf(j, i=i, cch=cch):
                    kc = cch * 4 + j
                    return WBUF[:, KT0 + kc * SEQ + i * 128:KT0 + kc * SEQ + (i + 1) * 128]
                s_done, v_ev = kq_rope_group(i, pP, 512, dstf, t_mm, rope_state)
                guard[id(pP)] = (vml.h, v_ev)
        kT_ready = sml.n

        def fwv(e, kT_ready=kT_ready):
            e.wait_ge(sml.h, kT_ready)
            e.dma_start(out=WBUF[:, WSLOT:WSLOT + ND * 512].rearrange("p (t c) -> p t c", c=512),
                        in_=wv.rearrange("(t p) c -> p t c", p=128)
                        ).then_inc(dma["gwv"].h, 16)
        G_(fwv)
        dma["gwv"].n += 16
        g_wv = dma["gwv"].n

        for i in range(NT):
            pP = pA if i % 2 == 0 else pB

            def fvm(e, i=i, pP=pP, g_wv=g_wv, g=gget(id(pP))):
                e.wait_ge(dma["gwv"].h, g_wv)
                need(e, g)
                for dt in range(ND):
                    ins = e.matmul(pP[:], BIG[:, dt * SEQ + i * 128:dt * SEQ + (i + 1) * 128],
                                   WBUF[:, WSLOT + dt * 512:WSLOT + (dt + 1) * 512],
                                   start=(dt == 0), stop=(dt == ND - 1))
                ins.then_inc(tmm.h, 1)
            T_(fvm)
            tmm.n += 1
            t_v = tmm.n

            def fve(e, i=i, pP=pP, t_v=t_v):
                e.wait_ge(tmm.h, t_v)
                e.activation(out=WBUF[:, VS0 + i * 512:VS0 + (i + 1) * 512],
                             in_=pP[:], func=AF.Copy).then_inc(sml.h, 1)
            C_(fve)
            sml.n += 1
            guard[id(pP)] = (sml.h, sml.n)
        v_ready = sml.n

        WQS = [WSLOT, WSLOT + ND * 256]
        wq_user = {}
        for h in range(nheads):
            slot = WQS[h % 2]
            kv = h // 4

            def fwq(e, h=h, slot=slot, w=wq_user.get(h - 2), vr=v_ready):
                if h < 2:
                    e.wait_ge(sml.h, vr)
                if w is not None:
                    e.wait_ge(tmm.h, w)
                e.dma_start(out=WBUF[:, slot:slot + ND * 256].rearrange(
                                "p (t c) -> p t c", c=256),
                            in_=wq[:, h * 256:(h + 1) * 256].rearrange(
                                "(t p) c -> p t c", p=128)).then_inc(GWQ[h % 2].h, 16)
            G_(fwq)
            GWQ[h % 2].n += 16
            g_wq = (GWQ[h % 2].h, GWQ[h % 2].n)

            for i in range(NT):
                pP = pA if i % 2 == 0 else pB

                def fqm(e, i=i, pP=pP, slot=slot, g_wq=g_wq, g=gget(id(pP)),
                        qg=gget("qTh") if i == 0 else None):
                    e.wait_ge(g_wq[0], g_wq[1])
                    need(e, g)
                    need(e, qg)
                    for dt in range(ND):
                        ins = e.matmul(pP[:, 0:256],
                                       BIG[:, dt * SEQ + i * 128:dt * SEQ + (i + 1) * 128],
                                       WBUF[:, slot + dt * 256:slot + (dt + 1) * 256],
                                       start=(dt == 0), stop=(dt == ND - 1))
                    ins.then_inc(tmm.h, 1)
                T_(fqm)
                tmm.n += 1
                t_mm = tmm.n

                def dstq(j, i=i):
                    return qTh[:, j * SEQ + i * 128:j * SEQ + (i + 1) * 128]
                s_done, v_ev = kq_rope_group(i, pP, 256, dstq, t_mm, rope_state)
                guard[id(pP)] = (vml.h, v_ev)
            wq_user[h] = tmm.n
            q_ready = sml.n

            EB0 = [e0, e0b]
            EB1 = [e1, e1b]
            ACC = [(sstat, 32, 36), (srch, 16, 20)]
            CHR = [(sstat, 40), (srch, 24)]

            def stageA(qi):
                par = qi % 2
                klen = (qi + 1) * 128
                nch = (klen + 511) // 512
                accT, a0, a1 = ACC[par]
                for kc in range(nch):
                    wch = min(512, klen - kc * 512)
                    isdiag = (kc == nch - 1)
                    hb = (kc % 2) * 512
                    hE0 = pE0[:, hb:hb + wch]
                    hE1 = pE1[:, hb:hb + wch]

                    def fem(e, qi=qi, kc=kc, wch=wch, isdiag=isdiag,
                            hE0=hE0, hE1=hE1, kv=kv,
                            g0=gget(("pE0", kc % 2)), g1=gget(("pE1", kc % 2)),
                            kr=(q_ready if (qi == 0 and kc == 0) else None),
                            lr=(v_lam if (h == 0 and qi == 0) else None)):
                        if kr is not None:
                            e.wait_ge(sml.h, kr)
                        if lr is not None:
                            e.wait_ge(vml.h, lr)
                        need(e, g0)
                        need(e, g1)
                        for a, hE in ((0, hE0), (1, hE1)):
                            kcg = kv * 2 + a
                            ins = e.matmul(hE,
                                           qTh[:, a * SEQ + qi * 128:a * SEQ + (qi + 1) * 128],
                                           WBUF[:, KT0 + kcg * SEQ + kc * 512:
                                                KT0 + kcg * SEQ + kc * 512 + wch],
                                           start=True, stop=not isdiag)
                            if isdiag:
                                ins = e.matmul(hE[:, wch - 128:wch], c_id_bf[:],
                                               c_emask[:], start=False, stop=True)
                        ins.then_inc(tmm.h, 1)
                    T_(fem)
                    tmm.n += 1
                    t_e = tmm.n

                    def fex(e, par=par, kc=kc, wch=wch, hE0=hE0, hE1=hE1, t_e=t_e,
                            accT=accT, a0=a0, a1=a1,
                            g=(gget(("eb", par)) if kc == 0 else None),
                            g2=(gget(("sacc", par)) if kc == 0 else None)):
                        e.wait_ge(tmm.h, t_e)
                        need(e, g)
                        need(e, g2)
                        e.activation(out=EB0[par][:, kc * 512:kc * 512 + wch], in_=hE0,
                                     func=AF.Exp, scale=CEXP,
                                     accum_out=accT[:, a0 + kc:a0 + kc + 1])
                        e.activation(out=EB1[par][:, kc * 512:kc * 512 + wch], in_=hE1,
                                     func=AF.Exp, scale=CEXP,
                                     accum_out=accT[:, a1 + kc:a1 + kc + 1])
                        e.copy(out=sstat[:, 59:60],
                               in_=accT[:, a1 + kc:a1 + kc + 1]).then_inc(sml.h, 1)
                    C_(fex)
                    sml.n += 1
                    guard[("pE0", kc % 2)] = (sml.h, sml.n)
                    guard[("pE1", kc % 2)] = (sml.h, sml.n)
                return sml.n

            def stageB(qi, s_exps):
                par = qi % 2
                klen = (qi + 1) * 128
                nch = (klen + 511) // 512
                accT, a0, a1 = ACC[par]
                chT, cb = CHR[par]

                def fg1(e, nch=nch, s_exps=s_exps, accT=accT, a0=a0, a1=a1,
                        chT=chT, cb=cb):
                    e.wait_ge(sml.h, s_exps)
                    if nch > 1:
                        e.tensor_reduce(out=chT[:, cb:cb + 1],
                                        in_=accT[:, a0:a0 + nch],
                                        axis=AX.X, op=ALU.add)
                        e.tensor_reduce(out=chT[:, cb + 1:cb + 2],
                                        in_=accT[:, a1:a1 + nch],
                                        axis=AX.X, op=ALU.add).then_inc(vml.h, 1)
                    else:
                        e.tensor_copy(out=chT[:, cb:cb + 1], in_=accT[:, a0:a0 + 1])
                        e.tensor_copy(out=chT[:, cb + 1:cb + 2],
                                      in_=accT[:, a1:a1 + 1]).then_inc(vml.h, 1)
                V_(fg1)
                vml.n += 1
                v_g1 = vml.n
                guard[("sacc", par)] = (vml.h, v_g1)

                def fg2(e, v_g1=v_g1, chT=chT, cb=cb):
                    e.wait_ge(vml.h, v_g1)
                    e.copy(out=chT[:, cb + 2:cb + 3], in_=chT[:, cb + 1:cb + 2])
                    e.copy(out=chT[:, cb + 3:cb + 4],
                           in_=chT[:, cb:cb + 1]).then_inc(sml.h, 1)
                C_(fg2)
                sml.n += 1
                s_g2 = sml.n

                def fg3(e, s_g2=s_g2, chT=chT, cb=cb):
                    e.wait_ge(sml.h, s_g2)
                    e.reciprocal(out=chT[:, cb + 4:cb + 5], in_=chT[:, cb + 2:cb + 3])
                    e.reciprocal(out=chT[:, cb + 5:cb + 6],
                                 in_=chT[:, cb + 3:cb + 4]).then_inc(vml.h, 1)
                V_(fg3)
                vml.n += 1
                v_g3 = vml.n

                def fg4(e, v_g3=v_g3, chT=chT, cb=cb):
                    e.wait_ge(vml.h, v_g3)
                    e.copy(out=chT[:, cb + 6:cb + 7], in_=chT[:, cb + 4:cb + 5])
                    e.activation(out=chT[:, cb + 7:cb + 8], in_=chT[:, cb + 5:cb + 6],
                                 func=AF.Identity, scale=-1.0).then_inc(sml.h, 1)
                C_(fg4)
                sml.n += 1
                s_g4 = sml.n

                def fg5(e, h=h, s_g4=s_g4, chT=chT, cb=cb, par=par):
                    e.wait_ge(sml.h, s_g4)
                    e.tensor_scalar(out=sstat[:, 53 + par:54 + par],
                                    in0=lamc[:, h:h + 1],
                                    scalar1=chT[:, cb + 3:cb + 4],
                                    scalar2=chT[:, cb + 6:cb + 7],
                                    op0=ALU.mult, op1=ALU.mult)
                    e.tensor_copy(out=sstat[:, 57 + par:58 + par],
                                  in_=chT[:, cb + 7:cb + 8]).then_inc(vml.h, 1)
                V_(fg5)
                vml.n += 1
                v_g5 = vml.n

                def fg6(e, v_g5=v_g5, par=par):
                    e.wait_ge(vml.h, v_g5)
                    e.copy(out=sstat[:, 55 + par:56 + par],
                           in_=sstat[:, 53 + par:54 + par]).then_inc(sml.h, 1)
                C_(fg6)
                sml.n += 1
                s_gb = sml.n

                fcb_last = None
                for kc in range(nch):
                    wch = min(512, klen - kc * 512)
                    ub = (kc % 2) * 512
                    isdiag = (kc == nch - 1)

                    def fcb(e, par=par, kc=kc, wch=wch, ub=ub, s_gb=s_gb,
                            w=state_u.get(kc % 2)):
                        e.wait_ge(sml.h, max(s_gb, w if w is not None else 0))
                        e.scalar_tensor_tensor(out=uneg[:, ub:ub + wch],
                                               in0=EB1[par][:, kc * 512:kc * 512 + wch],
                                               scalar=sstat[:, 55 + par:56 + par],
                                               in1=EB0[par][:, kc * 512:kc * 512 + wch],
                                               op0=ALU.mult,
                                               op1=ALU.subtract).then_inc(vml.h, 1)
                    V_(fcb)
                    vml.n += 1
                    v_u = vml.n

                    def fed(e, par=par, kc=kc, wch=wch, ub=ub, v_u=v_u,
                            w=state_d.get(kc % 2),
                            w2=(state_d.get("vsd") if kc == 0 else None)):
                        e.wait_ge(vml.h, v_u)
                        if w is not None:
                            e.wait_ge(tmm.h, w)
                        if w2 is not None:
                            e.wait_ge(vml.h, w2)
                        e.activation(out=dchk[:, ub:ub + wch], in_=uneg[:, ub:ub + wch],
                                     func=AF.Exp, scale=sstat[:, 57 + par:58 + par],
                                     accum_out=sstat[:, 48 + kc:49 + kc]).then_inc(sml.h, 1)
                    C_(fed)
                    sml.n += 1
                    state_u[kc % 2] = sml.n
                    s_d = sml.n

                    ntr = wch // 128
                    half = state_tr[0] % 2
                    state_tr[0] += 1

                    def ftd(e, ntr=ntr, ub=ub, half=half, s_d=s_d,
                            g=gget(("pTR", half))):
                        e.wait_ge(sml.h, s_d)
                        need(e, g)
                        for j in range(ntr):
                            ins = e.transpose(pTR[:, half * 1024 + j * 128:
                                                  half * 1024 + (j + 1) * 128],
                                              dchk[:, ub + j * 128:ub + (j + 1) * 128],
                                              c_id_bf[:])
                        ins.then_inc(tmm.h, 1)
                    T_(ftd)
                    tmm.n += 1
                    t_d = tmm.n
                    state_d[kc % 2] = t_d

                    ncopy = ntr - 1 if isdiag else ntr
                    if ncopy > 0:
                        def fde(e, kc=kc, ncopy=ncopy, half=half, t_d=t_d):
                            e.wait_ge(tmm.h, t_d)
                            e.tensor_copy(out=xn[:, kc * 512:kc * 512 + ncopy * 128],
                                          in_=pTR[:, half * 1024:
                                                   half * 1024 + ncopy * 128]
                                          ).then_inc(vml.h, 1)
                        V_(fde)
                        vml.n += 1
                    if isdiag:
                        def fdm(e, ntr=ntr, half=half, t_d=t_d, qi=qi):
                            e.wait_ge(tmm.h, t_d)
                            e.tensor_tensor(out=xn[:, qi * 128:(qi + 1) * 128],
                                            in0=pTR[:, half * 1024 + (ntr - 1) * 128:
                                                    half * 1024 + ntr * 128],
                                            in1=c_maskt[:], op=ALU.mult).then_inc(vml.h, 1)
                        V_(fdm)
                        vml.n += 1
                    guard[("pTR", half)] = (vml.h, vml.n)
                guard[("eb", par)] = (vml.h, vml.n - 1)
                s_dT_s, s_dT_v = sml.n, vml.n

                pV = pA if qi % 2 == 0 else pB

                def fpv(e, qi=qi, kv=kv, pV=pV, s_dT_s=s_dT_s, s_dT_v=s_dT_v,
                        g=gget(id(pV)),
                        vr=(v_ready if (h == 0 and qi == 0) else None)):
                    e.wait_ge(sml.h, s_dT_s)
                    e.wait_ge(vml.h, s_dT_v)
                    if vr is not None:
                        e.wait_ge(sml.h, vr)
                    need(e, g)
                    for kt in range(qi + 1):
                        ins = e.matmul(pV[:, 0:128], xn[:, kt * 128:(kt + 1) * 128],
                                       WBUF[:, VS0 + kt * 512 + kv * 128:
                                            VS0 + kt * 512 + (kv + 1) * 128],
                                       start=(kt == 0), stop=(kt == qi))
                    ins.then_inc(tmm.h, 1)
                T_(fpv)
                tmm.n += 1
                t_pv = tmm.n

                def fpe(e, qi=qi, pV=pV, t_pv=t_pv, g=gget(("attA", qi))):
                    e.wait_ge(tmm.h, t_pv)
                    need(e, g)
                    e.activation(out=attsc[:, 256:384], in_=pV[:, 0:128],
                                 func=AF.Square, accum_out=sstat[:, qi:qi + 1])
                    e.activation(out=snapt[:, qi * 128:(qi + 1) * 128],
                                 in_=pV[:, 0:128], func=AF.Copy).then_inc(sml.h, 1)
                C_(fpe)
                sml.n += 1
                guard[id(pV)] = (sml.h, sml.n)
                s_pe = sml.n

                def fsd(e, qi=qi, nch=nch, s_pe=s_pe):
                    e.wait_ge(sml.h, s_pe)
                    if nch > 1:
                        e.tensor_reduce(out=sstat[:, 16 + qi:17 + qi],
                                        in_=sstat[:, 48:48 + nch],
                                        axis=AX.X, op=ALU.add).then_inc(vml.h, 1)
                    else:
                        e.tensor_copy(out=sstat[:, 16 + qi:17 + qi],
                                      in_=sstat[:, 48:49]).then_inc(vml.h, 1)
                V_(fsd)
                vml.n += 1
                state_d["vsd"] = vml.n

            sA = {}
            for k in range(nqi + 1):
                if k < nqi:
                    sA[k] = stageA(k)
                if k >= 1:
                    stageB(k - 1, sA[k - 1])
            v_head = vml.n
            s_head = sml.n

            if "pv" in KSKIP or "comb" in KSKIP or "head" in KSKIP:
                continue
            def fscA(e, v_head=v_head, s_head=s_head):
                e.wait_ge(vml.h, v_head)
                e.wait_ge(sml.h, s_head)
                e.copy(out=sstat[:, 32:48], in_=sstat[:, 16:32]).then_inc(sml.h, 1)
            C_(fscA)
            sml.n += 1
            sA = sml.n

            def fscB(e, sA=sA):
                e.wait_ge(sml.h, sA)
                e.tensor_scalar(out=sstat[:, 16:32], in0=sstat[:, 32:48],
                                scalar1=c_mcnt[:], scalar2=None,
                                op0=ALU.subtract).then_inc(vml.h, 1)
            V_(fscB)
            vml.n += 1
            vB = vml.n

            def fscC(e, vB=vB):
                e.wait_ge(vml.h, vB)
                e.copy(out=sstat[:, 32:48], in_=sstat[:, 16:32]).then_inc(sml.h, 1)
            C_(fscC)
            sml.n += 1
            sC = sml.n

            def fscD(e, sC=sC):
                e.wait_ge(sml.h, sC)
                e.tensor_tensor(out=sstat[:, 48:64], in0=sstat[:, 32:48],
                                in1=sstat[:, 32:48], op=ALU.mult).then_inc(vml.h, 1)
            V_(fscD)
            vml.n += 1
            vD = vml.n

            def fscE(e, vD=vD):
                e.wait_ge(vml.h, vD)
                e.activation(out=sstat[:, 32:48], in_=sstat[:, 48:64],
                             func=AF.Identity, scale=128.0e-8).then_inc(sml.h, 1)
            C_(fscE)
            sml.n += 1
            sE = sml.n

            def fscF(e, sE=sE):
                e.wait_ge(sml.h, sE)
                e.tensor_tensor(out=sstat[:, 48:64], in0=sstat[:, 32:48],
                                in1=sstat[:, 0:16], op=ALU.add).then_inc(vml.h, 1)
            V_(fscF)
            vml.n += 1
            vF = vml.n

            def fscG(e, vF=vF):
                e.wait_ge(vml.h, vF)
                e.activation(out=sstat[:, 32:48], in_=sstat[:, 48:64],
                             func=AF.Ln).then_inc(sml.h, 1)
            C_(fscG)
            sml.n += 1
            sG = sml.n

            def fscH(e, sG=sG):
                e.wait_ge(sml.h, sG)
                e.tensor_copy(out=sstat[:, 48:64],
                              in_=sstat[:, 32:48]).then_inc(vml.h, 1)
            V_(fscH)
            vml.n += 1
            vH = vml.n

            def fscI(e, vH=vH):
                e.wait_ge(vml.h, vH)
                e.activation(out=sstat[:, 0:16], in_=sstat[:, 48:64], func=AF.Exp,
                             scale=-0.5, bias=c_misc[:, 2:3]).then_inc(sml.h, 1)
            C_(fscI)
            sml.n += 1
            v_sc3 = sml.n
            guard["sstat_e"] = (sml.h, v_sc3)

            for qi in (range(nqi) if "evac" not in KSKIP and "pv" not in KSKIP
                       and "comb" not in KSKIP else []):
                sl = (qi % 2) * 128
                so = 384 + (qi % 2) * 128

                def fas(e, qi=qi, sl=sl, v_sc3=v_sc3, w=state_att.get(qi % 2)):
                    e.wait_ge(sml.h, v_sc3)
                    if w is not None:
                        e.wait_ge(tmm.h, w)
                    e.tensor_scalar(out=attsc[:, sl:sl + 128],
                                    in0=snapt[:, qi * 128:(qi + 1) * 128],
                                    scalar1=sstat[:, 0 + qi:1 + qi], scalar2=None,
                                    op0=ALU.mult).then_inc(vml.h, 1)
                V_(fas)
                vml.n += 1
                v_as = vml.n
                guard[("attA", qi)] = (vml.h, v_as)

                half = state_tr[0] % 2
                state_tr[0] += 1

                def fat(e, sl=sl, half=half, v_as=v_as, g=gget(("pTR", half))):
                    e.wait_ge(vml.h, v_as)
                    need(e, g)
                    e.transpose(pTR[:, half * 1024:half * 1024 + 128],
                                attsc[:, sl:sl + 128], c_id_bf[:]).then_inc(tmm.h, 1)
                T_(fat)
                tmm.n += 1
                t_at = tmm.n
                state_att[qi % 2] = t_at

                def fae(e, so=so, half=half, t_at=t_at, w=state_att2.get(qi % 2)):
                    e.wait_ge(tmm.h, t_at)
                    if w is not None:
                        e.wait_ge(w[0], w[1])
                    e.tensor_copy(out=attsc[:, so:so + 128],
                                  in_=pTR[:, half * 1024:half * 1024 + 128]
                                  ).then_inc(vml.h, 1)
                V_(fae)
                vml.n += 1
                s_ae = vml.n
                guard[("pTR", half)] = (vml.h, s_ae)

                def fad(e, h=h, qi=qi, so=so, s_ae=s_ae):
                    e.wait_ge(vml.h, s_ae)
                    e.dma_start(out=attr[h * 128:(h + 1) * 128, qi * 128:(qi + 1) * 128],
                                in_=attsc[:, so:so + 128]).then_inc(GAT[qi % 2].h, 16)
                G_(fad)
                GAT[qi % 2].n += 16
                state_att2[qi % 2] = (GAT[qi % 2].h, GAT[qi % 2].n)
            guard["qTh"] = (tmm.h, tmm.n)
            rope_state["t"] = tmm.n

    def wo_phase():
        def fwo(e):
            e.dma_start(out=WBUF[:].rearrange("p (t c) -> p t c", c=2048),
                        in_=wo.rearrange("(t p) c -> p t c", p=128)
                        ).then_inc(dma["gwk"].h, 16)
        G_(fwo)
        dma["gwk"].n += 16
        g_wo = dma["gwk"].n

        def fws(e, g_wo=g_wo):
            e.wait_ge(dma["gwk"].h, g_wo)
            for dt in range(ND):
                ins = e.tensor_scalar(out=WBUF[:, dt * DM:(dt + 1) * DM],
                                      in0=WBUF[:, dt * DM:(dt + 1) * DM],
                                      scalar1=c_rmsw[:], scalar2=None, op0=ALU.mult)
            ins.then_inc(vml.h, 1)
        V_(fws)
        vml.n += 1
        v_wo = vml.n

        apply_v = {}
        big_tile_s = [None] * NT
        zt_guard = [None, None]
        for i in range(NT):
            sbase = (i % 2) * 2048

            def fsl(e, i=i, sbase=sbase, w=apply_v.get(i - 2)):
                if w is not None:
                    e.wait_ge(vml.h, w)
                e.dma_start(out=SCR[:, sbase:sbase + 2048].rearrange(
                                "p (t c) -> p t c", c=128),
                            in_=attr[:, i * 128:(i + 1) * 128].rearrange(
                                "(t p) c -> p t c", p=128)).then_inc(SSL[i % 2].h, 16)
            S_(fsl)
            SSL[i % 2].n += 16

            xb = xt[i % 2]

            def fx1(e, i=i, xb=xb, w=apply_v.get(i - 2)):
                if w is not None:
                    e.wait_ge(vml.h, w)
                e.dma_start(out=xb[:], in_=x1r[i * 128:(i + 1) * 128, :]
                            ).then_inc(SX[i % 2].h, 16)
            S_(fx1)
            SX[i % 2].n += 16
            sl_i = [(SSL[i % 2].h, SSL[i % 2].n), (SX[i % 2].h, SX[i % 2].n)]

            for cch in range(4):
                pW = pA if (i * 4 + cch) % 2 == 0 else pB

                def fwm(e, i=i, cch=cch, pW=pW, sbase=sbase, sl_i=sl_i,
                        v_wo=v_wo, g=gget(id(pW))):
                    for hh, nn in sl_i:
                        e.wait_ge(hh, nn)
                    if i == 0 and cch == 0:
                        e.wait_ge(vml.h, v_wo)
                    need(e, g)
                    for dt in range(ND):
                        ins = e.matmul(pW[:], SCR[:, sbase + dt * 128:sbase + (dt + 1) * 128],
                                       WBUF[:, dt * DM + cch * 512:dt * DM + (cch + 1) * 512],
                                       start=(dt == 0), stop=(dt == ND - 1))
                    ins.then_inc(tmm.h, 1)
                T_(fwm)
                tmm.n += 1
                t_w = tmm.n

                def fadd(e, cch=cch, pW=pW, xb=xb, t_w=t_w,
                         zg=(list(zt_guard) if cch == 0 else None)):
                    e.wait_ge(tmm.h, t_w)
                    if zg is not None:
                        for g in zg:
                            need(e, g)
                    e.tensor_tensor(out=zt[:, cch * 512:(cch + 1) * 512],
                                    in0=pW[:], in1=xb[:, cch * 512:(cch + 1) * 512],
                                    op=ALU.add).then_inc(vml.h, 1)
                V_(fadd)
                vml.n += 1
                guard[id(pW)] = (vml.h, vml.n)
            v_z = vml.n
            apply_v[i] = v_z

            def fzs(e, i=i, v_z=v_z):
                e.wait_ge(vml.h, v_z)
                e.dma_start(out=x2r[i * 128:(i + 1) * 128, :], in_=zt[:]
                            ).then_inc(dma["gzs"].h, 16)
            G_(fzs)
            dma["gzs"].n += 16
            g_z = dma["gzs"].n

            s_st = ln_stats(zt, [(vml.h, v_z)], i % 2)
            v_xn = ln_normalize(zt, s_st, i % 2)
            big_tile_s[i] = transpose_to_big(i, v_xn)
            zt_guard = [(vml.h, v_xn), (dma["gzs"].h, g_z)]
        guard["zt"] = zt_guard[0]
        guard["zt_store"] = zt_guard[1]
        xt_free = {-2: apply_v[NT - 2], -1: apply_v[NT - 1]}
        return big_tile_s, xt_free

    big_s, xt_free = initial_ln()
    if dbg:
        d_big1 = nc.dram_tensor("d_big1", [128, ND * SEQ], BF16,
                                kind="ExternalOutput").ap()
        _bs = big_s[NT - 1]

        def fdump1(e, _bs=_bs):
            e.wait_ge(sml.h, _bs)
            e.dma_start(out=d_big1, in_=BIG[:]).then_inc(dma["smc"].h, 16)
        S_(fdump1)
        dma["smc"].n += 16
    if stop >= 2:
        ffn_phase(0, x_in, x1r, fuse_next_ln=True, big_tile_s=big_s, xt_free=xt_free)
        barrier()
    if stop >= 3:
        nh_ = H
        nq_ = NT
        if stop == 3:
            if sub == 17:
                nh_, nq_ = 1, 0
            elif sub == 18:
                nh_, nq_ = 1, 1
            elif sub == 19:
                nh_, nq_ = 1, 4
            elif sub == 20:
                nh_, nq_ = 1, 2
            elif sub == 21:
                nh_, nq_ = 1, 3
            elif sub <= 16:
                nh_, nq_ = sub, NT
        attention_phase(nheads=nh_, nqi=nq_)
        barrier()
    if stop >= 4:
        big_s, xt_free = wo_phase()
        barrier()
    if stop >= 5:
        ffn_phase(1, x2r, y_out, fuse_next_ln=False, big_tile_s=big_s, xt_free=xt_free)

    if dbg:
        ddump = {}
        for nm, src in (("d_sall", sall), ("d_srch", srch), ("d_wcol", wcol),
                        ("d_stat", stat), ("d_snapt", snapt), ("d_sown", sown)):
            dt_ = nc.dram_tensor(nm, list(src.shape), F32, kind="ExternalOutput").ap()
            ddump[nm] = (dt_, src)
        d_big = nc.dram_tensor("d_big", [128, ND * SEQ], BF16, kind="ExternalOutput").ap()
        d_ccin = nc.dram_tensor("d_ccin", [16, SEQ], F32, kind="ExternalOutput").ap()
        d_ccout = nc.dram_tensor("d_ccout", [128, SEQ], F32, kind="ExternalOutput").ap()
        t9, v9, s9 = tmm.n, vml.n, sml.n
        dsn = [(d.h, d.n) for d in dma.values() if d.n > 0]

        def fdump(e, t9=t9, v9=v9, s9=s9, dsn=list(dsn)):
            e.wait_ge(tmm.h, t9)
            e.wait_ge(vml.h, v9)
            e.wait_ge(sml.h, s9)
            for h, n in dsn:
                e.wait_ge(h, n)
            for nm, (dt_, src) in ddump.items():
                if nm == "d_srch":
                    e.dma_start(out=dt_[:, 0:12], in_=src[:, 0:12]).then_inc(dma["smc"].h, 16)
                elif nm == "d_stat":
                    e.dma_start(out=dt_[:, 0:8], in_=src[:, 0:8]).then_inc(dma["smc"].h, 16)
                else:
                    e.dma_start(out=dt_, in_=src[:, :]).then_inc(dma["smc"].h, 16)
            e.dma_start(out=d_big, in_=BIG[:]).then_inc(dma["smc"].h, 16)
            e.dma_start(out=d_ccin, in_=cc_in[0].ap()).then_inc(dma["smc"].h, 16)
            e.dma_start(out=d_ccout, in_=cc_out[0].ap()).then_inc(dma["smc"].h, 16)
        S_(fdump)
        dma["smc"].n += 16 * 9

    dsnap_fin = [(d.h, d.n) for d in dma.values() if d.n > 0]

    def ffin(e, dsnap=list(dsnap_fin)):
        for h, n in dsnap:
            e.wait_ge(h, n)
    G_(ffin)

    with nc.Block() as blk:
        @blk.sync
        def _(e):
            for f in prog["sync"]:
                f(e)

        @blk.gpsimd
        def _(e):
            for f in prog["gpsimd"]:
                f(e)

        @blk.tensor
        def _(e):
            for f in prog["tensor"]:
                f(e)

        @blk.vector
        def _(e):
            for f in prog["vector"]:
                f(e)

        @blk.scalar
        def _(e):
            for f in prog["scalar"]:
                f(e)

    ctx.close()
    return nc


_CONSTS = make_consts()


def kernel(L0_attn_x, L0_attn_Wq, L0_attn_Wk, L0_attn_Wv, L0_attn_Wo,
           L0_attn_lq1, L0_attn_lq2, L0_attn_lk1, L0_attn_lk2, L0_attn_rms_w,
           L0_ffn_router_w, L0_ffn_router_b, L0_ffn_expert_w, L0_ffn_expert_b):
    nc = build_program()
    x = np.ascontiguousarray(np.asarray(L0_attn_x, dtype=np.float32))
    base = dict(
        wq=np.ascontiguousarray(np.asarray(L0_attn_Wq, np.float32)),
        wk=np.ascontiguousarray(np.asarray(L0_attn_Wk, np.float32)),
        wv=np.ascontiguousarray(np.asarray(L0_attn_Wv, np.float32)),
        wo=np.ascontiguousarray(np.asarray(L0_attn_Wo, np.float32)),
        lq1=np.ascontiguousarray(np.asarray(L0_attn_lq1, np.float32)),
        lq2=np.ascontiguousarray(np.asarray(L0_attn_lq2, np.float32)),
        lk1=np.ascontiguousarray(np.asarray(L0_attn_lk1, np.float32)),
        lk2=np.ascontiguousarray(np.asarray(L0_attn_lk2, np.float32)),
        rmsw=np.ascontiguousarray(np.asarray(L0_attn_rms_w, np.float32).reshape(128, 1)),
        rw=np.ascontiguousarray(np.asarray(L0_ffn_router_w, np.float32)),
        rb=np.ascontiguousarray(np.asarray(L0_ffn_router_b, np.float32).reshape(1, 16)),
        ew=np.ascontiguousarray(np.asarray(L0_ffn_expert_w, np.float32)),
        eb=np.ascontiguousarray(np.asarray(L0_ffn_expert_b, np.float32).reshape(1, DM)),
        **_CONSTS,
    )
    pidx = np.arange(128)
    in_maps = []
    for r in range(NCORES):
        m = dict(base)
        m["x_in"] = np.ascontiguousarray(x[r])
        m["selr"] = (pidx // 16 == r).astype(np.float32).reshape(128, 1)
        in_maps.append(m)
    res = run_bass_kernel_spmd(nc, in_maps, core_ids=list(range(NCORES)))
    return np.stack([res.results[r]["y_out"] for r in range(NCORES)], axis=0)


# version 05
# speedup vs baseline: 1.3834x; 1.3685x over previous
import math
import numpy as np
from contextlib import ExitStack

import concourse.bass as bass
import concourse.mybir as mybir
from concourse.bass_utils import run_bass_kernel_spmd

F32 = mybir.dt.float32
BF16 = mybir.dt.bfloat16
U32 = mybir.dt.uint32
AF = mybir.ActivationFunctionType
ALU = mybir.AluOpType
AX = mybir.AxisListType

SEQ = 2048
DM = 2048
NT = 16
ND = 16
H = 16
KVH = 4
NEXP = 16
TOPK = 1024
NCORES = 8
CEXP = 1.0 / math.sqrt(128.0)
LAM_INIT = 0.2
NEG_BIG = -1.0e30
BIGV = 1.0e6
SEARCH_ITERS = 34
RMS_SC = 0.8 * math.sqrt(128.0)


def make_consts():
    c = {}
    c["ident"] = np.eye(128, dtype=np.float32)
    j = np.arange(0, 256, 2, dtype=np.float32)
    theta = (np.float32(1.0) / (np.float32(10000.0) ** (j / np.float32(256.0)))).astype(np.float32)
    pos = np.arange(SEQ, dtype=np.float32)
    ang = pos[:, None] * theta[None, :]
    cosv = np.cos(ang).astype(np.float32)
    sinv = np.sin(ang).astype(np.float32)
    CF = np.empty((SEQ, 256), dtype=np.float32)
    SF = np.empty((SEQ, 256), dtype=np.float32)
    CF[:, 0::2] = cosv
    CF[:, 1::2] = cosv
    SF[:, 0::2] = -sinv
    SF[:, 1::2] = sinv
    c["ropec"] = CF
    c["ropes"] = SF
    p = np.arange(128)[:, None]
    cc = np.arange(128)[None, :]
    c["emask"] = np.where(cc <= p, 0.0, NEG_BIG).astype(np.float32)
    kk = np.arange(128)[:, None]
    qq = np.arange(128)[None, :]
    c["maskt"] = (qq >= kk).astype(np.float32)
    c["maskcnt"] = (127.0 - np.arange(128, dtype=np.float32)).reshape(128, 1)
    pidx = np.arange(128)
    c["mgrp"] = (pidx[:, None] % 16 == pidx[None, :] % 16).astype(np.float32)
    c["repl16"] = (np.arange(16)[:, None] == pidx[None, :] % 16).astype(np.float32)
    c["ones_row"] = np.ones((1, 128), dtype=np.float32)
    mc = np.zeros((128, 4), dtype=np.float32)
    mc[:, 0] = 1e-5
    mc[:, 1] = LAM_INIT
    mc[:, 2] = math.log(0.8 * math.sqrt(128.0))
    mc[:, 3] = 2.0
    c["cmisc"] = mc
    return c


class Ctr:
    def __init__(self, ctx, nc, name):
        self.h = ctx.enter_context(nc.semaphore(name))
        self.n = 0


def build_program(dbg=False, stop=5, sub=3):
    import os as _os0
    KSKIP = set((_os0.environ.get("KSKIP") or "").split(","))
    import os as _os
    nc = bass.Bass("TRN2", target_bir_lowering=False,
                   detect_race_conditions=bool(_os.environ.get("KSTRICT")))
    ctx = ExitStack()

    dri = lambda n, sh: nc.dram_tensor(n, sh, F32, kind="ExternalInput").ap()
    x_in = dri("x_in", [SEQ, DM])
    wq = dri("wq", [DM, 4096])
    wk = dri("wk", [DM, 1024])
    wv = dri("wv", [DM, 512])
    wo = dri("wo", [DM, DM])
    lq1 = dri("lq1", [16, 128])
    lq2 = dri("lq2", [16, 128])
    lk1 = dri("lk1", [16, 128])
    lk2 = dri("lk2", [16, 128])
    rmsw = dri("rmsw", [128, 1])
    rw = dri("rw", [DM, 16])
    rb = dri("rb", [1, 16])
    ew = dri("ew", [DM, DM])
    eb = dri("eb", [1, DM])
    ident = dri("ident", [128, 128])
    ropec = dri("ropec", [SEQ, 256])
    ropes = dri("ropes", [SEQ, 256])
    emask = dri("emask", [128, 128])
    maskt = dri("maskt", [128, 128])
    maskcnt = dri("maskcnt", [128, 1])
    mgrp = dri("mgrp", [128, 128])
    repl16 = dri("repl16", [16, 128])
    ones_row = dri("ones_row", [1, 128])
    selr = dri("selr", [128, 1])
    cmisc = dri("cmisc", [128, 4])

    y_out = nc.dram_tensor("y_out", [SEQ, DM], F32, kind="ExternalOutput").ap()
    ik = dict(kind="ExternalOutput") if dbg else {}
    x1r = nc.dram_tensor("x1r", [SEQ, DM], F32, **ik).ap()
    x2r = nc.dram_tensor("x2r", [SEQ, DM], F32, **ik).ap()
    attr = nc.dram_tensor("attr", [DM, SEQ], BF16, **ik).ap()
    cc_in = [nc.dram_tensor(f"cc_in{k}", [16, SEQ], F32) for k in range(2)]
    cc_out = [nc.dram_tensor(f"cc_out{k}", [128, SEQ], F32) for k in range(2)]

    sbt = lambda n, sh, dt: ctx.enter_context(nc.sbuf_tensor(n, sh, dt))
    pst = lambda n, sh, dt: ctx.enter_context(nc.psum_tensor(n, sh, dt))

    BIG = sbt("BIG", [128, ND * SEQ], BF16)
    WBUF = sbt("WBUF", [128, ND * DM], BF16)
    KT0 = 0
    WSLOT = 16384
    VS0 = 24576
    SCR = sbt("SCR", [128, 8192], BF16)
    qTh = sbt("qTh", [128, 2 * SEQ], BF16)
    xt = [sbt(f"xt{i}", [128, DM], F32) for i in range(2)]
    e0b = xt[0][:, :].bitcast(BF16)[:, 0:SEQ]
    e1b = xt[0][:, :].bitcast(BF16)[:, SEQ:2 * SEQ]
    e0c = xt[1][:, :].bitcast(BF16)[:, 0:SEQ]
    e1c = xt[1][:, :].bitcast(BF16)[:, SEQ:2 * SEQ]
    zt = sbt("zt", [128, DM], F32)
    e0d = zt[:, :].bitcast(BF16)[:, 0:SEQ]
    e1d = zt[:, :].bitcast(BF16)[:, SEQ:2 * SEQ]
    xn = sbt("xn", [128, DM], BF16)
    e0 = sbt("e0", [128, SEQ], BF16)
    e1 = sbt("e1", [128, SEQ], BF16)
    sall = qTh[:, :].bitcast(F32)
    snapt = sbt("snapt", [128, SEQ], F32)
    qrw = sbt("qrw", [128, 1024], BF16)
    qrt = sbt("qrt", [128, 1024], BF16)
    uneg = qrw
    dchk = qrt
    attsc = sbt("attsc", [128, 640], BF16)
    qro = attsc[:, 0:512]
    ST = snapt[0:16, :]
    lqk = [snapt[0:16, i * 128:(i + 1) * 128] for i in range(4)]
    sown = sbt("sown", [128, 64], F32)
    stat = sbt("stat", [128, 64], F32)
    srch = sbt("srch", [128, 32], F32)
    srchu = sbt("srchu", [128, 1], U32)
    srow = sbt("srow", [1, 16], F32)
    wcol = sbt("wcol", [128, 16], F32)
    sstat = sbt("sstat", [128, 64], F32)
    lamc = sbt("lamc", [128, 16], F32)
    lsc = sbt("lsc", [16, 8], F32)
    c_id_bf = sbt("c_id_bf", [128, 128], BF16)
    c_id_f = sbt("c_id_f", [128, 128], F32)
    c_emask = sbt("c_emask", [128, 128], BF16)
    c_maskt = sbt("c_maskt", [128, 128], BF16)
    c_mcnt = sbt("c_mcnt", [128, 1], F32)
    c_mgrp = sbt("c_mgrp", [128, 128], F32)
    c_repl = sbt("c_repl", [16, 128], F32)
    c_ones = sbt("c_ones", [1, 128], F32)
    c_selr = sbt("c_selr", [128, 1], F32)
    c_rb = sbt("c_rb", [1, 16], F32)
    c_ebbf = e1[0:1, 0:DM]
    jnk_bf = snapt[:, 0:1024].bitcast(BF16)
    jnk_bf2 = snapt[:, 1024:2048].bitcast(BF16)
    c_rmsw = sbt("c_rmsw", [128, 1], F32)
    c_misc = sbt("c_misc", [128, 4], F32)
    rwsb = sbt("rwsb", [128, ND * 16], BF16)
    c_onesb = c_maskt[0:1, :]
    c_rope_c = SCR[:, 0:NT * 256]
    c_rope_s = SCR[:, NT * 256:2 * NT * 256]

    pE0 = pst("pE0", [128, 1024], F32)
    pE1 = pst("pE1", [128, 1024], F32)
    pA = pst("pA", [128, 512], F32)
    pB = pst("pB", [128, 512], F32)
    pTR = pst("pTR", [128, 2048], BF16)

    pS_cnt = pE0[:, 0:1]
    pS_thx = pE0[:, 1:2]
    pS_lam = pE0[:, 2:18]
    pS_w = pE0[:, 18:34]
    pS_st = [pE0[0:16, 34:162], pE0[0:16, 162:290]]
    pS_r0 = pE0[0:1, 290:418]
    pS_r16 = pE0[0:16, 418:419]
    pS_lr = pE0[0:1, 420:436]

    dma = {n: Ctr(ctx, nc, n) for n in (
        "smc", "sx0", "sx1", "ssl0", "ssl1", "gmc", "geb", "gst", "gsall",
        "grope", "gwk", "gwv", "gwq0", "gwq1", "gzs", "gat0", "gat1")}
    SX = [dma["sx0"], dma["sx1"]]
    SSL = [dma["ssl0"], dma["ssl1"]]
    GWQ = [dma["gwq0"], dma["gwq1"]]
    GAT = [dma["gat0"], dma["gat1"]]
    ccs = Ctr(ctx, nc, "ccs")
    tmm = Ctr(ctx, nc, "tmm")
    pml = Ctr(ctx, nc, "pml")
    vml = Ctr(ctx, nc, "vml")
    sml = Ctr(ctx, nc, "sml")

    prog = {k: [] for k in ("sync", "gpsimd", "tensor", "vector", "scalar")}
    S_ = prog["sync"].append
    G_ = prog["gpsimd"].append
    T_ = prog["tensor"].append
    V_ = prog["vector"].append
    C_ = prog["scalar"].append

    guard = {}

    def need(e, g):
        if g is None:
            return
        if isinstance(g, list):
            for gg in g:
                need(e, gg)
        else:
            e.wait_ge(g[0], g[1])

    def gget(key):
        return guard.get(key)

    state_tr = [0]

    def _c0(e):
        e.dma_start(out=c_id_f[:], in_=ident).then_inc(dma["smc"].h, 16)
        e.dma_start(out=c_mcnt[:], in_=maskcnt).then_inc(dma["smc"].h, 16)
        e.dma_start(out=c_mgrp[:], in_=mgrp).then_inc(dma["smc"].h, 16)
        e.dma_start(out=c_repl[:], in_=repl16).then_inc(dma["smc"].h, 16)
        e.dma_start(out=c_ones[:], in_=ones_row).then_inc(dma["smc"].h, 16)
        e.dma_start(out=c_rb[:], in_=rb).then_inc(dma["smc"].h, 16)
        e.dma_start(out=c_selr[:], in_=selr).then_inc(dma["smc"].h, 16)
        e.dma_start(out=c_rmsw[:], in_=rmsw).then_inc(dma["smc"].h, 16)
        e.dma_start(out=c_misc[:], in_=cmisc).then_inc(dma["smc"].h, 16)
    S_(_c0)
    dma["smc"].n += 9 * 16
    SL_CONST = dma["smc"].n

    def _c1(e):
        e.dma_start(out=c_id_bf[:], in_=ident).then_inc(dma["gmc"].h, 16)
        e.dma_start(out=c_emask[:], in_=emask).then_inc(dma["gmc"].h, 16)
        e.dma_start(out=c_maskt[:], in_=maskt).then_inc(dma["gmc"].h, 16)
        e.dma_start(out=rwsb[:].rearrange("p (t c) -> p t c", c=16),
                        in_=rw.rearrange("(t p) c -> p t c", p=128)).then_inc(dma["gmc"].h, 16)
        e.dma_start(out=WBUF[:].rearrange("p (t c) -> p t c", c=2048),
                        in_=ew.rearrange("(t p) c -> p t c", p=128)).then_inc(dma["gmc"].h, 16)
    G_(_c1)
    dma["gmc"].n += 5 * 16
    GL_CONST = dma["gmc"].n

    def ln_stats(xtile, waits, par):
        b = par * 16

        def f(e, xtile=xtile, waits=list(waits) + [gget(("stats", par))], b=b):
            for g in waits:
                need(e, g)
            e.activation(out=jnk_bf, in_=xtile[:], func=AF.Identity,
                         accum_out=stat[:, b + 0:b + 1])
            e.activation(out=jnk_bf2, in_=xtile[:], func=AF.Square,
                         accum_out=stat[:, b + 1:b + 2])
            e.copy(out=stat[:, b + 10:b + 11],
                   in_=stat[:, b + 1:b + 2]).then_inc(sml.h, 1)
        C_(f)
        sml.n += 1
        return sml.n

    def ln_normalize(xtile, s_stats, par):
        b = par * 16

        def fA(e, s_stats=s_stats, b=b, g=gget(("statA", par))):
            e.wait_ge(sml.h, s_stats)
            need(e, g)
            e.activation(out=stat[:, b + 2:b + 3], in_=stat[:, b + 0:b + 1],
                         func=AF.Identity, scale=1.0 / DM)
            e.activation(out=stat[:, b + 3:b + 4], in_=stat[:, b + 1:b + 2],
                         func=AF.Identity, scale=1.0 / DM).then_inc(sml.h, 1)
        C_(fA)
        sml.n += 1
        s_A = sml.n

        def fB(e, s_A=s_A, b=b, g=gget(("statB", par))):
            e.wait_ge(sml.h, s_A)
            need(e, g)
            e.tensor_scalar(out=stat[:, b + 4:b + 5], in0=stat[:, b + 2:b + 3],
                            scalar1=stat[:, b + 2:b + 3],
                            scalar2=stat[:, b + 3:b + 4],
                            op0=ALU.mult, op1=ALU.subtract).then_inc(vml.h, 1)
        V_(fB)
        vml.n += 1
        v_B = vml.n
        guard[("stats", par)] = (vml.h, v_B)

        def fC(e, v_B=v_B, b=b):
            e.wait_ge(vml.h, v_B)
            e.activation(out=stat[:, b + 5:b + 6], in_=stat[:, b + 4:b + 5],
                         func=AF.Ln, scale=-1.0,
                         bias=c_misc[:, 0:1]).then_inc(sml.h, 1)
        C_(fC)
        sml.n += 1
        s_C = sml.n

        def fD(e, s_C=s_C, b=b):
            e.wait_ge(sml.h, s_C)
            e.tensor_copy(out=stat[:, b + 7:b + 8],
                          in_=stat[:, b + 5:b + 6]).then_inc(vml.h, 1)
        V_(fD)
        vml.n += 1
        v_D = vml.n

        def fE(e, v_D=v_D, b=b):
            e.wait_ge(vml.h, v_D)
            e.activation(out=stat[:, b + 6:b + 7], in_=stat[:, b + 7:b + 8],
                         func=AF.Exp, scale=-0.5).then_inc(sml.h, 1)
        C_(fE)
        sml.n += 1
        s_E = sml.n
        guard[("statB", par)] = (sml.h, s_E)

        def fF(e, xtile=xtile, s_E=s_E, b=b, g=gget("xn")):
            e.wait_ge(sml.h, s_E)
            need(e, g)
            e.tensor_scalar(out=xn[:], in0=xtile[:], scalar1=stat[:, b + 2:b + 3],
                            scalar2=stat[:, b + 6:b + 7], op0=ALU.subtract,
                            op1=ALU.mult).then_inc(vml.h, 1)
        V_(fF)
        vml.n += 1
        guard[("statA", par)] = (vml.h, vml.n)
        return vml.n

    def transpose_to_big(i, v_xn):
        for bq in range(4):
            half = state_tr[0] % 2
            state_tr[0] += 1

            def ft(e, bq=bq, half=half, v_xn=v_xn, g=gget(("pTR", half))):
                e.wait_ge(vml.h, v_xn)
                need(e, g)
                for j in range(4):
                    dt = bq * 4 + j
                    ins = e.transpose(pTR[:, half * 1024 + j * 128:half * 1024 + (j + 1) * 128],
                                      xn[:, dt * 128:(dt + 1) * 128], c_id_bf[:])
                ins.then_inc(tmm.h, 1)
            T_(ft)
            tmm.n += 1
            t_tr = tmm.n

            def fe(e, bq=bq, half=half, t_tr=t_tr, i=i):
                e.wait_ge(tmm.h, t_tr)
                out_ap = BIG[:, :].rearrange("p (t q) -> p t q", q=SEQ)[
                    :, bq * 4:(bq + 1) * 4, i * 128:(i + 1) * 128]
                e.activation(out=out_ap, in_=pTR[:, half * 1024:half * 1024 + 512],
                             func=AF.Copy).then_inc(sml.h, 1)
            C_(fe)
            sml.n += 1
            guard[("pTR", half)] = (sml.h, sml.n)
        guard["xn"] = (tmm.h, tmm.n)
        return sml.n

    def barrier():
        t1, v1, s1, p1 = tmm.n, vml.n, sml.n, pml.n
        dsnap = [(d.h, d.n) for d in dma.values() if d.n > 0]

        def wt(e, t1=t1, v1=v1, s1=s1, p1=p1, dsnap=list(dsnap)):
            e.wait_ge(tmm.h, t1)
            e.wait_ge(vml.h, v1)
            e.wait_ge(sml.h, s1)
            if p1 > 0:
                e.wait_ge(pml.h, p1)
            for h, n in dsnap:
                e.wait_ge(h, n)
        for EM in (T_, V_, C_, G_, S_):
            EM(wt)
        guard.clear()

    def ffn_phase(call, src_res, dst_res, fuse_next_ln, big_tile_s, xt_free):
        cin, cout = cc_in[call], cc_out[call]

        def feb(e, call=call):
            e.dma_start(out=c_ebbf, in_=eb).then_inc(dma["geb"].h, 16)
            if call == 1:
                e.dma_start(out=WBUF[:].rearrange("p (t c) -> p t c", c=2048),
                            in_=ew.rearrange("(t p) c -> p t c", p=128)
                            ).then_inc(dma["geb"].h, 16)
        G_(feb)
        dma["geb"].n += 16 * (1 + (1 if call == 1 else 0))
        g_eb = dma["geb"].n

        st_prev = {}
        for i in range(NT):
            pR = pA if i % 2 == 0 else pB
            par = i % 2
            sb = par * 16

            def fr(e, i=i, pR=pR, bs=big_tile_s[i], g=gget(id(pR))):
                e.wait_ge(sml.h, bs)
                if i == 0:
                    e.wait_ge(dma["gmc"].h, GL_CONST)
                    e.wait_ge(dma["smc"].h, SL_CONST)
                need(e, g)
                for dt in range(ND):
                    e.matmul(pR[:, 0:16],
                             BIG[:, dt * SEQ + i * 128:dt * SEQ + (i + 1) * 128],
                             rwsb[:, dt * 16:(dt + 1) * 16],
                             start=(dt == 0), stop=False)
                e.matmul(pR[:, 0:16], c_ones[:], c_rb[:], start=False,
                         stop=True).then_inc(tmm.h, 1)
            T_(fr)
            tmm.n += 1
            t_lg = tmm.n

            def fx(e, pR=pR, t_lg=t_lg, sb=sb, par=par,
                   g=[gget(("rtacc", par))]):
                e.wait_ge(tmm.h, t_lg)
                need(e, g)
                e.activation(out=sown[:, sb:sb + 16], in_=pR[:, 0:16], func=AF.Exp,
                             accum_out=stat[:, 32 + par:33 + par])
                e.copy(out=stat[:, 40 + par:41 + par],
                       in_=stat[:, 32 + par:33 + par]).then_inc(sml.h, 1)
            C_(fx)
            sml.n += 1
            s_exp = sml.n
            guard[id(pR)] = (sml.h, sml.n)

            def fn(e, s_exp=s_exp, par=par):
                e.wait_ge(sml.h, s_exp)
                e.reciprocal(out=stat[:, 34 + par:35 + par],
                             in_=stat[:, 32 + par:33 + par]).then_inc(vml.h, 1)
            V_(fn)
            vml.n += 1
            v_sm = vml.n
            guard[("rtacc", par)] = (vml.h, v_sm)

            def fnm(e, v_sm=v_sm, sb=sb, par=par, g=gget(("sownn", par))):
                e.wait_ge(vml.h, v_sm)
                need(e, g)
                e.activation(out=sown[:, 32 + sb:32 + sb + 16],
                             in_=sown[:, sb:sb + 16], func=AF.Identity,
                             scale=stat[:, 34 + par:35 + par]).then_inc(sml.h, 1)
            C_(fnm)
            sml.n += 1
            s_nm = sml.n

            def fst(e, par=par, sb=sb, s_nm=s_nm, g=gget("bankS")):
                e.wait_ge(sml.h, s_nm)
                need(e, g)
                e.transpose(pS_st[par], sown[:, 32 + sb:32 + sb + 16],
                            c_id_f[:]).then_inc(tmm.h, 1)
            T_(fst)
            tmm.n += 1
            t_st = tmm.n

            def fse(e, i=i, par=par, t_st=t_st):
                e.wait_ge(tmm.h, t_st)
                e.activation(out=ST[0:16, i * 128:(i + 1) * 128], in_=pS_st[par],
                             func=AF.Copy).then_inc(sml.h, 1)
            C_(fse)
            sml.n += 1
            guard[("sownn", par)] = (tmm.h, t_st)
            guard["bankS"] = (sml.h, sml.n)
        st_all = sml.n

        def fcc(e, st_all=st_all, cin=cin, cout=cout, gs0=dma["gst"].n, cc0=ccs.n):
            e.wait_ge(sml.h, st_all)
            e.dma_start(out=cin[:, :], in_=ST[0:16, :]).then_inc(dma["gst"].h, 16)
            e.wait_ge(dma["gst"].h, gs0 + 16)
            e.collective_compute(
                "AllGather", ALU.bypass,
                replica_groups=[list(range(NCORES))],
                ins=[cin.ap()], outs=[cout.ap()],
            ).then_inc(ccs.h, 1)
            e.wait_ge(ccs.h, cc0 + 1)
            e.dma_start(out=sall[:], in_=cout[:, :]).then_inc(dma["gsall"].h, 16)
        G_(fcc)
        dma["gst"].n += 16
        ccs.n += 1
        dma["gsall"].n += 16
        g_sall = dma["gsall"].n

        def fs0(e, g_sall=g_sall):
            e.wait_ge(dma["gsall"].h, g_sall)
            e.memset(srch[:, :], 0.0)
            e.memset(srch[:, 2:3], 1.0)
            e.memset(srch[:, 4:5], 0.5).then_inc(vml.h, 1)
        V_(fs0)
        vml.n += 1

        for it in range(SEARCH_ITERS):
            par = it % 2
            nxt = 1 - par
            v_mid = vml.n

            def fmb(e, v_mid=v_mid, par=par):
                e.wait_ge(vml.h, v_mid)
                e.copy(out=srch[:, 12:13], in_=srch[:, 4 + par:5 + par]).then_inc(sml.h, 1)
            C_(fmb)
            sml.n += 1
            s_mid = sml.n

            def fsa(e, s_mid=s_mid):
                e.wait_ge(sml.h, s_mid)
                e.tensor_scalar(out=e0[:], in0=sall[:], scalar1=srch[:, 12:13],
                                scalar2=0.0, op0=ALU.is_ge, op1=ALU.add,
                                accum_out=srch[:, 13:14])
                e.tensor_copy(out=srch[:, 14:15], in_=srch[:, 13:14]).then_inc(vml.h, 1)
            V_(fsa)
            vml.n += 1
            v_cnt = vml.n

            def fsb(e, v_cnt=v_cnt, g=gget("bankS")):
                e.wait_ge(vml.h, v_cnt)
                need(e, g)
                e.matmul(pS_cnt, c_mgrp[:], srch[:, 13:14], start=True,
                         stop=True).then_inc(tmm.h, 1)
            T_(fsb)
            tmm.n += 1
            t_cnt = tmm.n

            def fsA(e, t_cnt=t_cnt, par=par):
                e.wait_ge(tmm.h, t_cnt)
                e.tensor_scalar(out=srch[:, 8:9], in0=pS_cnt, scalar1=float(TOPK),
                                scalar2=None, op0=ALU.is_ge)
                e.tensor_tensor(out=srch[:, 6:7], in0=srch[:, 4 + par:5 + par],
                                in1=srch[:, 0 + par:1 + par], op=ALU.subtract)
                e.tensor_tensor(out=srch[:, 7:8], in0=srch[:, 2 + par:3 + par],
                                in1=srch[:, 4 + par:5 + par],
                                op=ALU.subtract).then_inc(vml.h, 1)
            V_(fsA)
            vml.n += 1
            v_A = vml.n
            guard["bankS"] = (vml.h, v_A)

            def fsB(e, v_A=v_A):
                e.wait_ge(vml.h, v_A)
                e.copy(out=srch[:, 9:10], in_=srch[:, 8:9])
                e.copy(out=srch[:, 10:11], in_=srch[:, 6:7])
                e.copy(out=srch[:, 11:12], in_=srch[:, 7:8]).then_inc(sml.h, 1)
            C_(fsB)
            sml.n += 1
            s_B = sml.n

            def fsC(e, s_B=s_B, par=par, nxt=nxt):
                e.wait_ge(sml.h, s_B)
                e.scalar_tensor_tensor(out=srch[:, 0 + nxt:1 + nxt],
                                       in0=srch[:, 10:11], scalar=srch[:, 9:10],
                                       in1=srch[:, 0 + par:1 + par],
                                       op0=ALU.mult, op1=ALU.add)
                e.scalar_tensor_tensor(out=srch[:, 2 + nxt:3 + nxt],
                                       in0=srch[:, 11:12], scalar=srch[:, 9:10],
                                       in1=srch[:, 4 + par:5 + par],
                                       op0=ALU.mult, op1=ALU.add).then_inc(vml.h, 1)
            V_(fsC)
            vml.n += 1
            v_C = vml.n

            def fsD(e, v_C=v_C, nxt=nxt):
                e.wait_ge(vml.h, v_C)
                e.activation(out=srch[:, 10:11], in_=srch[:, 0 + nxt:1 + nxt],
                             func=AF.Identity, scale=0.5)
                e.activation(out=srch[:, 11:12], in_=srch[:, 2 + nxt:3 + nxt],
                             func=AF.Identity, scale=0.5).then_inc(sml.h, 1)
            C_(fsD)
            sml.n += 1
            s_D = sml.n

            def fsE(e, s_D=s_D, nxt=nxt):
                e.wait_ge(sml.h, s_D)
                e.tensor_tensor(out=srch[:, 4 + nxt:5 + nxt], in0=srch[:, 10:11],
                                in1=srch[:, 11:12], op=ALU.add).then_inc(vml.h, 1)
            V_(fsE)
            vml.n += 1
        LOF = SEARCH_ITERS % 2

        v_lo = vml.n

        def flb(e, v_lo=v_lo, LOF=LOF):
            e.wait_ge(vml.h, v_lo)
            e.copy(out=srch[:, 15:16], in_=srch[:, LOF:LOF + 1]).then_inc(sml.h, 1)
        C_(flb)
        sml.n += 1
        s_lo = sml.n

        def fsn1(e, s_lo=s_lo):
            e.wait_ge(sml.h, s_lo)
            e.scalar_tensor_tensor(out=e0[:], in0=sall[:], scalar=srch[:, 15:16],
                                   in1=sall[:], op0=ALU.is_ge,
                                   op1=ALU.bypass).then_inc(vml.h, 1)
        V_(fsn1)
        vml.n += 1
        v_n1 = vml.n

        def fsn2(e, v_n1=v_n1):
            e.wait_ge(vml.h, v_n1)
            e.activation(out=xn[:], in_=e0[:], func=AF.Identity, scale=-2.0,
                         bias=c_misc[:, 3:4]).then_inc(sml.h, 1)
        C_(fsn2)
        sml.n += 1
        s_n2 = sml.n

        def fsn3(e, s_n2=s_n2):
            e.wait_ge(sml.h, s_n2)
            e.tensor_tensor(out=snapt[:], in0=sall[:], in1=xn[:], op=ALU.add)
            e.tensor_reduce(out=srch[:, 6:7], in_=snapt[:], axis=AX.X,
                            op=ALU.min).then_inc(vml.h, 1)
        V_(fsn3)
        vml.n += 1
        v_cm = vml.n

        def fst2(e, v_cm=v_cm, g=gget("bankS")):
            e.wait_ge(vml.h, v_cm)
            need(e, g)
            e.transpose(pS_r0, srch[:, 6:7], c_id_f[:]).then_inc(tmm.h, 1)
        T_(fst2)
        tmm.n += 1
        t_cm = tmm.n

        def fmx(e, t_cm=t_cm):
            e.wait_ge(tmm.h, t_cm)
            e.tensor_reduce(out=srow[:], in_=pS_r0.rearrange("p (r e) -> p e r", r=8),
                            axis=AX.X, op=ALU.min).then_inc(vml.h, 1)
        V_(fmx)
        vml.n += 1
        guard["bankS"] = (vml.h, vml.n)
        v_th = vml.n

        def fth(e, v_th=v_th, g=gget("bankS")):
            e.wait_ge(vml.h, v_th)
            need(e, g)
            e.transpose(pS_r16, srow[:], c_id_f[0:1, 0:1]).then_inc(tmm.h, 1)
        T_(fth)
        tmm.n += 1
        t_th = tmm.n

        def fthc(e, t_th=t_th):
            e.wait_ge(tmm.h, t_th)
            e.tensor_copy(out=srch[0:16, 7:8], in_=pS_r16).then_inc(vml.h, 1)
        V_(fthc)
        vml.n += 1
        guard["bankS"] = (vml.h, vml.n)
        v_thc = vml.n

        def fthx(e, v_thc=v_thc, g=gget("bankS")):
            e.wait_ge(vml.h, v_thc)
            need(e, g)
            e.matmul(pS_thx, c_repl[:], srch[0:16, 7:8], start=True,
                     stop=True).then_inc(tmm.h, 1)
        T_(fthx)
        tmm.n += 1
        t_thx = tmm.n

        def fw0(e, t_thx=t_thx):
            e.wait_ge(tmm.h, t_thx)
            e.activation(out=srch[:, 8:9], in_=pS_thx, func=AF.Copy).then_inc(sml.h, 1)
        C_(fw0)
        sml.n += 1
        s_th = sml.n
        guard["bankS"] = (sml.h, sml.n)

        def fw1(e, s_th=s_th):
            e.wait_ge(sml.h, s_th)
            e.scalar_tensor_tensor(out=snapt[:], in0=sall[:], scalar=srch[:, 8:9],
                                   in1=sall[:], op0=ALU.is_ge,
                                   op1=ALU.mult).then_inc(vml.h, 1)
        V_(fw1)
        vml.n += 1
        v_msk = vml.n

        def fw2(e, v_msk=v_msk, g=gget("bankS")):
            e.wait_ge(vml.h, v_msk)
            need(e, g)
            for i in range(NT):
                ins = e.matmul(pS_w[:, i:i + 1], snapt[:, i * 128:(i + 1) * 128],
                               c_selr[:], start=True, stop=True)
            ins.then_inc(tmm.h, 1)
        T_(fw2)
        tmm.n += 1
        t_w = tmm.n

        def fw3(e, t_w=t_w):
            e.wait_ge(tmm.h, t_w)
            e.activation(out=wcol[:], in_=pS_w, func=AF.Identity,
                         scale=0.5).then_inc(sml.h, 1)
        C_(fw3)
        sml.n += 1
        guard["bankS"] = (sml.h, sml.n)
        v_w = None
        s_w = sml.n

        y_evac = {}
        apply_v = dict(xt_free)
        store_g = {}
        big_next = [None] * NT
        zt_guard = [gget("zt"), gget("zt_store")]

        def emit_y(i):
            slot = (i % 4) * 2048
            for cch in range(4):
                pY = pA if (i * 4 + cch) % 2 == 0 else pB

                def fy(e, i=i, cch=cch, pY=pY, bs=big_tile_s[i], g=gget(id(pY)),
                       geb=(g_eb if i == 0 and cch == 0 else None)):
                    e.wait_ge(sml.h, bs)
                    if geb is not None:
                        e.wait_ge(dma["geb"].h, geb)
                    need(e, g)
                    for dt in range(ND):
                        e.matmul(pY[:],
                                 BIG[:, dt * SEQ + i * 128:dt * SEQ + (i + 1) * 128],
                                 WBUF[:, dt * DM + cch * 512:dt * DM + (cch + 1) * 512],
                                 start=(dt == 0), stop=False)
                    e.matmul(pY[:], c_onesb, c_ebbf[:, cch * 512:(cch + 1) * 512],
                             start=False, stop=True).then_inc(tmm.h, 1)
                T_(fy)
                tmm.n += 1
                t_y = tmm.n

                def fyev(e, i=i, cch=cch, pY=pY, t_y=t_y, slot=slot,
                         w=apply_v.get(i - 4)):
                    e.wait_ge(tmm.h, t_y)
                    if w is not None:
                        e.wait_ge(vml.h, w)
                    e.activation(out=SCR[:, slot + cch * 512:slot + (cch + 1) * 512],
                                 in_=pY[:], func=AF.Copy).then_inc(sml.h, 1)
                C_(fyev)
                sml.n += 1
                guard[id(pY)] = (sml.h, sml.n)
            y_evac[i] = sml.n

        def emit_apply(i):
            slot = (i % 4) * 2048
            xb = xt[i % 2]

            def fxl(e, i=i, xb=xb, w=apply_v.get(i - 2), sh=SX[i % 2].h):
                if w is not None:
                    e.wait_ge(vml.h, w)
                e.dma_start(out=xb[:], in_=src_res[i * 128:(i + 1) * 128, :]
                            ).then_inc(sh, 16)
            S_(fxl)
            SX[i % 2].n += 16
            sl_x = (SX[i % 2].h, SX[i % 2].n)

            def fap(e, i=i, xb=xb, slot=slot, sl_x=sl_x, sy=max(y_evac[i], s_w),
                    zg=list(zt_guard)):
                e.wait_ge(sml.h, sy)
                e.wait_ge(sl_x[0], sl_x[1])
                for g in zg:
                    need(e, g)
                e.scalar_tensor_tensor(out=zt[:], in0=SCR[:, slot:slot + 2048],
                                       scalar=wcol[:, i:i + 1], in1=xb[:],
                                       op0=ALU.mult, op1=ALU.add).then_inc(vml.h, 1)
            V_(fap)
            vml.n += 1
            v_z = vml.n
            apply_v[i] = v_z

            def fzs(e, i=i, v_z=v_z):
                e.wait_ge(vml.h, v_z)
                e.dma_start(out=dst_res[i * 128:(i + 1) * 128, :], in_=zt[:]
                            ).then_inc(dma["gzs"].h, 16)
            G_(fzs)
            dma["gzs"].n += 16
            g_z = dma["gzs"].n
            store_g[i] = g_z

            if fuse_next_ln:
                s_st = ln_stats(zt, [(vml.h, v_z)], i % 2)
                v_xn = ln_normalize(zt, s_st, i % 2)
                s_big = transpose_to_big(i, v_xn)
                big_next[i] = s_big
                zt_guard[0] = (vml.h, v_xn)
                zt_guard[1] = (dma["gzs"].h, g_z)
            else:
                zt_guard[0] = (dma["gzs"].h, g_z)
                zt_guard[1] = None

        for i in range(NT):
            emit_y(i)
            if i >= 3:
                emit_apply(i - 3)
        for i in range(NT - 3, NT):
            emit_apply(i)
        guard["zt"] = zt_guard[0]
        guard["zt_store"] = zt_guard[1]
        return big_next

    def initial_ln():
        big_tile_s = [None] * NT
        xn_done = {}
        for i in range(NT):
            xb = xt[i % 2]

            def fl(e, i=i, xb=xb, w=xn_done.get(i - 2), sh=SX[i % 2].h):
                if w is not None:
                    e.wait_ge(vml.h, w)
                e.dma_start(out=xb[:], in_=x_in[i * 128:(i + 1) * 128, :]
                            ).then_inc(sh, 16)
            S_(fl)
            SX[i % 2].n += 16
            s_st = ln_stats(xb, [(SX[i % 2].h, SX[i % 2].n)], i % 2)
            if sub < 2:
                xn_done[i] = vml.n
                big_tile_s[i] = sml.n
                continue
            v_xn = ln_normalize(xb, s_st, i % 2)
            xn_done[i] = v_xn
            if sub < 3:
                big_tile_s[i] = sml.n
                continue
            big_tile_s[i] = transpose_to_big(i, v_xn)
        xt_free = {-2: xn_done[NT - 2], -1: xn_done[NT - 1]}
        return big_tile_s, xt_free

    def kq_rope_group(i, pP, colw, dst_col_fn, t_mm, rope_state):
        ngr = colw // 256

        def fr1(e, i=i, ngr=ngr, pP=pP, t_mm=t_mm, gr=rope_state.get("g"),
                w=rope_state.get("v2")):
            e.wait_ge(tmm.h, t_mm)
            e.wait_ge(dma["grope"].h, gr)
            if w is not None:
                e.wait_ge(pml.h, w)
            CF = c_rope_c[:, i * 256:(i + 1) * 256]
            SF3 = c_rope_s[:, i * 256:(i + 1) * 256].rearrange(
                "p (j t) -> p j t", t=2)
            for g2 in range(ngr):
                b = g2 * 256
                pg3 = pP[:, b:b + 256].rearrange("p (j t) -> p j t", t=2)
                q3 = qrt[:, 512 + b:512 + b + 256].rearrange("p (j t) -> p j t", t=2)
                e.tensor_tensor(out=qrt[:, b:b + 256], in0=pP[:, b:b + 256],
                                in1=CF, op=ALU.mult)
                e.tensor_tensor(out=q3[:, :, 0:1], in0=pg3[:, :, 1:2],
                                in1=SF3[:, :, 0:1], op=ALU.mult)
                ins = e.tensor_tensor(out=q3[:, :, 1:2], in0=pg3[:, :, 0:1],
                                      in1=SF3[:, :, 1:2], op=ALU.mult)
            ins.then_inc(vml.h, 1)
        V_(fr1)
        vml.n += 1
        v_r1 = vml.n

        def fr2(e, colw=colw, v_r1=v_r1, w=rope_state.get("t")):
            e.wait_ge(vml.h, v_r1)
            if w is not None:
                e.wait_ge(tmm.h, w)
            e.tensor_tensor(out=qro[:, 0:colw], in0=qrt[:, 0:colw],
                            in1=qrt[:, 512:512 + colw],
                            op=ALU.add).then_inc(pml.h, 1)
        G_(fr2)
        pml.n += 1
        p_r2 = pml.n
        rope_state["v2"] = p_r2

        ntr = colw // 128
        half = state_tr[0] % 2
        state_tr[0] += 1

        def ftr(e, ntr=ntr, half=half, p_r2=p_r2, g=gget(("pTR", half))):
            e.wait_ge(pml.h, p_r2)
            need(e, g)
            for j in range(ntr):
                ins = e.transpose(pTR[:, half * 1024 + j * 128:half * 1024 + (j + 1) * 128],
                                  qro[:, j * 128:(j + 1) * 128], c_id_bf[:])
            ins.then_inc(tmm.h, 1)
        T_(ftr)
        tmm.n += 1
        t_tr = tmm.n

        def fte(e, ntr=ntr, half=half, t_tr=t_tr, dst=dst_col_fn):
            e.wait_ge(tmm.h, t_tr)
            for j in range(ntr):
                e.activation(out=dst(j), in_=pTR[:, half * 1024 + j * 128:
                                               half * 1024 + (j + 1) * 128],
                             func=AF.Copy).then_inc(sml.h, 1)
        C_(fte)
        sml.n += ntr
        guard[("pTR", half)] = (sml.h, sml.n)
        return sml.n, v_r1

    def attention_phase(nheads=H, nqi=NT):
        rope_state = {}
        state_u = {}
        state_d = {}
        state_att = {}
        state_att2 = {}
        state_pe = [0]
        state_uh = [0]

        def flq(e):
            e.dma_start(out=lqk[0], in_=lq1).then_inc(dma["smc"].h, 16)
            e.dma_start(out=lqk[1], in_=lq2).then_inc(dma["smc"].h, 16)
            e.dma_start(out=lqk[2], in_=lk1).then_inc(dma["smc"].h, 16)
            e.dma_start(out=lqk[3], in_=lk2).then_inc(dma["smc"].h, 16)
        S_(flq)
        dma["smc"].n += 64
        sl_lqk = dma["smc"].n

        def fl1(e, sl_lqk=sl_lqk):
            e.wait_ge(dma["smc"].h, sl_lqk)
            e.scalar_tensor_tensor(out=lqk[0], in0=lqk[0], scalar=1.0,
                                   in1=lqk[2], op0=ALU.mult, op1=ALU.mult,
                                   accum_out=lsc[:, 0:1])
            e.scalar_tensor_tensor(out=lqk[1], in0=lqk[1], scalar=1.0,
                                   in1=lqk[3], op0=ALU.mult, op1=ALU.mult,
                                   accum_out=lsc[:, 1:2])
            e.tensor_copy(out=lsc[:, 6:7], in_=lsc[:, 1:2]).then_inc(vml.h, 1)
        V_(fl1)
        vml.n += 1
        v_l1 = vml.n

        def fl2(e, v_l1=v_l1):
            e.wait_ge(vml.h, v_l1)
            e.activation(out=lsc[:, 2:4], in_=lsc[:, 0:2], func=AF.Exp).then_inc(sml.h, 1)
        C_(fl2)
        sml.n += 1
        s_l2 = sml.n

        def fl3(e, s_l2=s_l2):
            e.wait_ge(sml.h, s_l2)
            e.tensor_tensor(out=lsc[:, 4:5], in0=lsc[:, 2:3], in1=lsc[:, 3:4],
                            op=ALU.subtract).then_inc(vml.h, 1)
        V_(fl3)
        vml.n += 1
        v_l3a = vml.n

        def fl3b(e, v_l3a=v_l3a):
            e.wait_ge(vml.h, v_l3a)
            e.activation(out=lsc[:, 5:6], in_=lsc[:, 4:5], func=AF.Identity,
                         bias=c_misc[0:16, 1:2]).then_inc(sml.h, 1)
        C_(fl3b)
        sml.n += 1
        v_l3 = sml.n

        def fl4(e, v_l3=v_l3):
            e.wait_ge(sml.h, v_l3)
            e.transpose(pS_lr, lsc[:, 5:6], c_id_f[0:16, 0:16]).then_inc(tmm.h, 1)
        T_(fl4)
        tmm.n += 1
        t_l4 = tmm.n

        def fl5(e, t_l4=t_l4):
            e.wait_ge(tmm.h, t_l4)
            e.tensor_copy(out=srow[:], in_=pS_lr).then_inc(vml.h, 1)
        V_(fl5)
        vml.n += 1
        v_l5 = vml.n

        def fl6(e, v_l5=v_l5):
            e.wait_ge(vml.h, v_l5)
            e.matmul(pS_lam, c_ones[:], srow[:], start=True,
                     stop=True).then_inc(tmm.h, 1)
        T_(fl6)
        tmm.n += 1
        t_l6 = tmm.n

        def fl7(e, t_l6=t_l6):
            e.wait_ge(tmm.h, t_l6)
            e.tensor_copy(out=lamc[:], in_=pS_lam).then_inc(vml.h, 1)
        V_(fl7)
        vml.n += 1
        v_lam = vml.n

        def frt(e):
            e.dma_start(out=SCR[:, 0:NT * 256].rearrange("p (t c) -> p t c", c=256),
                        in_=ropec.rearrange("(t p) c -> p t c", p=128)
                        ).then_inc(dma["grope"].h, 16)
            e.dma_start(out=SCR[:, NT * 256:2 * NT * 256].rearrange("p (t c) -> p t c", c=256),
                        in_=ropes.rearrange("(t p) c -> p t c", p=128)
                        ).then_inc(dma["grope"].h, 16)
        G_(frt)
        dma["grope"].n += 32
        rope_state["g"] = dma["grope"].n

        def fwk(e):
            e.dma_start(out=WBUF[:, WSLOT:WSLOT + ND * 1024].rearrange("p (t c) -> p t c", c=1024),
                        in_=wk.rearrange("(t p) c -> p t c", p=128)
                        ).then_inc(dma["gwk"].h, 16)
        G_(fwk)
        dma["gwk"].n += 16
        g_wk = dma["gwk"].n

        pend_k = None
        kq_steps = [(i, cch) for i in range(NT) for cch in range(2)] + [None]
        for stp in kq_steps:
            if stp is not None:
                i, cch = stp
                pP = pA if (i * 2 + cch) % 2 == 0 else pB

                def fkm(e, i=i, cch=cch, pP=pP, g_wk=g_wk, g=gget(id(pP))):
                    e.wait_ge(dma["gwk"].h, g_wk)
                    need(e, g)
                    for dt in range(ND):
                        ins = e.matmul(pP[:],
                                       BIG[:, dt * SEQ + i * 128:dt * SEQ + (i + 1) * 128],
                                       WBUF[:, WSLOT + dt * 1024 + cch * 512:
                                            WSLOT + dt * 1024 + (cch + 1) * 512],
                                       start=(dt == 0), stop=(dt == ND - 1))
                    ins.then_inc(tmm.h, 1)
                T_(fkm)
                tmm.n += 1
                cur_k = (i, cch, pP, tmm.n)
            else:
                cur_k = None
            if pend_k is None:
                pend_k = cur_k
                continue
            i, cch, pP, t_mm = pend_k
            pend_k = cur_k

            def dstf(j, i=i, cch=cch):
                kc = cch * 4 + j
                return WBUF[:, KT0 + kc * SEQ + i * 128:KT0 + kc * SEQ + (i + 1) * 128]
            s_done, v_ev = kq_rope_group(i, pP, 512, dstf, t_mm, rope_state)
            guard[id(pP)] = (vml.h, v_ev)
        kT_ready = sml.n

        def fwv(e, kT_ready=kT_ready):
            e.wait_ge(sml.h, kT_ready)
            e.dma_start(out=WBUF[:, WSLOT:WSLOT + ND * 512].rearrange("p (t c) -> p t c", c=512),
                        in_=wv.rearrange("(t p) c -> p t c", p=128)
                        ).then_inc(dma["gwv"].h, 16)
        G_(fwv)
        dma["gwv"].n += 16
        g_wv = dma["gwv"].n

        for i in range(NT):
            pP = pA if i % 2 == 0 else pB

            def fvm(e, i=i, pP=pP, g_wv=g_wv, g=gget(id(pP))):
                e.wait_ge(dma["gwv"].h, g_wv)
                need(e, g)
                for dt in range(ND):
                    ins = e.matmul(pP[:], BIG[:, dt * SEQ + i * 128:dt * SEQ + (i + 1) * 128],
                                   WBUF[:, WSLOT + dt * 512:WSLOT + (dt + 1) * 512],
                                   start=(dt == 0), stop=(dt == ND - 1))
                ins.then_inc(tmm.h, 1)
            T_(fvm)
            tmm.n += 1
            t_v = tmm.n

            def fve(e, i=i, pP=pP, t_v=t_v):
                e.wait_ge(tmm.h, t_v)
                e.activation(out=WBUF[:, VS0 + i * 512:VS0 + (i + 1) * 512],
                             in_=pP[:], func=AF.Copy).then_inc(sml.h, 1)
            C_(fve)
            sml.n += 1
            guard[id(pP)] = (sml.h, sml.n)
        v_ready = sml.n

        WQS = [WSLOT, WSLOT + ND * 256]
        wq_user = {}
        for h in range(nheads):
            slot = WQS[h % 2]
            kv = h // 4

            def fwq(e, h=h, slot=slot, w=wq_user.get(h - 2), vr=v_ready):
                if h < 2:
                    e.wait_ge(sml.h, vr)
                if w is not None:
                    e.wait_ge(tmm.h, w)
                e.dma_start(out=WBUF[:, slot:slot + ND * 256].rearrange(
                                "p (t c) -> p t c", c=256),
                            in_=wq[:, h * 256:(h + 1) * 256].rearrange(
                                "(t p) c -> p t c", p=128)).then_inc(GWQ[h % 2].h, 16)
            G_(fwq)
            GWQ[h % 2].n += 16
            g_wq = (GWQ[h % 2].h, GWQ[h % 2].n)

            pend_q = None
            for i in list(range(NT)) + [None]:
                if i is not None:
                    pP = pA if i % 2 == 0 else pB

                    def fqm(e, i=i, pP=pP, slot=slot, g_wq=g_wq, g=gget(id(pP)),
                            qg=gget("qTh") if i == 0 else None):
                        e.wait_ge(g_wq[0], g_wq[1])
                        need(e, g)
                        need(e, qg)
                        for dt in range(ND):
                            ins = e.matmul(pP[:, 0:256],
                                           BIG[:, dt * SEQ + i * 128:dt * SEQ + (i + 1) * 128],
                                           WBUF[:, slot + dt * 256:slot + (dt + 1) * 256],
                                           start=(dt == 0), stop=(dt == ND - 1))
                        ins.then_inc(tmm.h, 1)
                    T_(fqm)
                    tmm.n += 1
                    cur_q = (i, pP, tmm.n)
                else:
                    cur_q = None
                if pend_q is None:
                    pend_q = cur_q
                    continue
                i_p, pP, t_mm = pend_q
                pend_q = cur_q

                def dstq(j, i=i_p):
                    return qTh[:, j * SEQ + i * 128:j * SEQ + (i + 1) * 128]
                s_done, v_ev = kq_rope_group(i_p, pP, 256, dstq, t_mm, rope_state)
                guard[id(pP)] = (vml.h, v_ev)
            wq_user[h] = tmm.n
            q_ready = sml.n

            EB0 = [e0, e0b, e0c, e0d]
            EB1 = [e1, e1b, e1c, e1d]
            for grp in range(nqi // 4):
                qis = [4 * grp + j for j in range(4)]
                nchs = [((qi + 1) * 128 + 511) // 512 for qi in qis]

                for j, qi in enumerate(qis):
                    klen = (qi + 1) * 128
                    for kc in range(nchs[j]):
                        wch = min(512, klen - kc * 512)
                        isdiag = (kc == nchs[j] - 1)
                        half = state_pe[0] % 2
                        state_pe[0] += 1
                        hb = half * 512
                        hE0 = pE0[:, hb:hb + wch]
                        hE1 = pE1[:, hb:hb + wch]

                        def fem(e, qi=qi, kc=kc, wch=wch, isdiag=isdiag,
                                hE0=hE0, hE1=hE1, kv=kv,
                                g0=gget(("pE", half)),
                                kr=(q_ready if (qi == qis[0] and kc == 0)
                                    else None),
                                lr=(v_lam if (h == 0 and qi == 0) else None)):
                            if kr is not None:
                                e.wait_ge(sml.h, kr)
                            if lr is not None:
                                e.wait_ge(vml.h, lr)
                            need(e, g0)
                            for a, hE in ((0, hE0), (1, hE1)):
                                kcg = kv * 2 + a
                                ins = e.matmul(hE,
                                               qTh[:, a * SEQ + qi * 128:
                                                   a * SEQ + (qi + 1) * 128],
                                               WBUF[:, KT0 + kcg * SEQ + kc * 512:
                                                    KT0 + kcg * SEQ + kc * 512 + wch],
                                               start=True, stop=not isdiag)
                                if isdiag:
                                    ins = e.matmul(hE[:, wch - 128:wch], c_id_bf[:],
                                                   c_emask[:], start=False, stop=True)
                            ins.then_inc(tmm.h, 1)
                        T_(fem)
                        tmm.n += 1
                        t_e = tmm.n

                        def fex(e, j=j, kc=kc, wch=wch, hE0=hE0, hE1=hE1, t_e=t_e,
                                g=(gget(("eb", j)) if kc == 0 else None),
                                g2=(gget("sacc") if (j == 0 and kc == 0) else None)):
                            e.wait_ge(tmm.h, t_e)
                            need(e, g)
                            need(e, g2)
                            e.activation(out=EB0[j][:, kc * 512:kc * 512 + wch],
                                         in_=hE0, func=AF.Exp, scale=CEXP,
                                         accum_out=sstat[:, 32 + 8 * j + kc:
                                                         33 + 8 * j + kc])
                            e.activation(out=EB1[j][:, kc * 512:kc * 512 + wch],
                                         in_=hE1, func=AF.Exp, scale=CEXP,
                                         accum_out=sstat[:, 36 + 8 * j + kc:
                                                         37 + 8 * j + kc]
                                         ).then_inc(sml.h, 1)
                        C_(fex)
                        sml.n += 1
                        guard[("pE", half)] = (sml.h, sml.n)

                def fg1(e, nchs=nchs):
                    for j in range(4):
                        nch = nchs[j]
                        e.activation(out=srch[:, 12:12 + nch],
                                     in_=sstat[:, 32 + 8 * j:32 + 8 * j + nch],
                                     func=AF.Identity,
                                     accum_out=srch[:, 0 + j:1 + j])
                        e.activation(out=srch[:, 12:12 + nch],
                                     in_=sstat[:, 32 + 8 * j:32 + 8 * j + nch],
                                     func=AF.Identity, scale=-1.0,
                                     accum_out=srch[:, 4 + j:5 + j])
                        e.activation(out=srch[:, 12:12 + nch],
                                     in_=sstat[:, 36 + 8 * j:36 + 8 * j + nch],
                                     func=AF.Identity,
                                     accum_out=srch[:, 8 + j:9 + j])
                    e.copy(out=srch[:, 12:13], in_=srch[:, 0:1]).then_inc(sml.h, 1)
                C_(fg1)
                sml.n += 1
                s_g1 = sml.n

                def fg2(e, s_g1=s_g1):
                    e.wait_ge(sml.h, s_g1)
                    e.reciprocal(out=srch[:, 16:20], in_=srch[:, 4:8])
                    e.reciprocal(out=srch[:, 20:24],
                                 in_=srch[:, 8:12]).then_inc(vml.h, 1)
                V_(fg2)
                vml.n += 1
                v_g2 = vml.n

                def fg3(e, v_g2=v_g2):
                    e.wait_ge(vml.h, v_g2)
                    e.copy(out=srch[:, 24:28], in_=srch[:, 20:24]).then_inc(sml.h, 1)
                C_(fg3)
                sml.n += 1
                s_g3 = sml.n

                def fg4(e, s_g3=s_g3, h=h):
                    e.wait_ge(sml.h, s_g3)
                    for j in range(4):
                        ins = e.tensor_scalar(out=wcol[:, j:j + 1],
                                              in0=lamc[:, h:h + 1],
                                              scalar1=srch[:, 0 + j:1 + j],
                                              scalar2=srch[:, 24 + j:25 + j],
                                              op0=ALU.mult, op1=ALU.mult)
                    ins.then_inc(vml.h, 1)
                V_(fg4)
                vml.n += 1
                v_g4 = vml.n

                def fg5(e, v_g4=v_g4):
                    e.wait_ge(vml.h, v_g4)
                    e.copy(out=wcol[:, 4:8], in_=wcol[:, 0:4]).then_inc(sml.h, 1)
                C_(fg5)
                sml.n += 1
                s_gb = sml.n

                pend_pe = []
                steps = []
                for j, qi in enumerate(qis):
                    klen = (qi + 1) * 128
                    for kc in range(nchs[j]):
                        steps.append((j, qi, kc, min(512, klen - kc * 512),
                                      kc == nchs[j] - 1))
                pend = None
                for st in steps + [None]:
                    if st is not None:
                        j, qi, kc, wch, isdiag = st
                        half_u = state_uh[0] % 2
                        state_uh[0] += 1
                        ub = half_u * 512

                        def fcb(e, j=j, kc=kc, wch=wch, ub=ub, s_gb=s_gb,
                                w=state_u.get(half_u)):
                            e.wait_ge(sml.h, max(s_gb, w if w is not None else 0))
                            e.scalar_tensor_tensor(out=uneg[:, ub:ub + wch],
                                                   in0=EB1[j][:, kc * 512:
                                                              kc * 512 + wch],
                                                   scalar=wcol[:, 4 + j:5 + j],
                                                   in1=EB0[j][:, kc * 512:
                                                              kc * 512 + wch],
                                                   op0=ALU.mult,
                                                   op1=ALU.subtract
                                                   ).then_inc(vml.h, 1)
                        V_(fcb)
                        vml.n += 1
                        if kc == nchs[j] - 1:
                            guard[("eb", j)] = (vml.h, vml.n)
                        cur = (st, half_u, ub, vml.n)
                    else:
                        cur = None
                    if pend is None:
                        pend = cur
                        continue
                    (j, qi, kc, wch, isdiag), half_u, ub, v_u = pend
                    pend = cur

                    def fed(e, j=j, kc=kc, wch=wch, ub=ub, v_u=v_u,
                            w=state_d.get(half_u)):
                        e.wait_ge(vml.h, v_u)
                        if w is not None:
                            e.wait_ge(tmm.h, w)
                        e.activation(out=dchk[:, ub:ub + wch],
                                     in_=uneg[:, ub:ub + wch],
                                     func=AF.Exp, scale=srch[:, 16 + j:17 + j],
                                     accum_out=sstat[:, 32 + 8 * j + kc:
                                                     33 + 8 * j + kc]
                                     ).then_inc(sml.h, 1)
                    C_(fed)
                    sml.n += 1
                    state_u[half_u] = sml.n
                    s_d = sml.n

                    ntr = wch // 128
                    half = state_tr[0] % 2
                    state_tr[0] += 1

                    def ftd(e, ntr=ntr, ub=ub, half=half, s_d=s_d,
                            g=gget(("pTR", half))):
                        e.wait_ge(sml.h, s_d)
                        need(e, g)
                        for jt in range(ntr):
                            ins = e.transpose(pTR[:, half * 1024 + jt * 128:
                                                  half * 1024 + (jt + 1) * 128],
                                              dchk[:, ub + jt * 128:
                                                   ub + (jt + 1) * 128],
                                              c_id_bf[:])
                        ins.then_inc(tmm.h, 1)
                    T_(ftd)
                    tmm.n += 1
                    t_d = tmm.n
                    state_d[half_u] = t_d

                    ncopy = ntr - 1 if isdiag else ntr
                    if ncopy > 0:
                        def fde(e, kc=kc, ncopy=ncopy, half=half, t_d=t_d):
                            e.wait_ge(tmm.h, t_d)
                            e.tensor_copy(out=xn[:, kc * 512:
                                                 kc * 512 + ncopy * 128],
                                          in_=pTR[:, half * 1024:
                                                   half * 1024 + ncopy * 128]
                                          ).then_inc(vml.h, 1)
                        V_(fde)
                        vml.n += 1
                    if isdiag:
                        def fdm(e, ntr=ntr, half=half, t_d=t_d, qi=qi):
                            e.wait_ge(tmm.h, t_d)
                            e.tensor_tensor(out=xn[:, qi * 128:(qi + 1) * 128],
                                            in0=pTR[:, half * 1024 +
                                                    (ntr - 1) * 128:
                                                    half * 1024 + ntr * 128],
                                            in1=c_maskt[:],
                                            op=ALU.mult).then_inc(vml.h, 1)
                        V_(fdm)
                        vml.n += 1
                    guard[("pTR", half)] = (vml.h, vml.n)
                    if not isdiag:
                        continue
                    s_dT_s, s_dT_v = sml.n, vml.n
                    pV = pA if qi % 2 == 0 else pB

                    def fpv(e, qi=qi, kv=kv, pV=pV, s_dT_s=s_dT_s, s_dT_v=s_dT_v,
                            g=gget(id(pV)),
                            vr=(v_ready if (h == 0 and qi == 0) else None)):
                        e.wait_ge(sml.h, s_dT_s)
                        e.wait_ge(vml.h, s_dT_v)
                        if vr is not None:
                            e.wait_ge(sml.h, vr)
                        need(e, g)
                        for kt in range(qi + 1):
                            ins = e.matmul(pV[:, 0:128],
                                           xn[:, kt * 128:(kt + 1) * 128],
                                           WBUF[:, VS0 + kt * 512 + kv * 128:
                                                VS0 + kt * 512 + (kv + 1) * 128],
                                           start=(kt == 0), stop=(kt == qi))
                        ins.then_inc(tmm.h, 1)
                    T_(fpv)
                    tmm.n += 1
                    pend_pe.append((qi, pV, tmm.n))

                    if len(pend_pe) > 1:
                        qi_p, pV_p, t_pv_p = pend_pe.pop(0)

                        def fpe(e, qi=qi_p, pV=pV_p, t_pv=t_pv_p,
                                g=gget(("attA", qi_p))):
                            e.wait_ge(tmm.h, t_pv)
                            need(e, g)
                            e.activation(out=attsc[:, 256:384], in_=pV[:, 0:128],
                                         func=AF.Square,
                                         accum_out=sstat[:, qi:qi + 1])
                            e.activation(out=snapt[:, qi * 128:(qi + 1) * 128],
                                         in_=pV[:, 0:128],
                                         func=AF.Copy).then_inc(sml.h, 1)
                        C_(fpe)
                        sml.n += 1
                        guard[id(pV_p)] = (sml.h, sml.n)

                while pend_pe:
                    qi_p, pV_p, t_pv_p = pend_pe.pop(0)

                    def fpe(e, qi=qi_p, pV=pV_p, t_pv=t_pv_p,
                            g=gget(("attA", qi_p))):
                        e.wait_ge(tmm.h, t_pv)
                        need(e, g)
                        e.activation(out=attsc[:, 256:384], in_=pV[:, 0:128],
                                     func=AF.Square,
                                     accum_out=sstat[:, qi:qi + 1])
                        e.activation(out=snapt[:, qi * 128:(qi + 1) * 128],
                                     in_=pV[:, 0:128],
                                     func=AF.Copy).then_inc(sml.h, 1)
                    C_(fpe)
                    sml.n += 1
                    guard[id(pV_p)] = (sml.h, sml.n)
                s_pe_last = sml.n

                def fsd(e, qis=qis, nchs=nchs, s_pe_last=s_pe_last):
                    e.wait_ge(sml.h, s_pe_last)
                    for j, qi in enumerate(qis):
                        nch = nchs[j]
                        if nch > 1:
                            ins = e.tensor_reduce(out=sstat[:, 16 + qi:17 + qi],
                                                  in_=sstat[:, 32 + 8 * j:
                                                            32 + 8 * j + nch],
                                                  axis=AX.X, op=ALU.add)
                        else:
                            ins = e.tensor_copy(out=sstat[:, 16 + qi:17 + qi],
                                                in_=sstat[:, 32 + 8 * j:
                                                          33 + 8 * j])
                    ins.then_inc(vml.h, 1)
                V_(fsd)
                vml.n += 1
                guard["sacc"] = (vml.h, vml.n)
            v_head = vml.n
            s_head = sml.n

            if "pv" in KSKIP or "comb" in KSKIP or "head" in KSKIP:
                continue
            def fscA(e, v_head=v_head, s_head=s_head):
                e.wait_ge(vml.h, v_head)
                e.wait_ge(sml.h, s_head)
                e.copy(out=sstat[:, 32:48], in_=sstat[:, 16:32]).then_inc(sml.h, 1)
            C_(fscA)
            sml.n += 1
            sA = sml.n

            def fscB(e, sA=sA):
                e.wait_ge(sml.h, sA)
                e.tensor_scalar(out=sstat[:, 16:32], in0=sstat[:, 32:48],
                                scalar1=c_mcnt[:], scalar2=None,
                                op0=ALU.subtract).then_inc(vml.h, 1)
            V_(fscB)
            vml.n += 1
            vB = vml.n

            def fscC(e, vB=vB):
                e.wait_ge(vml.h, vB)
                e.copy(out=sstat[:, 32:48], in_=sstat[:, 16:32]).then_inc(sml.h, 1)
            C_(fscC)
            sml.n += 1
            sC = sml.n

            def fscD(e, sC=sC):
                e.wait_ge(sml.h, sC)
                e.tensor_tensor(out=sstat[:, 48:64], in0=sstat[:, 32:48],
                                in1=sstat[:, 32:48], op=ALU.mult).then_inc(vml.h, 1)
            V_(fscD)
            vml.n += 1
            vD = vml.n

            def fscE(e, vD=vD):
                e.wait_ge(vml.h, vD)
                e.activation(out=sstat[:, 32:48], in_=sstat[:, 48:64],
                             func=AF.Identity, scale=128.0e-8).then_inc(sml.h, 1)
            C_(fscE)
            sml.n += 1
            sE = sml.n

            def fscF(e, sE=sE):
                e.wait_ge(sml.h, sE)
                e.tensor_tensor(out=sstat[:, 48:64], in0=sstat[:, 32:48],
                                in1=sstat[:, 0:16], op=ALU.add).then_inc(vml.h, 1)
            V_(fscF)
            vml.n += 1
            vF = vml.n

            def fscG(e, vF=vF):
                e.wait_ge(vml.h, vF)
                e.activation(out=sstat[:, 32:48], in_=sstat[:, 48:64],
                             func=AF.Ln).then_inc(sml.h, 1)
            C_(fscG)
            sml.n += 1
            sG = sml.n

            def fscH(e, sG=sG):
                e.wait_ge(sml.h, sG)
                e.tensor_copy(out=sstat[:, 48:64],
                              in_=sstat[:, 32:48]).then_inc(vml.h, 1)
            V_(fscH)
            vml.n += 1
            vH = vml.n

            def fscI(e, vH=vH):
                e.wait_ge(vml.h, vH)
                e.activation(out=sstat[:, 0:16], in_=sstat[:, 48:64], func=AF.Exp,
                             scale=-0.5, bias=c_misc[:, 2:3]).then_inc(sml.h, 1)
            C_(fscI)
            sml.n += 1
            v_sc3 = sml.n
            guard["sstat_e"] = (sml.h, v_sc3)

            pend_ae = None
            for qi in (list(range(nqi)) + [None] if "evac" not in KSKIP
                       and "pv" not in KSKIP and "comb" not in KSKIP else []):
                if qi is not None:
                    sl = (qi % 2) * 128
                    so = 384 + (qi % 2) * 128

                    def fas(e, qi=qi, sl=sl, v_sc3=v_sc3,
                            w=state_att.get(qi % 2)):
                        e.wait_ge(sml.h, v_sc3)
                        if w is not None:
                            e.wait_ge(tmm.h, w)
                        e.tensor_scalar(out=attsc[:, sl:sl + 128],
                                        in0=snapt[:, qi * 128:(qi + 1) * 128],
                                        scalar1=sstat[:, 0 + qi:1 + qi],
                                        scalar2=None,
                                        op0=ALU.mult).then_inc(vml.h, 1)
                    V_(fas)
                    vml.n += 1
                    v_as = vml.n
                    guard[("attA", qi)] = (vml.h, v_as)

                    half = state_tr[0] % 2
                    state_tr[0] += 1

                    def fat(e, sl=sl, half=half, v_as=v_as,
                            g=gget(("pTR", half))):
                        e.wait_ge(vml.h, v_as)
                        need(e, g)
                        e.transpose(pTR[:, half * 1024:half * 1024 + 128],
                                    attsc[:, sl:sl + 128],
                                    c_id_bf[:]).then_inc(tmm.h, 1)
                    T_(fat)
                    tmm.n += 1
                    state_att[qi % 2] = tmm.n
                    cur_ae = (qi, so, half, tmm.n)
                else:
                    cur_ae = None
                if pend_ae is None:
                    pend_ae = cur_ae
                    continue
                qi_p, so, half, t_at = pend_ae
                pend_ae = cur_ae

                def fae(e, so=so, half=half, t_at=t_at,
                        w=state_att2.get(qi_p % 2)):
                    e.wait_ge(tmm.h, t_at)
                    if w is not None:
                        e.wait_ge(w[0], w[1])
                    e.tensor_copy(out=attsc[:, so:so + 128],
                                  in_=pTR[:, half * 1024:half * 1024 + 128]
                                  ).then_inc(vml.h, 1)
                V_(fae)
                vml.n += 1
                s_ae = vml.n
                guard[("pTR", half)] = (vml.h, s_ae)

                def fad(e, h=h, qi=qi_p, so=so, s_ae=s_ae):
                    e.wait_ge(vml.h, s_ae)
                    e.dma_start(out=attr[h * 128:(h + 1) * 128,
                                         qi * 128:(qi + 1) * 128],
                                in_=attsc[:, so:so + 128]).then_inc(GAT[qi % 2].h, 16)
                G_(fad)
                GAT[qi_p % 2].n += 16
                state_att2[qi_p % 2] = (GAT[qi_p % 2].h, GAT[qi_p % 2].n)
            guard["qTh"] = (tmm.h, tmm.n)
            rope_state["t"] = tmm.n

    def wo_phase():
        def fwo(e):
            e.dma_start(out=WBUF[:].rearrange("p (t c) -> p t c", c=2048),
                        in_=wo.rearrange("(t p) c -> p t c", p=128)
                        ).then_inc(dma["gwk"].h, 16)
        G_(fwo)
        dma["gwk"].n += 16
        g_wo = dma["gwk"].n

        def fws(e, g_wo=g_wo):
            e.wait_ge(dma["gwk"].h, g_wo)
            for dt in range(ND):
                ins = e.tensor_scalar(out=WBUF[:, dt * DM:(dt + 1) * DM],
                                      in0=WBUF[:, dt * DM:(dt + 1) * DM],
                                      scalar1=c_rmsw[:], scalar2=None, op0=ALU.mult)
            ins.then_inc(vml.h, 1)
        V_(fws)
        vml.n += 1
        v_wo = vml.n

        apply_v = {}
        big_tile_s = [None] * NT
        zt_guard = [None, None]
        for i in range(NT):
            sbase = (i % 2) * 2048

            def fsl(e, i=i, sbase=sbase, w=apply_v.get(i - 2)):
                if w is not None:
                    e.wait_ge(vml.h, w)
                e.dma_start(out=SCR[:, sbase:sbase + 2048].rearrange(
                                "p (t c) -> p t c", c=128),
                            in_=attr[:, i * 128:(i + 1) * 128].rearrange(
                                "(t p) c -> p t c", p=128)).then_inc(SSL[i % 2].h, 16)
            S_(fsl)
            SSL[i % 2].n += 16

            xb = xt[i % 2]

            def fx1(e, i=i, xb=xb, w=apply_v.get(i - 2)):
                if w is not None:
                    e.wait_ge(vml.h, w)
                e.dma_start(out=xb[:], in_=x1r[i * 128:(i + 1) * 128, :]
                            ).then_inc(SX[i % 2].h, 16)
            S_(fx1)
            SX[i % 2].n += 16
            sl_i = [(SSL[i % 2].h, SSL[i % 2].n), (SX[i % 2].h, SX[i % 2].n)]

            for cch in range(4):
                pW = pA if (i * 4 + cch) % 2 == 0 else pB

                def fwm(e, i=i, cch=cch, pW=pW, sbase=sbase, sl_i=sl_i,
                        v_wo=v_wo, g=gget(id(pW))):
                    for hh, nn in sl_i:
                        e.wait_ge(hh, nn)
                    if i == 0 and cch == 0:
                        e.wait_ge(vml.h, v_wo)
                    need(e, g)
                    for dt in range(ND):
                        ins = e.matmul(pW[:], SCR[:, sbase + dt * 128:sbase + (dt + 1) * 128],
                                       WBUF[:, dt * DM + cch * 512:dt * DM + (cch + 1) * 512],
                                       start=(dt == 0), stop=(dt == ND - 1))
                    ins.then_inc(tmm.h, 1)
                T_(fwm)
                tmm.n += 1
                t_w = tmm.n

                def fadd(e, cch=cch, pW=pW, xb=xb, t_w=t_w,
                         zg=(list(zt_guard) if cch == 0 else None)):
                    e.wait_ge(tmm.h, t_w)
                    if zg is not None:
                        for g in zg:
                            need(e, g)
                    e.tensor_tensor(out=zt[:, cch * 512:(cch + 1) * 512],
                                    in0=pW[:], in1=xb[:, cch * 512:(cch + 1) * 512],
                                    op=ALU.add).then_inc(vml.h, 1)
                V_(fadd)
                vml.n += 1
                guard[id(pW)] = (vml.h, vml.n)
            v_z = vml.n
            apply_v[i] = v_z

            def fzs(e, i=i, v_z=v_z):
                e.wait_ge(vml.h, v_z)
                e.dma_start(out=x2r[i * 128:(i + 1) * 128, :], in_=zt[:]
                            ).then_inc(dma["gzs"].h, 16)
            G_(fzs)
            dma["gzs"].n += 16
            g_z = dma["gzs"].n

            s_st = ln_stats(zt, [(vml.h, v_z)], i % 2)
            v_xn = ln_normalize(zt, s_st, i % 2)
            big_tile_s[i] = transpose_to_big(i, v_xn)
            zt_guard = [(vml.h, v_xn), (dma["gzs"].h, g_z)]
        guard["zt"] = zt_guard[0]
        guard["zt_store"] = zt_guard[1]
        xt_free = {-2: apply_v[NT - 2], -1: apply_v[NT - 1]}
        return big_tile_s, xt_free

    big_s, xt_free = initial_ln()
    if dbg:
        d_big1 = nc.dram_tensor("d_big1", [128, ND * SEQ], BF16,
                                kind="ExternalOutput").ap()
        _bs = big_s[NT - 1]

        def fdump1(e, _bs=_bs):
            e.wait_ge(sml.h, _bs)
            e.dma_start(out=d_big1, in_=BIG[:]).then_inc(dma["smc"].h, 16)
        S_(fdump1)
        dma["smc"].n += 16
    if stop >= 2:
        ffn_phase(0, x_in, x1r, fuse_next_ln=True, big_tile_s=big_s, xt_free=xt_free)
        barrier()
    if stop >= 3:
        nh_ = H
        nq_ = NT
        if stop == 3:
            if sub == 17:
                nh_, nq_ = 1, 0
            elif sub == 18:
                nh_, nq_ = 1, 1
            elif sub == 19:
                nh_, nq_ = 1, 4
            elif sub == 20:
                nh_, nq_ = 1, 2
            elif sub == 21:
                nh_, nq_ = 1, 3
            elif sub <= 16:
                nh_, nq_ = sub, NT
        attention_phase(nheads=nh_, nqi=nq_)
        barrier()
    if stop >= 4:
        big_s, xt_free = wo_phase()
        barrier()
    if stop >= 5:
        ffn_phase(1, x2r, y_out, fuse_next_ln=False, big_tile_s=big_s, xt_free=xt_free)

    if dbg:
        ddump = {}
        for nm, src in (("d_sall", sall), ("d_srch", srch), ("d_wcol", wcol),
                        ("d_stat", stat), ("d_snapt", snapt), ("d_sown", sown)):
            dt_ = nc.dram_tensor(nm, list(src.shape), F32, kind="ExternalOutput").ap()
            ddump[nm] = (dt_, src)
        d_big = nc.dram_tensor("d_big", [128, ND * SEQ], BF16, kind="ExternalOutput").ap()
        d_ccin = nc.dram_tensor("d_ccin", [16, SEQ], F32, kind="ExternalOutput").ap()
        d_ccout = nc.dram_tensor("d_ccout", [128, SEQ], F32, kind="ExternalOutput").ap()
        t9, v9, s9 = tmm.n, vml.n, sml.n
        dsn = [(d.h, d.n) for d in dma.values() if d.n > 0]

        def fdump(e, t9=t9, v9=v9, s9=s9, dsn=list(dsn)):
            e.wait_ge(tmm.h, t9)
            e.wait_ge(vml.h, v9)
            e.wait_ge(sml.h, s9)
            for h, n in dsn:
                e.wait_ge(h, n)
            for nm, (dt_, src) in ddump.items():
                if nm == "d_srch":
                    e.dma_start(out=dt_[:, 0:12], in_=src[:, 0:12]).then_inc(dma["smc"].h, 16)
                elif nm == "d_stat":
                    e.dma_start(out=dt_[:, 0:8], in_=src[:, 0:8]).then_inc(dma["smc"].h, 16)
                else:
                    e.dma_start(out=dt_, in_=src[:, :]).then_inc(dma["smc"].h, 16)
            e.dma_start(out=d_big, in_=BIG[:]).then_inc(dma["smc"].h, 16)
            e.dma_start(out=d_ccin, in_=cc_in[0].ap()).then_inc(dma["smc"].h, 16)
            e.dma_start(out=d_ccout, in_=cc_out[0].ap()).then_inc(dma["smc"].h, 16)
        S_(fdump)
        dma["smc"].n += 16 * 9

    dsnap_fin = [(d.h, d.n) for d in dma.values() if d.n > 0]

    def ffin(e, dsnap=list(dsnap_fin)):
        for h, n in dsnap:
            e.wait_ge(h, n)
    G_(ffin)

    with nc.Block() as blk:
        @blk.sync
        def _(e):
            for f in prog["sync"]:
                f(e)

        @blk.gpsimd
        def _(e):
            for f in prog["gpsimd"]:
                f(e)

        @blk.tensor
        def _(e):
            for f in prog["tensor"]:
                f(e)

        @blk.vector
        def _(e):
            for f in prog["vector"]:
                f(e)

        @blk.scalar
        def _(e):
            for f in prog["scalar"]:
                f(e)

    ctx.close()
    return nc


_CONSTS = make_consts()


def kernel(L0_attn_x, L0_attn_Wq, L0_attn_Wk, L0_attn_Wv, L0_attn_Wo,
           L0_attn_lq1, L0_attn_lq2, L0_attn_lk1, L0_attn_lk2, L0_attn_rms_w,
           L0_ffn_router_w, L0_ffn_router_b, L0_ffn_expert_w, L0_ffn_expert_b):
    nc = build_program()
    x = np.ascontiguousarray(np.asarray(L0_attn_x, dtype=np.float32))
    base = dict(
        wq=np.ascontiguousarray(np.asarray(L0_attn_Wq, np.float32)),
        wk=np.ascontiguousarray(np.asarray(L0_attn_Wk, np.float32)),
        wv=np.ascontiguousarray(np.asarray(L0_attn_Wv, np.float32)),
        wo=np.ascontiguousarray(np.asarray(L0_attn_Wo, np.float32)),
        lq1=np.ascontiguousarray(np.asarray(L0_attn_lq1, np.float32)),
        lq2=np.ascontiguousarray(np.asarray(L0_attn_lq2, np.float32)),
        lk1=np.ascontiguousarray(np.asarray(L0_attn_lk1, np.float32)),
        lk2=np.ascontiguousarray(np.asarray(L0_attn_lk2, np.float32)),
        rmsw=np.ascontiguousarray(np.asarray(L0_attn_rms_w, np.float32).reshape(128, 1)),
        rw=np.ascontiguousarray(np.asarray(L0_ffn_router_w, np.float32)),
        rb=np.ascontiguousarray(np.asarray(L0_ffn_router_b, np.float32).reshape(1, 16)),
        ew=np.ascontiguousarray(np.asarray(L0_ffn_expert_w, np.float32)),
        eb=np.ascontiguousarray(np.asarray(L0_ffn_expert_b, np.float32).reshape(1, DM)),
        **_CONSTS,
    )
    pidx = np.arange(128)
    in_maps = []
    for r in range(NCORES):
        m = dict(base)
        m["x_in"] = np.ascontiguousarray(x[r])
        m["selr"] = (pidx // 16 == r).astype(np.float32).reshape(128, 1)
        in_maps.append(m)
    res = run_bass_kernel_spmd(nc, in_maps, core_ids=list(range(NCORES)))
    return np.stack([res.results[r]["y_out"] for r in range(NCORES)], axis=0)


# version 07
# speedup vs baseline: 1.4523x; 1.0498x over previous
import math
import numpy as np
from contextlib import ExitStack

import concourse.bass as bass
import concourse.mybir as mybir
from concourse.bass_utils import run_bass_kernel_spmd

F32 = mybir.dt.float32
BF16 = mybir.dt.bfloat16
U32 = mybir.dt.uint32
AF = mybir.ActivationFunctionType
ALU = mybir.AluOpType
AX = mybir.AxisListType

SEQ = 2048
DM = 2048
NT = 16
ND = 16
H = 16
KVH = 4
NEXP = 16
TOPK = 1024
NCORES = 8
CEXP = 1.0 / math.sqrt(128.0)
LAM_INIT = 0.2
NEG_BIG = -1.0e30
BIGV = 1.0e6
SEARCH_ITERS = 34
RMS_SC = 0.8 * math.sqrt(128.0)


def make_consts():
    c = {}
    c["ident"] = np.eye(128, dtype=np.float32)
    j = np.arange(0, 256, 2, dtype=np.float32)
    theta = (np.float32(1.0) / (np.float32(10000.0) ** (j / np.float32(256.0)))).astype(np.float32)
    pos = np.arange(SEQ, dtype=np.float32)
    ang = pos[:, None] * theta[None, :]
    cosv = np.cos(ang).astype(np.float32)
    sinv = np.sin(ang).astype(np.float32)
    CF = np.empty((SEQ, 256), dtype=np.float32)
    SF = np.empty((SEQ, 256), dtype=np.float32)
    CF[:, 0::2] = cosv
    CF[:, 1::2] = cosv
    SF[:, 0::2] = -sinv
    SF[:, 1::2] = sinv
    c["ropec"] = CF
    c["ropes"] = SF
    p = np.arange(128)[:, None]
    cc = np.arange(128)[None, :]
    c["emask"] = np.where(cc <= p, 0.0, NEG_BIG).astype(np.float32)
    kk = np.arange(128)[:, None]
    qq = np.arange(128)[None, :]
    c["maskt"] = (qq >= kk).astype(np.float32)
    c["maskcnt"] = (127.0 - np.arange(128, dtype=np.float32)).reshape(128, 1)
    pidx = np.arange(128)
    c["mgrp"] = (pidx[:, None] % 16 == pidx[None, :] % 16).astype(np.float32)
    c["repl16"] = (np.arange(16)[:, None] == pidx[None, :] % 16).astype(np.float32)
    c["ones_row"] = np.ones((1, 128), dtype=np.float32)
    mc = np.zeros((128, 4), dtype=np.float32)
    mc[:, 0] = 1e-5
    mc[:, 1] = LAM_INIT
    mc[:, 2] = math.log(0.8 * math.sqrt(128.0))
    mc[:, 3] = 2.0
    c["cmisc"] = mc
    return c


class Ctr:
    def __init__(self, ctx, nc, name):
        self.h = ctx.enter_context(nc.semaphore(name))
        self.n = 0


def build_program(dbg=False, stop=5, sub=3):
    import os as _os0
    KSKIP = set((_os0.environ.get("KSKIP") or "").split(","))
    import os as _os
    nc = bass.Bass("TRN2", target_bir_lowering=False,
                   detect_race_conditions=bool(_os.environ.get("KSTRICT")))
    ctx = ExitStack()

    dri = lambda n, sh: nc.dram_tensor(n, sh, F32, kind="ExternalInput").ap()
    x_in = dri("x_in", [SEQ, DM])
    wq = dri("wq", [DM, 4096])
    wk = dri("wk", [DM, 1024])
    wv = dri("wv", [DM, 512])
    wo = dri("wo", [DM, DM])
    lq1 = dri("lq1", [16, 128])
    lq2 = dri("lq2", [16, 128])
    lk1 = dri("lk1", [16, 128])
    lk2 = dri("lk2", [16, 128])
    rmsw = dri("rmsw", [128, 1])
    rw = dri("rw", [DM, 16])
    rb = dri("rb", [1, 16])
    ew = dri("ew", [DM, DM])
    eb = dri("eb", [1, DM])
    ident = dri("ident", [128, 128])
    ropec = dri("ropec", [SEQ, 256])
    ropes = dri("ropes", [SEQ, 256])
    emask = dri("emask", [128, 128])
    maskt = dri("maskt", [128, 128])
    maskcnt = dri("maskcnt", [128, 1])
    mgrp = dri("mgrp", [128, 128])
    repl16 = dri("repl16", [16, 128])
    ones_row = dri("ones_row", [1, 128])
    selr = dri("selr", [128, 1])
    cmisc = dri("cmisc", [128, 4])

    y_out = nc.dram_tensor("y_out", [SEQ, DM], F32, kind="ExternalOutput").ap()
    ik = dict(kind="ExternalOutput") if dbg else {}
    x1r = nc.dram_tensor("x1r", [SEQ, DM], F32, **ik).ap()
    x2r = nc.dram_tensor("x2r", [SEQ, DM], F32, **ik).ap()
    attr = nc.dram_tensor("attr", [DM, SEQ], BF16, **ik).ap()
    cc_in = [nc.dram_tensor(f"cc_in{k}", [16, SEQ], F32) for k in range(2)]
    cc_out = [nc.dram_tensor(f"cc_out{k}", [128, SEQ], F32) for k in range(2)]

    sbt = lambda n, sh, dt: ctx.enter_context(nc.sbuf_tensor(n, sh, dt))
    pst = lambda n, sh, dt: ctx.enter_context(nc.psum_tensor(n, sh, dt))

    BIG = sbt("BIG", [128, ND * SEQ], BF16)
    WBUF = sbt("WBUF", [128, ND * DM], BF16)
    KT0 = 0
    WSLOT = 16384
    VS0 = 24576
    SCR = sbt("SCR", [128, 8192], BF16)
    qTh = sbt("qTh", [128, 2 * SEQ], BF16)
    xt = [sbt(f"xt{i}", [128, DM], F32) for i in range(2)]
    e0b = xt[0][:, :].bitcast(BF16)[:, 0:SEQ]
    e1b = xt[0][:, :].bitcast(BF16)[:, SEQ:2 * SEQ]
    e0c = xt[1][:, :].bitcast(BF16)[:, 0:SEQ]
    e1c = xt[1][:, :].bitcast(BF16)[:, SEQ:2 * SEQ]
    zt = sbt("zt", [128, DM], F32)
    e0d = zt[:, :].bitcast(BF16)[:, 0:SEQ]
    e1d = zt[:, :].bitcast(BF16)[:, SEQ:2 * SEQ]
    xn = sbt("xn", [128, DM], BF16)
    e0 = sbt("e0", [128, SEQ], BF16)
    e1 = sbt("e1", [128, SEQ], BF16)
    sall = qTh[:, :].bitcast(F32)
    snapt = sbt("snapt", [128, SEQ], F32)
    qrw = sbt("qrw", [128, 1024], BF16)
    qrt = sbt("qrt", [128, 1024], BF16)
    uneg = qrw
    dchk = qrt
    attsc = sbt("attsc", [128, 640], BF16)
    qro = attsc[:, 0:512]
    ST = snapt[0:16, :]
    lqk = [snapt[0:16, i * 128:(i + 1) * 128] for i in range(4)]
    sown = sbt("sown", [128, 64], F32)
    stat = sbt("stat", [128, 64], F32)
    srch = sbt("srch", [128, 32], F32)
    srchu = sbt("srchu", [128, 1], U32)
    srow = sbt("srow", [1, 16], F32)
    wcol = sbt("wcol", [128, 16], F32)
    sstat = sbt("sstat", [128, 96], F32)
    lamc = sbt("lamc", [128, 16], F32)
    lsc = sbt("lsc", [16, 8], F32)
    c_id_bf = sbt("c_id_bf", [128, 128], BF16)
    c_id_f = sbt("c_id_f", [128, 128], F32)
    c_emask = sbt("c_emask", [128, 128], BF16)
    c_maskt = sbt("c_maskt", [128, 128], BF16)
    c_mcnt = sbt("c_mcnt", [128, 1], F32)
    c_mgrp = sbt("c_mgrp", [128, 128], F32)
    c_repl = sbt("c_repl", [16, 128], F32)
    c_ones = sbt("c_ones", [1, 128], F32)
    c_selr = sbt("c_selr", [128, 1], F32)
    c_rb = sbt("c_rb", [1, 16], F32)
    c_ebbf = e1[0:1, 0:DM]
    jnk_bf = snapt[:, 0:1024].bitcast(BF16)
    jnk_bf2 = snapt[:, 1024:2048].bitcast(BF16)
    c_rmsw = sbt("c_rmsw", [128, 1], F32)
    c_misc = sbt("c_misc", [128, 4], F32)
    rwsb = sbt("rwsb", [128, ND * 16], BF16)
    c_onesb = c_maskt[0:1, :]
    c_rope_c = SCR[:, 0:NT * 256]
    c_rope_s = SCR[:, NT * 256:2 * NT * 256]

    pE0 = pst("pE0", [128, 1024], F32)
    pE1 = pst("pE1", [128, 1024], F32)
    pA = pst("pA", [128, 512], F32)
    pB = pst("pB", [128, 512], F32)
    pTR = pst("pTR", [128, 2048], BF16)

    pS_cnt = pE0[:, 0:1]
    pS_thx = pE0[:, 1:2]
    pS_lam = pE0[:, 2:18]
    pS_w = pE0[:, 18:34]
    pS_st = [pE0[0:16, 34:162], pE0[0:16, 162:290]]
    pS_r0 = pE0[0:1, 290:418]
    pS_r16 = pE0[0:16, 418:419]
    pS_lr = pE0[0:1, 420:436]

    dma = {n: Ctr(ctx, nc, n) for n in (
        "smc", "sx0", "sx1", "ssl0", "ssl1", "gmc", "geb", "gst", "gsall",
        "grope", "gwk", "gwv", "gwq0", "gwq1", "gzs", "gat0", "gat1")}
    SX = [dma["sx0"], dma["sx1"]]
    SSL = [dma["ssl0"], dma["ssl1"]]
    GWQ = [dma["gwq0"], dma["gwq1"]]
    GAT = [dma["gat0"], dma["gat1"]]
    ccs = Ctr(ctx, nc, "ccs")
    tmm = Ctr(ctx, nc, "tmm")
    pml = Ctr(ctx, nc, "pml")
    vml = Ctr(ctx, nc, "vml")
    sml = Ctr(ctx, nc, "sml")

    prog = {k: [] for k in ("sync", "gpsimd", "tensor", "vector", "scalar")}
    S_ = prog["sync"].append
    G_ = prog["gpsimd"].append
    T_ = prog["tensor"].append
    V_ = prog["vector"].append
    C_ = prog["scalar"].append

    guard = {}

    def need(e, g):
        if g is None:
            return
        if isinstance(g, list):
            for gg in g:
                need(e, gg)
        else:
            e.wait_ge(g[0], g[1])

    def gget(key):
        return guard.get(key)

    state_tr = [0]

    def _c0(e):
        e.dma_start(out=c_id_f[:], in_=ident).then_inc(dma["smc"].h, 16)
        e.dma_start(out=c_mcnt[:], in_=maskcnt).then_inc(dma["smc"].h, 16)
        e.dma_start(out=c_mgrp[:], in_=mgrp).then_inc(dma["smc"].h, 16)
        e.dma_start(out=c_repl[:], in_=repl16).then_inc(dma["smc"].h, 16)
        e.dma_start(out=c_ones[:], in_=ones_row).then_inc(dma["smc"].h, 16)
        e.dma_start(out=c_rb[:], in_=rb).then_inc(dma["smc"].h, 16)
        e.dma_start(out=c_selr[:], in_=selr).then_inc(dma["smc"].h, 16)
        e.dma_start(out=c_rmsw[:], in_=rmsw).then_inc(dma["smc"].h, 16)
        e.dma_start(out=c_misc[:], in_=cmisc).then_inc(dma["smc"].h, 16)
    S_(_c0)
    dma["smc"].n += 9 * 16
    SL_CONST = dma["smc"].n

    def _c1(e):
        e.dma_start(out=c_id_bf[:], in_=ident).then_inc(dma["gmc"].h, 16)
        e.dma_start(out=c_emask[:], in_=emask).then_inc(dma["gmc"].h, 16)
        e.dma_start(out=c_maskt[:], in_=maskt).then_inc(dma["gmc"].h, 16)
        e.dma_start(out=rwsb[:].rearrange("p (t c) -> p t c", c=16),
                        in_=rw.rearrange("(t p) c -> p t c", p=128)).then_inc(dma["gmc"].h, 16)
        e.dma_start(out=WBUF[:].rearrange("p (t c) -> p t c", c=2048),
                        in_=ew.rearrange("(t p) c -> p t c", p=128)).then_inc(dma["gmc"].h, 16)
    G_(_c1)
    dma["gmc"].n += 5 * 16
    GL_CONST = dma["gmc"].n

    def ln_stats(xtile, waits, par):
        b = par * 16

        def f(e, xtile=xtile, waits=list(waits) + [gget(("stats", par))], b=b):
            for g in waits:
                need(e, g)
            e.activation(out=jnk_bf, in_=xtile[:], func=AF.Identity,
                         accum_out=stat[:, b + 0:b + 1])
            e.activation(out=jnk_bf2, in_=xtile[:], func=AF.Square,
                         accum_out=stat[:, b + 1:b + 2])
            e.copy(out=stat[:, b + 10:b + 11],
                   in_=stat[:, b + 1:b + 2]).then_inc(sml.h, 1)
        C_(f)
        sml.n += 1
        return sml.n

    def ln_normalize(xtile, s_stats, par):
        b = par * 16

        def fA(e, s_stats=s_stats, b=b, g=gget(("statA", par))):
            e.wait_ge(sml.h, s_stats)
            need(e, g)
            e.activation(out=stat[:, b + 2:b + 3], in_=stat[:, b + 0:b + 1],
                         func=AF.Identity, scale=1.0 / DM)
            e.activation(out=stat[:, b + 3:b + 4], in_=stat[:, b + 1:b + 2],
                         func=AF.Identity, scale=1.0 / DM).then_inc(sml.h, 1)
        C_(fA)
        sml.n += 1
        s_A = sml.n

        def fB(e, s_A=s_A, b=b, g=gget(("statB", par))):
            e.wait_ge(sml.h, s_A)
            need(e, g)
            e.tensor_scalar(out=stat[:, b + 4:b + 5], in0=stat[:, b + 2:b + 3],
                            scalar1=stat[:, b + 2:b + 3],
                            scalar2=stat[:, b + 3:b + 4],
                            op0=ALU.mult, op1=ALU.subtract).then_inc(vml.h, 1)
        V_(fB)
        vml.n += 1
        v_B = vml.n
        guard[("stats", par)] = (vml.h, v_B)

        def fC(e, v_B=v_B, b=b):
            e.wait_ge(vml.h, v_B)
            e.activation(out=stat[:, b + 5:b + 6], in_=stat[:, b + 4:b + 5],
                         func=AF.Ln, scale=-1.0,
                         bias=c_misc[:, 0:1]).then_inc(sml.h, 1)
        C_(fC)
        sml.n += 1
        s_C = sml.n

        def fD(e, s_C=s_C, b=b):
            e.wait_ge(sml.h, s_C)
            e.tensor_copy(out=stat[:, b + 7:b + 8],
                          in_=stat[:, b + 5:b + 6]).then_inc(vml.h, 1)
        V_(fD)
        vml.n += 1
        v_D = vml.n

        def fE(e, v_D=v_D, b=b):
            e.wait_ge(vml.h, v_D)
            e.activation(out=stat[:, b + 6:b + 7], in_=stat[:, b + 7:b + 8],
                         func=AF.Exp, scale=-0.5).then_inc(sml.h, 1)
        C_(fE)
        sml.n += 1
        s_E = sml.n
        guard[("statB", par)] = (sml.h, s_E)

        def fF(e, xtile=xtile, s_E=s_E, b=b, g=gget("xn")):
            e.wait_ge(sml.h, s_E)
            need(e, g)
            e.tensor_scalar(out=xn[:], in0=xtile[:], scalar1=stat[:, b + 2:b + 3],
                            scalar2=stat[:, b + 6:b + 7], op0=ALU.subtract,
                            op1=ALU.mult).then_inc(vml.h, 1)
        V_(fF)
        vml.n += 1
        guard[("statA", par)] = (vml.h, vml.n)
        return vml.n

    def transpose_to_big(i, v_xn):
        for bq in range(4):
            half = state_tr[0] % 2
            state_tr[0] += 1

            def ft(e, bq=bq, half=half, v_xn=v_xn, g=gget(("pTR", half))):
                e.wait_ge(vml.h, v_xn)
                need(e, g)
                for j in range(4):
                    dt = bq * 4 + j
                    ins = e.transpose(pTR[:, half * 1024 + j * 128:half * 1024 + (j + 1) * 128],
                                      xn[:, dt * 128:(dt + 1) * 128], c_id_bf[:])
                ins.then_inc(tmm.h, 1)
            T_(ft)
            tmm.n += 1
            t_tr = tmm.n

            def fe(e, bq=bq, half=half, t_tr=t_tr, i=i):
                e.wait_ge(tmm.h, t_tr)
                out_ap = BIG[:, :].rearrange("p (t q) -> p t q", q=SEQ)[
                    :, bq * 4:(bq + 1) * 4, i * 128:(i + 1) * 128]
                e.activation(out=out_ap, in_=pTR[:, half * 1024:half * 1024 + 512],
                             func=AF.Copy).then_inc(sml.h, 1)
            C_(fe)
            sml.n += 1
            guard[("pTR", half)] = (sml.h, sml.n)
        guard["xn"] = (tmm.h, tmm.n)
        return sml.n

    def barrier():
        t1, v1, s1, p1 = tmm.n, vml.n, sml.n, pml.n
        dsnap = [(d.h, d.n) for d in dma.values() if d.n > 0]

        def wt(e, t1=t1, v1=v1, s1=s1, p1=p1, dsnap=list(dsnap)):
            e.wait_ge(tmm.h, t1)
            e.wait_ge(vml.h, v1)
            e.wait_ge(sml.h, s1)
            if p1 > 0:
                e.wait_ge(pml.h, p1)
            for h, n in dsnap:
                e.wait_ge(h, n)
        for EM in (T_, V_, C_, G_, S_):
            EM(wt)
        guard.clear()

    def ffn_phase(call, src_res, dst_res, fuse_next_ln, big_tile_s, xt_free):
        cin, cout = cc_in[call], cc_out[call]

        geb0 = dma["geb"].n

        def feb(e, call=call):
            e.dma_start(out=c_ebbf, in_=eb).then_inc(dma["geb"].h, 16)
            if call == 1:
                for dt in range(ND):
                    e.dma_start(out=WBUF[:, dt * DM:(dt + 1) * DM],
                                in_=ew[dt * 128:(dt + 1) * 128, :]
                                ).then_inc(dma["geb"].h, 16)
        G_(feb)
        dma["geb"].n += 16 * (1 + (ND if call == 1 else 0))
        g_eb = dma["geb"].n

        st_prev = {}
        for i in range(NT):
            pR = pA if i % 2 == 0 else pB
            par = i % 2
            sb = par * 16

            def fr(e, i=i, pR=pR, bs=big_tile_s[i], g=gget(id(pR))):
                e.wait_ge(sml.h, bs)
                if i == 0:
                    e.wait_ge(dma["gmc"].h, GL_CONST)
                    e.wait_ge(dma["smc"].h, SL_CONST)
                need(e, g)
                for dt in range(ND):
                    e.matmul(pR[:, 0:16],
                             BIG[:, dt * SEQ + i * 128:dt * SEQ + (i + 1) * 128],
                             rwsb[:, dt * 16:(dt + 1) * 16],
                             start=(dt == 0), stop=False)
                e.matmul(pR[:, 0:16], c_ones[:], c_rb[:], start=False,
                         stop=True).then_inc(tmm.h, 1)
            T_(fr)
            tmm.n += 1
            t_lg = tmm.n

            def fx(e, pR=pR, t_lg=t_lg, sb=sb, par=par,
                   g=[gget(("rtacc", par))]):
                e.wait_ge(tmm.h, t_lg)
                need(e, g)
                e.activation(out=sown[:, sb:sb + 16], in_=pR[:, 0:16], func=AF.Exp,
                             accum_out=stat[:, 32 + par:33 + par])
                e.copy(out=stat[:, 40 + par:41 + par],
                       in_=stat[:, 32 + par:33 + par]).then_inc(sml.h, 1)
            C_(fx)
            sml.n += 1
            s_exp = sml.n
            guard[id(pR)] = (sml.h, sml.n)

            def fn(e, s_exp=s_exp, par=par):
                e.wait_ge(sml.h, s_exp)
                e.reciprocal(out=stat[:, 34 + par:35 + par],
                             in_=stat[:, 32 + par:33 + par]).then_inc(vml.h, 1)
            V_(fn)
            vml.n += 1
            v_sm = vml.n
            guard[("rtacc", par)] = (vml.h, v_sm)

            def fnm(e, v_sm=v_sm, sb=sb, par=par, g=gget(("sownn", par))):
                e.wait_ge(vml.h, v_sm)
                need(e, g)
                e.activation(out=sown[:, 32 + sb:32 + sb + 16],
                             in_=sown[:, sb:sb + 16], func=AF.Identity,
                             scale=stat[:, 34 + par:35 + par]).then_inc(sml.h, 1)
            C_(fnm)
            sml.n += 1
            s_nm = sml.n

            def fst(e, par=par, sb=sb, s_nm=s_nm, g=gget("bankS")):
                e.wait_ge(sml.h, s_nm)
                need(e, g)
                e.transpose(pS_st[par], sown[:, 32 + sb:32 + sb + 16],
                            c_id_f[:]).then_inc(tmm.h, 1)
            T_(fst)
            tmm.n += 1
            t_st = tmm.n

            def fse(e, i=i, par=par, t_st=t_st):
                e.wait_ge(tmm.h, t_st)
                e.activation(out=ST[0:16, i * 128:(i + 1) * 128], in_=pS_st[par],
                             func=AF.Copy).then_inc(sml.h, 1)
            C_(fse)
            sml.n += 1
            guard[("sownn", par)] = (tmm.h, t_st)
            guard["bankS"] = (sml.h, sml.n)
        st_all = sml.n

        def fcc(e, st_all=st_all, cin=cin, cout=cout, gs0=dma["gst"].n, cc0=ccs.n):
            e.wait_ge(sml.h, st_all)
            e.dma_start(out=cin[:, :], in_=ST[0:16, :]).then_inc(dma["gst"].h, 16)
            e.wait_ge(dma["gst"].h, gs0 + 16)
            e.collective_compute(
                "AllGather", ALU.bypass,
                replica_groups=[list(range(NCORES))],
                ins=[cin.ap()], outs=[cout.ap()],
            ).then_inc(ccs.h, 1)
            e.wait_ge(ccs.h, cc0 + 1)
            e.dma_start(out=sall[:], in_=cout[:, :]).then_inc(dma["gsall"].h, 16)
        G_(fcc)
        dma["gst"].n += 16
        ccs.n += 1
        dma["gsall"].n += 16
        g_sall = dma["gsall"].n

        def fs0(e, g_sall=g_sall):
            e.wait_ge(dma["gsall"].h, g_sall)
            e.memset(srch[:, :], 0.0)
            e.memset(srch[:, 2:3], 1.0)
            e.memset(srch[:, 4:5], 0.5).then_inc(vml.h, 1)
        V_(fs0)
        vml.n += 1

        for it in range(SEARCH_ITERS):
            par = it % 2
            nxt = 1 - par
            v_mid = vml.n

            def fmb(e, v_mid=v_mid, par=par):
                e.wait_ge(vml.h, v_mid)
                e.copy(out=srch[:, 12:13], in_=srch[:, 4 + par:5 + par]).then_inc(sml.h, 1)
            C_(fmb)
            sml.n += 1
            s_mid = sml.n

            def fsa(e, s_mid=s_mid):
                e.wait_ge(sml.h, s_mid)
                e.tensor_scalar(out=e0[:], in0=sall[:], scalar1=srch[:, 12:13],
                                scalar2=0.0, op0=ALU.is_ge, op1=ALU.add,
                                accum_out=srch[:, 13:14])
                e.tensor_copy(out=srch[:, 14:15], in_=srch[:, 13:14]).then_inc(vml.h, 1)
            V_(fsa)
            vml.n += 1
            v_cnt = vml.n

            def fsb(e, v_cnt=v_cnt, g=gget("bankS")):
                e.wait_ge(vml.h, v_cnt)
                need(e, g)
                e.matmul(pS_cnt, c_mgrp[:], srch[:, 13:14], start=True,
                         stop=True).then_inc(tmm.h, 1)
            T_(fsb)
            tmm.n += 1
            t_cnt = tmm.n

            def fsA(e, t_cnt=t_cnt, par=par):
                e.wait_ge(tmm.h, t_cnt)
                e.tensor_scalar(out=srch[:, 8:9], in0=pS_cnt, scalar1=float(TOPK),
                                scalar2=None, op0=ALU.is_ge)
                e.tensor_tensor(out=srch[:, 6:7], in0=srch[:, 4 + par:5 + par],
                                in1=srch[:, 0 + par:1 + par], op=ALU.subtract)
                e.tensor_tensor(out=srch[:, 7:8], in0=srch[:, 2 + par:3 + par],
                                in1=srch[:, 4 + par:5 + par],
                                op=ALU.subtract).then_inc(vml.h, 1)
            V_(fsA)
            vml.n += 1
            v_A = vml.n
            guard["bankS"] = (vml.h, v_A)

            def fsB(e, v_A=v_A):
                e.wait_ge(vml.h, v_A)
                e.copy(out=srch[:, 9:10], in_=srch[:, 8:9])
                e.copy(out=srch[:, 10:11], in_=srch[:, 6:7])
                e.copy(out=srch[:, 11:12], in_=srch[:, 7:8]).then_inc(sml.h, 1)
            C_(fsB)
            sml.n += 1
            s_B = sml.n

            def fsC(e, s_B=s_B, par=par, nxt=nxt):
                e.wait_ge(sml.h, s_B)
                e.scalar_tensor_tensor(out=srch[:, 0 + nxt:1 + nxt],
                                       in0=srch[:, 10:11], scalar=srch[:, 9:10],
                                       in1=srch[:, 0 + par:1 + par],
                                       op0=ALU.mult, op1=ALU.add)
                e.scalar_tensor_tensor(out=srch[:, 2 + nxt:3 + nxt],
                                       in0=srch[:, 11:12], scalar=srch[:, 9:10],
                                       in1=srch[:, 4 + par:5 + par],
                                       op0=ALU.mult, op1=ALU.add).then_inc(vml.h, 1)
            V_(fsC)
            vml.n += 1
            v_C = vml.n

            def fsD(e, v_C=v_C, nxt=nxt):
                e.wait_ge(vml.h, v_C)
                e.activation(out=srch[:, 10:11], in_=srch[:, 0 + nxt:1 + nxt],
                             func=AF.Identity, scale=0.5)
                e.activation(out=srch[:, 11:12], in_=srch[:, 2 + nxt:3 + nxt],
                             func=AF.Identity, scale=0.5).then_inc(sml.h, 1)
            C_(fsD)
            sml.n += 1
            s_D = sml.n

            def fsE(e, s_D=s_D, nxt=nxt):
                e.wait_ge(sml.h, s_D)
                e.tensor_tensor(out=srch[:, 4 + nxt:5 + nxt], in0=srch[:, 10:11],
                                in1=srch[:, 11:12], op=ALU.add).then_inc(vml.h, 1)
            V_(fsE)
            vml.n += 1
        LOF = SEARCH_ITERS % 2

        v_lo = vml.n

        def flb(e, v_lo=v_lo, LOF=LOF):
            e.wait_ge(vml.h, v_lo)
            e.copy(out=srch[:, 15:16], in_=srch[:, LOF:LOF + 1]).then_inc(sml.h, 1)
        C_(flb)
        sml.n += 1
        s_lo = sml.n

        def fsn1(e, s_lo=s_lo):
            e.wait_ge(sml.h, s_lo)
            e.scalar_tensor_tensor(out=e0[:], in0=sall[:], scalar=srch[:, 15:16],
                                   in1=sall[:], op0=ALU.is_ge,
                                   op1=ALU.bypass).then_inc(vml.h, 1)
        V_(fsn1)
        vml.n += 1
        v_n1 = vml.n

        def fsn2(e, v_n1=v_n1):
            e.wait_ge(vml.h, v_n1)
            e.activation(out=xn[:], in_=e0[:], func=AF.Identity, scale=-2.0,
                         bias=c_misc[:, 3:4]).then_inc(sml.h, 1)
        C_(fsn2)
        sml.n += 1
        s_n2 = sml.n

        def fsn3(e, s_n2=s_n2):
            e.wait_ge(sml.h, s_n2)
            e.tensor_tensor(out=snapt[:], in0=sall[:], in1=xn[:], op=ALU.add)
            e.tensor_reduce(out=srch[:, 6:7], in_=snapt[:], axis=AX.X,
                            op=ALU.min).then_inc(vml.h, 1)
        V_(fsn3)
        vml.n += 1
        v_cm = vml.n

        def fst2(e, v_cm=v_cm, g=gget("bankS")):
            e.wait_ge(vml.h, v_cm)
            need(e, g)
            e.transpose(pS_r0, srch[:, 6:7], c_id_f[:]).then_inc(tmm.h, 1)
        T_(fst2)
        tmm.n += 1
        t_cm = tmm.n

        def fmx(e, t_cm=t_cm):
            e.wait_ge(tmm.h, t_cm)
            e.tensor_reduce(out=srow[:], in_=pS_r0.rearrange("p (r e) -> p e r", r=8),
                            axis=AX.X, op=ALU.min).then_inc(vml.h, 1)
        V_(fmx)
        vml.n += 1
        guard["bankS"] = (vml.h, vml.n)
        v_th = vml.n

        def fth(e, v_th=v_th, g=gget("bankS")):
            e.wait_ge(vml.h, v_th)
            need(e, g)
            e.transpose(pS_r16, srow[:], c_id_f[0:1, 0:1]).then_inc(tmm.h, 1)
        T_(fth)
        tmm.n += 1
        t_th = tmm.n

        def fthc(e, t_th=t_th):
            e.wait_ge(tmm.h, t_th)
            e.tensor_copy(out=srch[0:16, 7:8], in_=pS_r16).then_inc(vml.h, 1)
        V_(fthc)
        vml.n += 1
        guard["bankS"] = (vml.h, vml.n)
        v_thc = vml.n

        def fthx(e, v_thc=v_thc, g=gget("bankS")):
            e.wait_ge(vml.h, v_thc)
            need(e, g)
            e.matmul(pS_thx, c_repl[:], srch[0:16, 7:8], start=True,
                     stop=True).then_inc(tmm.h, 1)
        T_(fthx)
        tmm.n += 1
        t_thx = tmm.n

        def fw0(e, t_thx=t_thx):
            e.wait_ge(tmm.h, t_thx)
            e.activation(out=srch[:, 8:9], in_=pS_thx, func=AF.Copy).then_inc(sml.h, 1)
        C_(fw0)
        sml.n += 1
        s_th = sml.n
        guard["bankS"] = (sml.h, sml.n)

        def fw1(e, s_th=s_th):
            e.wait_ge(sml.h, s_th)
            e.scalar_tensor_tensor(out=snapt[:], in0=sall[:], scalar=srch[:, 8:9],
                                   in1=sall[:], op0=ALU.is_ge,
                                   op1=ALU.mult).then_inc(vml.h, 1)
        V_(fw1)
        vml.n += 1
        v_msk = vml.n

        def fw2(e, v_msk=v_msk, g=gget("bankS")):
            e.wait_ge(vml.h, v_msk)
            need(e, g)
            for i in range(NT):
                ins = e.matmul(pS_w[:, i:i + 1], snapt[:, i * 128:(i + 1) * 128],
                               c_selr[:], start=True, stop=True)
            ins.then_inc(tmm.h, 1)
        T_(fw2)
        tmm.n += 1
        t_w = tmm.n

        def fw3(e, t_w=t_w):
            e.wait_ge(tmm.h, t_w)
            e.activation(out=wcol[:], in_=pS_w, func=AF.Identity,
                         scale=0.5).then_inc(sml.h, 1)
        C_(fw3)
        sml.n += 1
        guard["bankS"] = (sml.h, sml.n)
        v_w = None
        s_w = sml.n

        y_evac = {}
        apply_v = dict(xt_free)
        store_g = {}
        big_next = [None] * NT
        zt_guard = [gget("zt"), gget("zt_store")]

        def emit_y(i):
            slot = (i % 4) * 2048
            for cch in range(4):
                pY = pA if (i * 4 + cch) % 2 == 0 else pB

                def fy(e, i=i, cch=cch, pY=pY, bs=big_tile_s[i], g=gget(id(pY)),
                       geb=(g_eb if i == 0 and cch == 0 else None)):
                    e.wait_ge(sml.h, bs)
                    if geb is not None:
                        e.wait_ge(dma["geb"].h, geb)
                    need(e, g)
                    for dt in range(ND):
                        e.matmul(pY[:],
                                 BIG[:, dt * SEQ + i * 128:dt * SEQ + (i + 1) * 128],
                                 WBUF[:, dt * DM + cch * 512:dt * DM + (cch + 1) * 512],
                                 start=(dt == 0), stop=False)
                    e.matmul(pY[:], c_onesb, c_ebbf[:, cch * 512:(cch + 1) * 512],
                             start=False, stop=True).then_inc(tmm.h, 1)
                T_(fy)
                tmm.n += 1
                t_y = tmm.n

                def fyev(e, i=i, cch=cch, pY=pY, t_y=t_y, slot=slot,
                         w=apply_v.get(i - 4)):
                    e.wait_ge(tmm.h, t_y)
                    if w is not None:
                        e.wait_ge(vml.h, w)
                    e.activation(out=SCR[:, slot + cch * 512:slot + (cch + 1) * 512],
                                 in_=pY[:], func=AF.Copy).then_inc(sml.h, 1)
                C_(fyev)
                sml.n += 1
                guard[id(pY)] = (sml.h, sml.n)
            y_evac[i] = sml.n

        def emit_apply(i):
            slot = (i % 4) * 2048
            xb = xt[i % 2]

            def fxl(e, i=i, xb=xb, w=apply_v.get(i - 2), sh=SX[i % 2].h):
                if w is not None:
                    e.wait_ge(vml.h, w)
                e.dma_start(out=xb[:], in_=src_res[i * 128:(i + 1) * 128, :]
                            ).then_inc(sh, 16)
            S_(fxl)
            SX[i % 2].n += 16
            sl_x = (SX[i % 2].h, SX[i % 2].n)

            def fap(e, i=i, xb=xb, slot=slot, sl_x=sl_x, sy=max(y_evac[i], s_w),
                    zg=list(zt_guard)):
                e.wait_ge(sml.h, sy)
                e.wait_ge(sl_x[0], sl_x[1])
                for g in zg:
                    need(e, g)
                e.scalar_tensor_tensor(out=zt[:], in0=SCR[:, slot:slot + 2048],
                                       scalar=wcol[:, i:i + 1], in1=xb[:],
                                       op0=ALU.mult, op1=ALU.add).then_inc(vml.h, 1)
            V_(fap)
            vml.n += 1
            v_z = vml.n
            apply_v[i] = v_z

            def fzs(e, i=i, v_z=v_z):
                e.wait_ge(vml.h, v_z)
                e.dma_start(out=dst_res[i * 128:(i + 1) * 128, :], in_=zt[:]
                            ).then_inc(dma["gzs"].h, 16)
            G_(fzs)
            dma["gzs"].n += 16
            g_z = dma["gzs"].n
            store_g[i] = g_z

            if fuse_next_ln:
                s_st = ln_stats(zt, [(vml.h, v_z)], i % 2)
                v_xn = ln_normalize(zt, s_st, i % 2)
                s_big = transpose_to_big(i, v_xn)
                big_next[i] = s_big
                zt_guard[0] = (vml.h, v_xn)
                zt_guard[1] = (dma["gzs"].h, g_z)
            else:
                zt_guard[0] = (dma["gzs"].h, g_z)
                zt_guard[1] = None

        for i in range(NT):
            emit_y(i)
            if i >= 3:
                emit_apply(i - 3)
        for i in range(NT - 3, NT):
            emit_apply(i)
        guard["zt"] = zt_guard[0]
        guard["zt_store"] = zt_guard[1]
        return big_next

    def initial_ln():
        big_tile_s = [None] * NT
        xn_done = {}
        for i in range(NT):
            xb = xt[i % 2]

            def fl(e, i=i, xb=xb, w=xn_done.get(i - 2), sh=SX[i % 2].h):
                if w is not None:
                    e.wait_ge(vml.h, w)
                e.dma_start(out=xb[:], in_=x_in[i * 128:(i + 1) * 128, :]
                            ).then_inc(sh, 16)
            S_(fl)
            SX[i % 2].n += 16
            s_st = ln_stats(xb, [(SX[i % 2].h, SX[i % 2].n)], i % 2)
            if sub < 2:
                xn_done[i] = vml.n
                big_tile_s[i] = sml.n
                continue
            v_xn = ln_normalize(xb, s_st, i % 2)
            xn_done[i] = v_xn
            if sub < 3:
                big_tile_s[i] = sml.n
                continue
            big_tile_s[i] = transpose_to_big(i, v_xn)
        xt_free = {-2: xn_done[NT - 2], -1: xn_done[NT - 1]}
        return big_tile_s, xt_free

    def kq_rope_group(i, pP, colw, dst_col_fn, t_mm, rope_state):
        ngr = colw // 256

        def fr1(e, i=i, ngr=ngr, pP=pP, t_mm=t_mm, gr=rope_state.get("g"),
                w=rope_state.get("v2")):
            e.wait_ge(tmm.h, t_mm)
            e.wait_ge(dma["grope"].h, gr)
            if w is not None:
                e.wait_ge(pml.h, w)
            CF = c_rope_c[:, i * 256:(i + 1) * 256]
            SF3 = c_rope_s[:, i * 256:(i + 1) * 256].rearrange(
                "p (j t) -> p j t", t=2)
            for g2 in range(ngr):
                b = g2 * 256
                pg3 = pP[:, b:b + 256].rearrange("p (j t) -> p j t", t=2)
                q3 = qrt[:, 512 + b:512 + b + 256].rearrange("p (j t) -> p j t", t=2)
                e.tensor_tensor(out=qrt[:, b:b + 256], in0=pP[:, b:b + 256],
                                in1=CF, op=ALU.mult)
                e.tensor_tensor(out=q3[:, :, 0:1], in0=pg3[:, :, 1:2],
                                in1=SF3[:, :, 0:1], op=ALU.mult)
                ins = e.tensor_tensor(out=q3[:, :, 1:2], in0=pg3[:, :, 0:1],
                                      in1=SF3[:, :, 1:2], op=ALU.mult)
            ins.then_inc(vml.h, 1)
        V_(fr1)
        vml.n += 1
        v_r1 = vml.n

        def fr2(e, colw=colw, v_r1=v_r1, w=rope_state.get("t")):
            e.wait_ge(vml.h, v_r1)
            if w is not None:
                e.wait_ge(tmm.h, w)
            e.tensor_tensor(out=qro[:, 0:colw], in0=qrt[:, 0:colw],
                            in1=qrt[:, 512:512 + colw],
                            op=ALU.add).then_inc(pml.h, 1)
        G_(fr2)
        pml.n += 1
        p_r2 = pml.n
        rope_state["v2"] = p_r2

        ntr = colw // 128
        half = state_tr[0] % 2
        state_tr[0] += 1

        def ftr(e, ntr=ntr, half=half, p_r2=p_r2, g=gget(("pTR", half))):
            e.wait_ge(pml.h, p_r2)
            need(e, g)
            for j in range(ntr):
                ins = e.transpose(pTR[:, half * 1024 + j * 128:half * 1024 + (j + 1) * 128],
                                  qro[:, j * 128:(j + 1) * 128], c_id_bf[:])
            ins.then_inc(tmm.h, 1)
        T_(ftr)
        tmm.n += 1
        t_tr = tmm.n

        def fte(e, ntr=ntr, half=half, t_tr=t_tr, dst=dst_col_fn):
            e.wait_ge(tmm.h, t_tr)
            for j in range(ntr):
                e.activation(out=dst(j), in_=pTR[:, half * 1024 + j * 128:
                                               half * 1024 + (j + 1) * 128],
                             func=AF.Copy).then_inc(sml.h, 1)
        C_(fte)
        sml.n += ntr
        guard[("pTR", half)] = (sml.h, sml.n)
        return sml.n, v_r1

    def attention_phase(nheads=H, nqi=NT):
        rope_state = {}
        state_u = {}
        state_d = {}
        state_dr = {}
        state_att = {}
        state_att2 = {}
        state_pe = [0]
        state_uh = [0]

        def flq(e):
            e.dma_start(out=lqk[0], in_=lq1).then_inc(dma["smc"].h, 16)
            e.dma_start(out=lqk[1], in_=lq2).then_inc(dma["smc"].h, 16)
            e.dma_start(out=lqk[2], in_=lk1).then_inc(dma["smc"].h, 16)
            e.dma_start(out=lqk[3], in_=lk2).then_inc(dma["smc"].h, 16)
        S_(flq)
        dma["smc"].n += 64
        sl_lqk = dma["smc"].n

        def fl1(e, sl_lqk=sl_lqk):
            e.wait_ge(dma["smc"].h, sl_lqk)
            e.scalar_tensor_tensor(out=lqk[0], in0=lqk[0], scalar=1.0,
                                   in1=lqk[2], op0=ALU.mult, op1=ALU.mult,
                                   accum_out=lsc[:, 0:1])
            e.scalar_tensor_tensor(out=lqk[1], in0=lqk[1], scalar=1.0,
                                   in1=lqk[3], op0=ALU.mult, op1=ALU.mult,
                                   accum_out=lsc[:, 1:2])
            e.tensor_copy(out=lsc[:, 6:7], in_=lsc[:, 1:2]).then_inc(vml.h, 1)
        V_(fl1)
        vml.n += 1
        v_l1 = vml.n

        def fl2(e, v_l1=v_l1):
            e.wait_ge(vml.h, v_l1)
            e.activation(out=lsc[:, 2:4], in_=lsc[:, 0:2], func=AF.Exp).then_inc(sml.h, 1)
        C_(fl2)
        sml.n += 1
        s_l2 = sml.n

        def fl3(e, s_l2=s_l2):
            e.wait_ge(sml.h, s_l2)
            e.tensor_tensor(out=lsc[:, 4:5], in0=lsc[:, 2:3], in1=lsc[:, 3:4],
                            op=ALU.subtract).then_inc(vml.h, 1)
        V_(fl3)
        vml.n += 1
        v_l3a = vml.n

        def fl3b(e, v_l3a=v_l3a):
            e.wait_ge(vml.h, v_l3a)
            e.activation(out=lsc[:, 5:6], in_=lsc[:, 4:5], func=AF.Identity,
                         bias=c_misc[0:16, 1:2]).then_inc(sml.h, 1)
        C_(fl3b)
        sml.n += 1
        v_l3 = sml.n

        def fl4(e, v_l3=v_l3):
            e.wait_ge(sml.h, v_l3)
            e.transpose(pS_lr, lsc[:, 5:6], c_id_f[0:16, 0:16]).then_inc(tmm.h, 1)
        T_(fl4)
        tmm.n += 1
        t_l4 = tmm.n

        def fl5(e, t_l4=t_l4):
            e.wait_ge(tmm.h, t_l4)
            e.tensor_copy(out=srow[:], in_=pS_lr).then_inc(vml.h, 1)
        V_(fl5)
        vml.n += 1
        v_l5 = vml.n

        def fl6(e, v_l5=v_l5):
            e.wait_ge(vml.h, v_l5)
            e.matmul(pS_lam, c_ones[:], srow[:], start=True,
                     stop=True).then_inc(tmm.h, 1)
        T_(fl6)
        tmm.n += 1
        t_l6 = tmm.n

        def fl7(e, t_l6=t_l6):
            e.wait_ge(tmm.h, t_l6)
            e.tensor_copy(out=lamc[:], in_=pS_lam).then_inc(vml.h, 1)
        V_(fl7)
        vml.n += 1
        v_lam = vml.n

        def frt(e):
            e.dma_start(out=SCR[:, 0:NT * 256].rearrange("p (t c) -> p t c", c=256),
                        in_=ropec.rearrange("(t p) c -> p t c", p=128)
                        ).then_inc(dma["grope"].h, 16)
            e.dma_start(out=SCR[:, NT * 256:2 * NT * 256].rearrange("p (t c) -> p t c", c=256),
                        in_=ropes.rearrange("(t p) c -> p t c", p=128)
                        ).then_inc(dma["grope"].h, 16)
        G_(frt)
        dma["grope"].n += 32
        rope_state["g"] = dma["grope"].n

        def fwk(e):
            e.dma_start(out=WBUF[:, WSLOT:WSLOT + ND * 1024].rearrange("p (t c) -> p t c", c=1024),
                        in_=wk.rearrange("(t p) c -> p t c", p=128)
                        ).then_inc(dma["gwk"].h, 16)
        G_(fwk)
        dma["gwk"].n += 16
        g_wk = dma["gwk"].n

        pend_k = None
        kq_steps = [(i, cch) for i in range(NT) for cch in range(2)] + [None]
        for stp in kq_steps:
            if stp is not None:
                i, cch = stp
                pP = pA if (i * 2 + cch) % 2 == 0 else pB

                def fkm(e, i=i, cch=cch, pP=pP, g_wk=g_wk, g=gget(id(pP))):
                    e.wait_ge(dma["gwk"].h, g_wk)
                    need(e, g)
                    for dt in range(ND):
                        ins = e.matmul(pP[:],
                                       BIG[:, dt * SEQ + i * 128:dt * SEQ + (i + 1) * 128],
                                       WBUF[:, WSLOT + dt * 1024 + cch * 512:
                                            WSLOT + dt * 1024 + (cch + 1) * 512],
                                       start=(dt == 0), stop=(dt == ND - 1))
                    ins.then_inc(tmm.h, 1)
                T_(fkm)
                tmm.n += 1
                cur_k = (i, cch, pP, tmm.n)
            else:
                cur_k = None
            if pend_k is None:
                pend_k = cur_k
                continue
            i, cch, pP, t_mm = pend_k
            pend_k = cur_k

            def dstf(j, i=i, cch=cch):
                kc = cch * 4 + j
                return WBUF[:, KT0 + kc * SEQ + i * 128:KT0 + kc * SEQ + (i + 1) * 128]
            s_done, v_ev = kq_rope_group(i, pP, 512, dstf, t_mm, rope_state)
            guard[id(pP)] = (vml.h, v_ev)
        kT_ready = sml.n

        def fwv(e, kT_ready=kT_ready):
            e.wait_ge(sml.h, kT_ready)
            e.dma_start(out=WBUF[:, WSLOT:WSLOT + ND * 512].rearrange("p (t c) -> p t c", c=512),
                        in_=wv.rearrange("(t p) c -> p t c", p=128)
                        ).then_inc(dma["gwv"].h, 16)
        G_(fwv)
        dma["gwv"].n += 16
        g_wv = dma["gwv"].n

        for i in range(NT):
            pP = pA if i % 2 == 0 else pB

            def fvm(e, i=i, pP=pP, g_wv=g_wv, g=gget(id(pP))):
                e.wait_ge(dma["gwv"].h, g_wv)
                need(e, g)
                for dt in range(ND):
                    ins = e.matmul(pP[:], BIG[:, dt * SEQ + i * 128:dt * SEQ + (i + 1) * 128],
                                   WBUF[:, WSLOT + dt * 512:WSLOT + (dt + 1) * 512],
                                   start=(dt == 0), stop=(dt == ND - 1))
                ins.then_inc(tmm.h, 1)
            T_(fvm)
            tmm.n += 1
            t_v = tmm.n

            def fve(e, i=i, pP=pP, t_v=t_v):
                e.wait_ge(tmm.h, t_v)
                e.activation(out=WBUF[:, VS0 + i * 512:VS0 + (i + 1) * 512],
                             in_=pP[:], func=AF.Copy).then_inc(sml.h, 1)
            C_(fve)
            sml.n += 1
            guard[id(pP)] = (sml.h, sml.n)
        v_ready = sml.n

        WQS = [WSLOT, WSLOT + ND * 256]
        wq_user = {}
        wq_g = {}

        def emit_wq(hh):
            def fwq(e, hh=hh, slot=WQS[hh % 2], w=wq_user.get(hh - 2),
                    vr=v_ready):
                if hh < 2:
                    e.wait_ge(sml.h, vr)
                if w is not None:
                    e.wait_ge(tmm.h, w)
                e.dma_start(out=WBUF[:, WQS[hh % 2]:WQS[hh % 2] + ND * 256]
                            .rearrange("p (t c) -> p t c", c=256),
                            in_=wq[:, hh * 256:(hh + 1) * 256].rearrange(
                                "(t p) c -> p t c", p=128)
                            ).then_inc(GWQ[hh % 2].h, 16)
            G_(fwq)
            GWQ[hh % 2].n += 16
            wq_g[hh] = (GWQ[hh % 2].h, GWQ[hh % 2].n)

        for h in range(nheads):
            slot = WQS[h % 2]
            kv = h // 4
            if h < 2:
                emit_wq(h)
            g_wq = wq_g[h]

            pend_q = None
            for i in list(range(NT)) + [None]:
                if i is not None:
                    pP = pA if i % 2 == 0 else pB

                    def fqm(e, i=i, pP=pP, slot=slot, g_wq=g_wq, g=gget(id(pP)),
                            qg=gget("qTh") if i == 0 else None):
                        e.wait_ge(g_wq[0], g_wq[1])
                        need(e, g)
                        need(e, qg)
                        for dt in range(ND):
                            ins = e.matmul(pP[:, 0:256],
                                           BIG[:, dt * SEQ + i * 128:dt * SEQ + (i + 1) * 128],
                                           WBUF[:, slot + dt * 256:slot + (dt + 1) * 256],
                                           start=(dt == 0), stop=(dt == ND - 1))
                        ins.then_inc(tmm.h, 1)
                    T_(fqm)
                    tmm.n += 1
                    cur_q = (i, pP, tmm.n)
                else:
                    cur_q = None
                if pend_q is None:
                    pend_q = cur_q
                    continue
                i_p, pP, t_mm = pend_q
                pend_q = cur_q

                def dstq(j, i=i_p):
                    return qTh[:, j * SEQ + i * 128:j * SEQ + (i + 1) * 128]
                s_done, v_ev = kq_rope_group(i_p, pP, 256, dstq, t_mm, rope_state)
                guard[id(pP)] = (vml.h, v_ev)
            wq_user[h] = tmm.n
            if h + 2 < nheads:
                emit_wq(h + 2)
            q_ready = sml.n

            EB0 = [e0, e0b, e0c, e0d]
            EB1 = [e1, e1b, e1c, e1d]
            for grp in range(nqi // 4):
                qis = [4 * grp + j for j in range(4)]
                nchs = [((qi + 1) * 128 + 511) // 512 for qi in qis]

                for j, qi in enumerate(qis):
                    klen = (qi + 1) * 128
                    for kc in range(nchs[j]):
                        wch = min(512, klen - kc * 512)
                        isdiag = (kc == nchs[j] - 1)
                        half = state_pe[0] % 2
                        state_pe[0] += 1
                        hb = half * 512
                        hE0 = pE0[:, hb:hb + wch]
                        hE1 = pE1[:, hb:hb + wch]

                        def fem(e, qi=qi, kc=kc, wch=wch, isdiag=isdiag,
                                hE0=hE0, hE1=hE1, kv=kv,
                                g0=gget(("pE", half)),
                                kr=(q_ready if (qi == qis[0] and kc == 0)
                                    else None),
                                lr=(v_lam if (h == 0 and qi == 0) else None)):
                            if kr is not None:
                                e.wait_ge(sml.h, kr)
                            if lr is not None:
                                e.wait_ge(vml.h, lr)
                            need(e, g0)
                            for a, hE in ((0, hE0), (1, hE1)):
                                kcg = kv * 2 + a
                                ins = e.matmul(hE,
                                               qTh[:, a * SEQ + qi * 128:
                                                   a * SEQ + (qi + 1) * 128],
                                               WBUF[:, KT0 + kcg * SEQ + kc * 512:
                                                    KT0 + kcg * SEQ + kc * 512 + wch],
                                               start=True, stop=not isdiag)
                                if isdiag:
                                    ins = e.matmul(hE[:, wch - 128:wch], c_id_bf[:],
                                                   c_emask[:], start=False, stop=True)
                            ins.then_inc(tmm.h, 1)
                        T_(fem)
                        tmm.n += 1
                        t_e = tmm.n

                        def fex(e, j=j, kc=kc, wch=wch, hE0=hE0, hE1=hE1, t_e=t_e,
                                g=(gget(("eb", j)) if kc == 0 else None)):
                            e.wait_ge(tmm.h, t_e)
                            need(e, g)
                            e.activation(out=EB0[j][:, kc * 512:kc * 512 + wch],
                                         in_=hE0, func=AF.Exp, scale=CEXP)
                            e.activation(out=EB1[j][:, kc * 512:kc * 512 + wch],
                                         in_=hE1, func=AF.Exp,
                                         scale=CEXP).then_inc(sml.h, 1)
                        C_(fex)
                        sml.n += 1
                        guard[("pE", half)] = (sml.h, sml.n)
                        s_fex = sml.n

                        def fexr(e, j=j, kc=kc, wch=wch, s_fex=s_fex):
                            e.wait_ge(sml.h, s_fex)
                            e.tensor_reduce(out=sstat[:, 32 + 8 * j + kc:
                                                      33 + 8 * j + kc],
                                            in_=EB0[j][:, kc * 512:kc * 512 + wch],
                                            axis=AX.X, op=ALU.add)
                            e.tensor_reduce(out=sstat[:, 36 + 8 * j + kc:
                                                      37 + 8 * j + kc],
                                            in_=EB1[j][:, kc * 512:kc * 512 + wch],
                                            axis=AX.X, op=ALU.add).then_inc(vml.h, 1)
                        V_(fexr)
                        vml.n += 1
                v_stageA = vml.n

                def fg1(e, nchs=nchs, v_stageA=v_stageA):
                    e.wait_ge(vml.h, v_stageA)
                    for j in range(4):
                        nch = nchs[j]
                        e.activation(out=srch[:, 12:12 + nch],
                                     in_=sstat[:, 32 + 8 * j:32 + 8 * j + nch],
                                     func=AF.Identity,
                                     accum_out=srch[:, 0 + j:1 + j])
                        e.activation(out=srch[:, 12:12 + nch],
                                     in_=sstat[:, 32 + 8 * j:32 + 8 * j + nch],
                                     func=AF.Identity, scale=-1.0,
                                     accum_out=srch[:, 4 + j:5 + j])
                        e.activation(out=srch[:, 12:12 + nch],
                                     in_=sstat[:, 36 + 8 * j:36 + 8 * j + nch],
                                     func=AF.Identity,
                                     accum_out=srch[:, 8 + j:9 + j])
                    e.copy(out=srch[:, 12:13], in_=srch[:, 0:1]).then_inc(sml.h, 1)
                C_(fg1)
                sml.n += 1
                s_g1 = sml.n

                def fg2(e, s_g1=s_g1):
                    e.wait_ge(sml.h, s_g1)
                    e.reciprocal(out=srch[:, 16:20], in_=srch[:, 4:8])
                    e.reciprocal(out=srch[:, 20:24],
                                 in_=srch[:, 8:12]).then_inc(vml.h, 1)
                V_(fg2)
                vml.n += 1
                v_g2 = vml.n

                def fg3(e, v_g2=v_g2):
                    e.wait_ge(vml.h, v_g2)
                    e.copy(out=srch[:, 24:28], in_=srch[:, 20:24]).then_inc(sml.h, 1)
                C_(fg3)
                sml.n += 1
                s_g3 = sml.n

                def fg4(e, s_g3=s_g3, h=h):
                    e.wait_ge(sml.h, s_g3)
                    for j in range(4):
                        ins = e.tensor_scalar(out=wcol[:, j:j + 1],
                                              in0=lamc[:, h:h + 1],
                                              scalar1=srch[:, 0 + j:1 + j],
                                              scalar2=srch[:, 24 + j:25 + j],
                                              op0=ALU.mult, op1=ALU.mult)
                    ins.then_inc(vml.h, 1)
                V_(fg4)
                vml.n += 1
                v_g4 = vml.n

                def fg5(e, v_g4=v_g4):
                    e.wait_ge(vml.h, v_g4)
                    e.copy(out=wcol[:, 4:8], in_=wcol[:, 0:4]).then_inc(sml.h, 1)
                C_(fg5)
                sml.n += 1
                s_gb = sml.n

                pend_pe = []
                steps = []
                for j, qi in enumerate(qis):
                    klen = (qi + 1) * 128
                    for kc in range(nchs[j]):
                        steps.append((j, qi, kc, min(512, klen - kc * 512),
                                      kc == nchs[j] - 1))
                pend = None
                for st in steps + [None]:
                    if st is not None:
                        j, qi, kc, wch, isdiag = st
                        half_u = state_uh[0] % 2
                        state_uh[0] += 1
                        ub = half_u * 512

                        def fcb(e, j=j, kc=kc, wch=wch, ub=ub, s_gb=s_gb,
                                w=state_u.get(half_u)):
                            e.wait_ge(sml.h, max(s_gb, w if w is not None else 0))
                            e.scalar_tensor_tensor(out=uneg[:, ub:ub + wch],
                                                   in0=EB1[j][:, kc * 512:
                                                              kc * 512 + wch],
                                                   scalar=wcol[:, 4 + j:5 + j],
                                                   in1=EB0[j][:, kc * 512:
                                                              kc * 512 + wch],
                                                   op0=ALU.mult,
                                                   op1=ALU.subtract
                                                   ).then_inc(vml.h, 1)
                        V_(fcb)
                        vml.n += 1
                        if kc == nchs[j] - 1:
                            guard[("eb", j)] = (vml.h, vml.n)
                        cur = (st, half_u, ub, vml.n)
                    else:
                        cur = None
                    if pend is None:
                        pend = cur
                        continue
                    (j, qi, kc, wch, isdiag), half_u, ub, v_u = pend
                    pend = cur

                    def fed(e, j=j, kc=kc, wch=wch, ub=ub, v_u=v_u,
                            w=state_d.get(half_u), w2=state_dr.get(half_u)):
                        e.wait_ge(vml.h, v_u)
                        if w is not None:
                            e.wait_ge(tmm.h, w)
                        if w2 is not None:
                            e.wait_ge(vml.h, w2)
                        e.activation(out=dchk[:, ub:ub + wch],
                                     in_=uneg[:, ub:ub + wch],
                                     func=AF.Exp, scale=srch[:, 16 + j:17 + j]
                                     ).then_inc(sml.h, 1)
                    C_(fed)
                    sml.n += 1
                    state_u[half_u] = sml.n
                    s_d = sml.n

                    def fedr(e, j=j, kc=kc, wch=wch, ub=ub, s_d=s_d):
                        e.wait_ge(sml.h, s_d)
                        e.tensor_reduce(out=sstat[:, 64 + 8 * j + kc:
                                                  65 + 8 * j + kc],
                                        in_=dchk[:, ub:ub + wch],
                                        axis=AX.X, op=ALU.add).then_inc(vml.h, 1)
                    V_(fedr)
                    vml.n += 1
                    state_dr[half_u] = vml.n

                    ntr = wch // 128
                    half = state_tr[0] % 2
                    state_tr[0] += 1

                    def ftd(e, ntr=ntr, ub=ub, half=half, s_d=s_d,
                            g=gget(("pTR", half))):
                        e.wait_ge(sml.h, s_d)
                        need(e, g)
                        for jt in range(ntr):
                            ins = e.transpose(pTR[:, half * 1024 + jt * 128:
                                                  half * 1024 + (jt + 1) * 128],
                                              dchk[:, ub + jt * 128:
                                                   ub + (jt + 1) * 128],
                                              c_id_bf[:])
                        ins.then_inc(tmm.h, 1)
                    T_(ftd)
                    tmm.n += 1
                    t_d = tmm.n
                    state_d[half_u] = t_d

                    ncopy = ntr - 1 if isdiag else ntr
                    if ncopy > 0:
                        def fde(e, kc=kc, ncopy=ncopy, half=half, t_d=t_d):
                            e.wait_ge(tmm.h, t_d)
                            e.tensor_copy(out=xn[:, kc * 512:
                                                 kc * 512 + ncopy * 128],
                                          in_=pTR[:, half * 1024:
                                                   half * 1024 + ncopy * 128]
                                          ).then_inc(vml.h, 1)
                        V_(fde)
                        vml.n += 1
                    if isdiag:
                        def fdm(e, ntr=ntr, half=half, t_d=t_d, qi=qi):
                            e.wait_ge(tmm.h, t_d)
                            e.tensor_tensor(out=xn[:, qi * 128:(qi + 1) * 128],
                                            in0=pTR[:, half * 1024 +
                                                    (ntr - 1) * 128:
                                                    half * 1024 + ntr * 128],
                                            in1=c_maskt[:],
                                            op=ALU.mult).then_inc(vml.h, 1)
                        V_(fdm)
                        vml.n += 1
                    guard[("pTR", half)] = (vml.h, vml.n)
                    if not isdiag:
                        continue
                    s_dT_s, s_dT_v = sml.n, vml.n
                    pV = pA if qi % 2 == 0 else pB

                    def fpv(e, qi=qi, kv=kv, pV=pV, s_dT_s=s_dT_s, s_dT_v=s_dT_v,
                            g=gget(id(pV)),
                            vr=(v_ready if (h == 0 and qi == 0) else None)):
                        e.wait_ge(sml.h, s_dT_s)
                        e.wait_ge(vml.h, s_dT_v)
                        if vr is not None:
                            e.wait_ge(sml.h, vr)
                        need(e, g)
                        for kt in range(qi + 1):
                            ins = e.matmul(pV[:, 0:128],
                                           xn[:, kt * 128:(kt + 1) * 128],
                                           WBUF[:, VS0 + kt * 512 + kv * 128:
                                                VS0 + kt * 512 + (kv + 1) * 128],
                                           start=(kt == 0), stop=(kt == qi))
                        ins.then_inc(tmm.h, 1)
                    T_(fpv)
                    tmm.n += 1
                    pend_pe.append((qi, pV, tmm.n))

                    if len(pend_pe) > 1:
                        qi_p, pV_p, t_pv_p = pend_pe.pop(0)

                        def fpe(e, qi=qi_p, pV=pV_p, t_pv=t_pv_p,
                                g=gget(("attA", qi_p))):
                            e.wait_ge(tmm.h, t_pv)
                            need(e, g)
                            e.activation(out=attsc[:, 256:384], in_=pV[:, 0:128],
                                         func=AF.Square,
                                         accum_out=sstat[:, qi:qi + 1])
                            e.activation(out=snapt[:, qi * 128:(qi + 1) * 128],
                                         in_=pV[:, 0:128],
                                         func=AF.Copy).then_inc(sml.h, 1)
                        C_(fpe)
                        sml.n += 1
                        guard[id(pV_p)] = (sml.h, sml.n)

                while pend_pe:
                    qi_p, pV_p, t_pv_p = pend_pe.pop(0)

                    def fpe(e, qi=qi_p, pV=pV_p, t_pv=t_pv_p,
                            g=gget(("attA", qi_p))):
                        e.wait_ge(tmm.h, t_pv)
                        need(e, g)
                        e.activation(out=attsc[:, 256:384], in_=pV[:, 0:128],
                                     func=AF.Square,
                                     accum_out=sstat[:, qi:qi + 1])
                        e.activation(out=snapt[:, qi * 128:(qi + 1) * 128],
                                     in_=pV[:, 0:128],
                                     func=AF.Copy).then_inc(sml.h, 1)
                    C_(fpe)
                    sml.n += 1
                    guard[id(pV_p)] = (sml.h, sml.n)
                s_pe_last = sml.n

                def fsd(e, qis=qis, nchs=nchs, s_pe_last=s_pe_last):
                    e.wait_ge(sml.h, s_pe_last)
                    for j, qi in enumerate(qis):
                        nch = nchs[j]
                        if nch > 1:
                            ins = e.tensor_reduce(out=sstat[:, 16 + qi:17 + qi],
                                                  in_=sstat[:, 64 + 8 * j:
                                                            64 + 8 * j + nch],
                                                  axis=AX.X, op=ALU.add)
                        else:
                            ins = e.tensor_copy(out=sstat[:, 16 + qi:17 + qi],
                                                in_=sstat[:, 64 + 8 * j:
                                                          65 + 8 * j])
                    ins.then_inc(vml.h, 1)
                V_(fsd)
                vml.n += 1
                guard["sacc"] = (vml.h, vml.n)
            v_head = vml.n
            s_head = sml.n

            if "pv" in KSKIP or "comb" in KSKIP or "head" in KSKIP:
                continue
            def fscA(e, v_head=v_head, s_head=s_head):
                e.wait_ge(vml.h, v_head)
                e.wait_ge(sml.h, s_head)
                e.copy(out=sstat[:, 32:48], in_=sstat[:, 16:32]).then_inc(sml.h, 1)
            C_(fscA)
            sml.n += 1
            sA = sml.n

            def fscB(e, sA=sA):
                e.wait_ge(sml.h, sA)
                e.tensor_scalar(out=sstat[:, 16:32], in0=sstat[:, 32:48],
                                scalar1=c_mcnt[:], scalar2=None,
                                op0=ALU.subtract).then_inc(vml.h, 1)
            V_(fscB)
            vml.n += 1
            vB = vml.n

            def fscC(e, vB=vB):
                e.wait_ge(vml.h, vB)
                e.copy(out=sstat[:, 32:48], in_=sstat[:, 16:32]).then_inc(sml.h, 1)
            C_(fscC)
            sml.n += 1
            sC = sml.n

            def fscD(e, sC=sC):
                e.wait_ge(sml.h, sC)
                e.tensor_tensor(out=sstat[:, 48:64], in0=sstat[:, 32:48],
                                in1=sstat[:, 32:48], op=ALU.mult).then_inc(vml.h, 1)
            V_(fscD)
            vml.n += 1
            vD = vml.n

            def fscE(e, vD=vD):
                e.wait_ge(vml.h, vD)
                e.activation(out=sstat[:, 32:48], in_=sstat[:, 48:64],
                             func=AF.Identity, scale=128.0e-8).then_inc(sml.h, 1)
            C_(fscE)
            sml.n += 1
            sE = sml.n

            def fscF(e, sE=sE):
                e.wait_ge(sml.h, sE)
                e.tensor_tensor(out=sstat[:, 48:64], in0=sstat[:, 32:48],
                                in1=sstat[:, 0:16], op=ALU.add).then_inc(vml.h, 1)
            V_(fscF)
            vml.n += 1
            vF = vml.n

            def fscG(e, vF=vF):
                e.wait_ge(vml.h, vF)
                e.activation(out=sstat[:, 32:48], in_=sstat[:, 48:64],
                             func=AF.Ln).then_inc(sml.h, 1)
            C_(fscG)
            sml.n += 1
            sG = sml.n

            def fscH(e, sG=sG):
                e.wait_ge(sml.h, sG)
                e.tensor_copy(out=sstat[:, 48:64],
                              in_=sstat[:, 32:48]).then_inc(vml.h, 1)
            V_(fscH)
            vml.n += 1
            vH = vml.n

            def fscI(e, vH=vH):
                e.wait_ge(vml.h, vH)
                e.activation(out=sstat[:, 0:16], in_=sstat[:, 48:64], func=AF.Exp,
                             scale=-0.5, bias=c_misc[:, 2:3]).then_inc(sml.h, 1)
            C_(fscI)
            sml.n += 1
            v_sc3 = sml.n
            guard["sstat_e"] = (sml.h, v_sc3)

            pend_ae = None
            for qi in (list(range(nqi)) + [None] if "evac" not in KSKIP
                       and "pv" not in KSKIP and "comb" not in KSKIP else []):
                if qi is not None:
                    sl = (qi % 2) * 128
                    so = 384 + (qi % 2) * 128

                    def fas(e, qi=qi, sl=sl, v_sc3=v_sc3,
                            w=state_att.get(qi % 2)):
                        e.wait_ge(sml.h, v_sc3)
                        if w is not None:
                            e.wait_ge(tmm.h, w)
                        e.tensor_scalar(out=attsc[:, sl:sl + 128],
                                        in0=snapt[:, qi * 128:(qi + 1) * 128],
                                        scalar1=sstat[:, 0 + qi:1 + qi],
                                        scalar2=None,
                                        op0=ALU.mult).then_inc(vml.h, 1)
                    V_(fas)
                    vml.n += 1
                    v_as = vml.n
                    guard[("attA", qi)] = (vml.h, v_as)

                    half = state_tr[0] % 2
                    state_tr[0] += 1

                    def fat(e, sl=sl, half=half, v_as=v_as,
                            g=gget(("pTR", half))):
                        e.wait_ge(vml.h, v_as)
                        need(e, g)
                        e.transpose(pTR[:, half * 1024:half * 1024 + 128],
                                    attsc[:, sl:sl + 128],
                                    c_id_bf[:]).then_inc(tmm.h, 1)
                    T_(fat)
                    tmm.n += 1
                    state_att[qi % 2] = tmm.n
                    cur_ae = (qi, so, half, tmm.n)
                else:
                    cur_ae = None
                if pend_ae is None:
                    pend_ae = cur_ae
                    continue
                qi_p, so, half, t_at = pend_ae
                pend_ae = cur_ae

                def fae(e, so=so, half=half, t_at=t_at,
                        w=state_att2.get(qi_p % 2)):
                    e.wait_ge(tmm.h, t_at)
                    if w is not None:
                        e.wait_ge(w[0], w[1])
                    e.tensor_copy(out=attsc[:, so:so + 128],
                                  in_=pTR[:, half * 1024:half * 1024 + 128]
                                  ).then_inc(vml.h, 1)
                V_(fae)
                vml.n += 1
                s_ae = vml.n
                guard[("pTR", half)] = (vml.h, s_ae)

                def fad(e, h=h, qi=qi_p, so=so, s_ae=s_ae):
                    e.wait_ge(vml.h, s_ae)
                    e.dma_start(out=attr[h * 128:(h + 1) * 128,
                                         qi * 128:(qi + 1) * 128],
                                in_=attsc[:, so:so + 128]).then_inc(GAT[qi % 2].h, 16)
                G_(fad)
                GAT[qi_p % 2].n += 16
                state_att2[qi_p % 2] = (GAT[qi_p % 2].h, GAT[qi_p % 2].n)
            guard["qTh"] = (tmm.h, tmm.n)
            rope_state["t"] = tmm.n

    def wo_phase():
        g_wo0 = dma["gwk"].n

        def fwo(e):
            for dt in range(ND):
                e.dma_start(out=WBUF[:, dt * DM:(dt + 1) * DM],
                            in_=wo[dt * 128:(dt + 1) * 128, :]
                            ).then_inc(dma["gwk"].h, 16)
        G_(fwo)
        dma["gwk"].n += 16 * ND

        def fws(e, gw=g_wo0 + 16 * ND):
            e.wait_ge(dma["gwk"].h, gw)
            for dt in range(ND):
                ins = e.tensor_scalar(out=WBUF[:, dt * DM:(dt + 1) * DM],
                                      in0=WBUF[:, dt * DM:(dt + 1) * DM],
                                      scalar1=c_rmsw[:], scalar2=None,
                                      op0=ALU.mult)
            ins.then_inc(vml.h, 1)
        V_(fws)
        vml.n += 1
        v_wo = vml.n

        apply_v = {}
        big_tile_s = [None] * NT
        zt_guard = [None, None]
        for i in range(NT):
            sbase = (i % 2) * 2048

            def fsl(e, i=i, sbase=sbase, w=apply_v.get(i - 2)):
                if w is not None:
                    e.wait_ge(vml.h, w)
                e.dma_start(out=SCR[:, sbase:sbase + 2048].rearrange(
                                "p (t c) -> p t c", c=128),
                            in_=attr[:, i * 128:(i + 1) * 128].rearrange(
                                "(t p) c -> p t c", p=128)).then_inc(SSL[i % 2].h, 16)
            S_(fsl)
            SSL[i % 2].n += 16

            xb = xt[i % 2]

            def fx1(e, i=i, xb=xb, w=apply_v.get(i - 2)):
                if w is not None:
                    e.wait_ge(vml.h, w)
                e.dma_start(out=xb[:], in_=x1r[i * 128:(i + 1) * 128, :]
                            ).then_inc(SX[i % 2].h, 16)
            S_(fx1)
            SX[i % 2].n += 16
            sl_i = [(SSL[i % 2].h, SSL[i % 2].n), (SX[i % 2].h, SX[i % 2].n)]

            for cch in range(4):
                pW = pA if (i * 4 + cch) % 2 == 0 else pB

                def fwm(e, i=i, cch=cch, pW=pW, sbase=sbase, sl_i=sl_i,
                        v_wo=v_wo, g=gget(id(pW))):
                    for hh, nn in sl_i:
                        e.wait_ge(hh, nn)
                    if i == 0 and cch == 0:
                        e.wait_ge(vml.h, v_wo)
                    need(e, g)
                    for dt in range(ND):
                        ins = e.matmul(pW[:], SCR[:, sbase + dt * 128:sbase + (dt + 1) * 128],
                                       WBUF[:, dt * DM + cch * 512:dt * DM + (cch + 1) * 512],
                                       start=(dt == 0), stop=(dt == ND - 1))
                    ins.then_inc(tmm.h, 1)
                T_(fwm)
                tmm.n += 1
                t_w = tmm.n

                def fadd(e, cch=cch, pW=pW, xb=xb, t_w=t_w,
                         zg=(list(zt_guard) if cch == 0 else None)):
                    e.wait_ge(tmm.h, t_w)
                    if zg is not None:
                        for g in zg:
                            need(e, g)
                    e.tensor_tensor(out=zt[:, cch * 512:(cch + 1) * 512],
                                    in0=pW[:], in1=xb[:, cch * 512:(cch + 1) * 512],
                                    op=ALU.add).then_inc(vml.h, 1)
                V_(fadd)
                vml.n += 1
                guard[id(pW)] = (vml.h, vml.n)
            v_z = vml.n
            apply_v[i] = v_z

            def fzs(e, i=i, v_z=v_z):
                e.wait_ge(vml.h, v_z)
                e.dma_start(out=x2r[i * 128:(i + 1) * 128, :], in_=zt[:]
                            ).then_inc(dma["gzs"].h, 16)
            G_(fzs)
            dma["gzs"].n += 16
            g_z = dma["gzs"].n

            s_st = ln_stats(zt, [(vml.h, v_z)], i % 2)
            v_xn = ln_normalize(zt, s_st, i % 2)
            big_tile_s[i] = transpose_to_big(i, v_xn)
            zt_guard = [(vml.h, v_xn), (dma["gzs"].h, g_z)]
        guard["zt"] = zt_guard[0]
        guard["zt_store"] = zt_guard[1]
        xt_free = {-2: apply_v[NT - 2], -1: apply_v[NT - 1]}
        return big_tile_s, xt_free

    big_s, xt_free = initial_ln()
    if dbg:
        d_big1 = nc.dram_tensor("d_big1", [128, ND * SEQ], BF16,
                                kind="ExternalOutput").ap()
        _bs = big_s[NT - 1]

        def fdump1(e, _bs=_bs):
            e.wait_ge(sml.h, _bs)
            e.dma_start(out=d_big1, in_=BIG[:]).then_inc(dma["smc"].h, 16)
        S_(fdump1)
        dma["smc"].n += 16
    if stop >= 2:
        ffn_phase(0, x_in, x1r, fuse_next_ln=True, big_tile_s=big_s, xt_free=xt_free)
        barrier()
    if stop >= 3:
        nh_ = H
        nq_ = NT
        if stop == 3:
            if sub == 17:
                nh_, nq_ = 1, 0
            elif sub == 18:
                nh_, nq_ = 1, 1
            elif sub == 19:
                nh_, nq_ = 1, 4
            elif sub == 20:
                nh_, nq_ = 1, 2
            elif sub == 21:
                nh_, nq_ = 1, 3
            elif sub <= 16:
                nh_, nq_ = sub, NT
        attention_phase(nheads=nh_, nqi=nq_)
        barrier()
    if stop >= 4:
        big_s, xt_free = wo_phase()
        barrier()
    if stop >= 5:
        ffn_phase(1, x2r, y_out, fuse_next_ln=False, big_tile_s=big_s, xt_free=xt_free)

    if dbg:
        ddump = {}
        for nm, src in (("d_sall", sall), ("d_srch", srch), ("d_wcol", wcol),
                        ("d_stat", stat), ("d_snapt", snapt), ("d_sown", sown)):
            dt_ = nc.dram_tensor(nm, list(src.shape), F32, kind="ExternalOutput").ap()
            ddump[nm] = (dt_, src)
        d_big = nc.dram_tensor("d_big", [128, ND * SEQ], BF16, kind="ExternalOutput").ap()
        d_ccin = nc.dram_tensor("d_ccin", [16, SEQ], F32, kind="ExternalOutput").ap()
        d_ccout = nc.dram_tensor("d_ccout", [128, SEQ], F32, kind="ExternalOutput").ap()
        t9, v9, s9 = tmm.n, vml.n, sml.n
        dsn = [(d.h, d.n) for d in dma.values() if d.n > 0]

        def fdump(e, t9=t9, v9=v9, s9=s9, dsn=list(dsn)):
            e.wait_ge(tmm.h, t9)
            e.wait_ge(vml.h, v9)
            e.wait_ge(sml.h, s9)
            for h, n in dsn:
                e.wait_ge(h, n)
            for nm, (dt_, src) in ddump.items():
                if nm == "d_srch":
                    e.dma_start(out=dt_[:, 0:12], in_=src[:, 0:12]).then_inc(dma["smc"].h, 16)
                elif nm == "d_stat":
                    e.dma_start(out=dt_[:, 0:8], in_=src[:, 0:8]).then_inc(dma["smc"].h, 16)
                else:
                    e.dma_start(out=dt_, in_=src[:, :]).then_inc(dma["smc"].h, 16)
            e.dma_start(out=d_big, in_=BIG[:]).then_inc(dma["smc"].h, 16)
            e.dma_start(out=d_ccin, in_=cc_in[0].ap()).then_inc(dma["smc"].h, 16)
            e.dma_start(out=d_ccout, in_=cc_out[0].ap()).then_inc(dma["smc"].h, 16)
        S_(fdump)
        dma["smc"].n += 16 * 9

    dsnap_fin = [(d.h, d.n) for d in dma.values() if d.n > 0]

    def ffin(e, dsnap=list(dsnap_fin)):
        for h, n in dsnap:
            e.wait_ge(h, n)
    G_(ffin)

    with nc.Block() as blk:
        @blk.sync
        def _(e):
            for f in prog["sync"]:
                f(e)

        @blk.gpsimd
        def _(e):
            for f in prog["gpsimd"]:
                f(e)

        @blk.tensor
        def _(e):
            for f in prog["tensor"]:
                f(e)

        @blk.vector
        def _(e):
            for f in prog["vector"]:
                f(e)

        @blk.scalar
        def _(e):
            for f in prog["scalar"]:
                f(e)

    ctx.close()
    return nc


_CONSTS = make_consts()


def kernel(L0_attn_x, L0_attn_Wq, L0_attn_Wk, L0_attn_Wv, L0_attn_Wo,
           L0_attn_lq1, L0_attn_lq2, L0_attn_lk1, L0_attn_lk2, L0_attn_rms_w,
           L0_ffn_router_w, L0_ffn_router_b, L0_ffn_expert_w, L0_ffn_expert_b):
    nc = build_program()
    x = np.ascontiguousarray(np.asarray(L0_attn_x, dtype=np.float32))
    base = dict(
        wq=np.ascontiguousarray(np.asarray(L0_attn_Wq, np.float32)),
        wk=np.ascontiguousarray(np.asarray(L0_attn_Wk, np.float32)),
        wv=np.ascontiguousarray(np.asarray(L0_attn_Wv, np.float32)),
        wo=np.ascontiguousarray(np.asarray(L0_attn_Wo, np.float32)),
        lq1=np.ascontiguousarray(np.asarray(L0_attn_lq1, np.float32)),
        lq2=np.ascontiguousarray(np.asarray(L0_attn_lq2, np.float32)),
        lk1=np.ascontiguousarray(np.asarray(L0_attn_lk1, np.float32)),
        lk2=np.ascontiguousarray(np.asarray(L0_attn_lk2, np.float32)),
        rmsw=np.ascontiguousarray(np.asarray(L0_attn_rms_w, np.float32).reshape(128, 1)),
        rw=np.ascontiguousarray(np.asarray(L0_ffn_router_w, np.float32)),
        rb=np.ascontiguousarray(np.asarray(L0_ffn_router_b, np.float32).reshape(1, 16)),
        ew=np.ascontiguousarray(np.asarray(L0_ffn_expert_w, np.float32)),
        eb=np.ascontiguousarray(np.asarray(L0_ffn_expert_b, np.float32).reshape(1, DM)),
        **_CONSTS,
    )
    pidx = np.arange(128)
    in_maps = []
    for r in range(NCORES):
        m = dict(base)
        m["x_in"] = np.ascontiguousarray(x[r])
        m["selr"] = (pidx // 16 == r).astype(np.float32).reshape(128, 1)
        in_maps.append(m)
    res = run_bass_kernel_spmd(nc, in_maps, core_ids=list(range(NCORES)))
    return np.stack([res.results[r]["y_out"] for r in range(NCORES)], axis=0)


# version 08
# speedup vs baseline: 1.4707x; 1.0127x over previous
import math
import numpy as np
from contextlib import ExitStack

import concourse.bass as bass
import concourse.mybir as mybir
from concourse.bass_utils import run_bass_kernel_spmd

F32 = mybir.dt.float32
BF16 = mybir.dt.bfloat16
U32 = mybir.dt.uint32
AF = mybir.ActivationFunctionType
ALU = mybir.AluOpType
AX = mybir.AxisListType

SEQ = 2048
DM = 2048
NT = 16
ND = 16
H = 16
KVH = 4
NEXP = 16
TOPK = 1024
NCORES = 8
CEXP = 1.0 / math.sqrt(128.0)
LAM_INIT = 0.2
NEG_BIG = -1.0e30
BIGV = 1.0e6
SEARCH_ITERS = 34
RMS_SC = 0.8 * math.sqrt(128.0)


def make_consts():
    c = {}
    c["ident"] = np.eye(128, dtype=np.float32)
    j = np.arange(0, 256, 2, dtype=np.float32)
    theta = (np.float32(1.0) / (np.float32(10000.0) ** (j / np.float32(256.0)))).astype(np.float32)
    pos = np.arange(SEQ, dtype=np.float32)
    ang = pos[:, None] * theta[None, :]
    cosv = np.cos(ang).astype(np.float32)
    sinv = np.sin(ang).astype(np.float32)
    CF = np.empty((SEQ, 256), dtype=np.float32)
    SF = np.empty((SEQ, 256), dtype=np.float32)
    CF[:, 0::2] = cosv
    CF[:, 1::2] = cosv
    SF[:, 0::2] = -sinv
    SF[:, 1::2] = sinv
    c["ropec"] = CF
    c["ropes"] = SF
    p = np.arange(128)[:, None]
    cc = np.arange(128)[None, :]
    c["emask"] = np.where(cc <= p, 0.0, NEG_BIG).astype(np.float32)
    kk = np.arange(128)[:, None]
    qq = np.arange(128)[None, :]
    c["maskt"] = (qq >= kk).astype(np.float32)
    c["maskcnt"] = (127.0 - np.arange(128, dtype=np.float32)).reshape(128, 1)
    pidx = np.arange(128)
    c["mgrp"] = (pidx[:, None] % 16 == pidx[None, :] % 16).astype(np.float32)
    c["repl16"] = (np.arange(16)[:, None] == pidx[None, :] % 16).astype(np.float32)
    c["ones_row"] = np.ones((1, 128), dtype=np.float32)
    mc = np.zeros((128, 4), dtype=np.float32)
    mc[:, 0] = 1e-5
    mc[:, 1] = LAM_INIT
    mc[:, 2] = math.log(0.8 * math.sqrt(128.0))
    mc[:, 3] = 2.0
    c["cmisc"] = mc
    return c


class Ctr:
    def __init__(self, ctx, nc, name):
        self.h = ctx.enter_context(nc.semaphore(name))
        self.n = 0


def build_program(dbg=False, stop=5, sub=3):
    import os as _os0
    KSKIP = set((_os0.environ.get("KSKIP") or "").split(","))
    import os as _os
    nc = bass.Bass("TRN2", target_bir_lowering=False,
                   detect_race_conditions=bool(_os.environ.get("KSTRICT")))
    ctx = ExitStack()

    dri = lambda n, sh: nc.dram_tensor(n, sh, F32, kind="ExternalInput").ap()
    x_in = dri("x_in", [SEQ, DM])
    wq = dri("wq", [DM, 4096])
    wk = dri("wk", [DM, 1024])
    wv = dri("wv", [DM, 512])
    wo = dri("wo", [DM, DM])
    lq1 = dri("lq1", [16, 128])
    lq2 = dri("lq2", [16, 128])
    lk1 = dri("lk1", [16, 128])
    lk2 = dri("lk2", [16, 128])
    rmsw = dri("rmsw", [128, 1])
    rw = dri("rw", [DM, 16])
    rb = dri("rb", [1, 16])
    ew = dri("ew", [DM, DM])
    eb = dri("eb", [1, DM])
    ident = dri("ident", [128, 128])
    ropec = dri("ropec", [SEQ, 256])
    ropes = dri("ropes", [SEQ, 256])
    emask = dri("emask", [128, 128])
    maskt = dri("maskt", [128, 128])
    maskcnt = dri("maskcnt", [128, 1])
    mgrp = dri("mgrp", [128, 128])
    repl16 = dri("repl16", [16, 128])
    ones_row = dri("ones_row", [1, 128])
    selr = dri("selr", [128, 1])
    cmisc = dri("cmisc", [128, 4])

    y_out = nc.dram_tensor("y_out", [SEQ, DM], F32, kind="ExternalOutput").ap()
    ik = dict(kind="ExternalOutput") if dbg else {}
    x1r = nc.dram_tensor("x1r", [SEQ, DM], F32, **ik).ap()
    x2r = nc.dram_tensor("x2r", [SEQ, DM], F32, **ik).ap()
    attr = nc.dram_tensor("attr", [DM, SEQ], BF16, **ik).ap()
    cc_in = [nc.dram_tensor(f"cc_in{k}", [16, SEQ], F32) for k in range(2)]
    cc_out = [nc.dram_tensor(f"cc_out{k}", [128, SEQ], F32) for k in range(2)]

    sbt = lambda n, sh, dt: ctx.enter_context(nc.sbuf_tensor(n, sh, dt))
    pst = lambda n, sh, dt: ctx.enter_context(nc.psum_tensor(n, sh, dt))

    BIG = sbt("BIG", [128, ND * SEQ], BF16)
    WBUF = sbt("WBUF", [128, ND * DM], BF16)
    KT0 = 0
    WSLOT = 16384
    VS0 = 24576
    SCR = sbt("SCR", [128, 8192], BF16)
    qTh = sbt("qTh", [128, 2 * SEQ], BF16)
    xt = [sbt(f"xt{i}", [128, DM], F32) for i in range(2)]
    e0b = xt[0][:, :].bitcast(BF16)[:, 0:SEQ]
    e1b = xt[0][:, :].bitcast(BF16)[:, SEQ:2 * SEQ]
    e0c = xt[1][:, :].bitcast(BF16)[:, 0:SEQ]
    e1c = xt[1][:, :].bitcast(BF16)[:, SEQ:2 * SEQ]
    zt = sbt("zt", [128, DM], F32)
    e0d = zt[:, :].bitcast(BF16)[:, 0:SEQ]
    e1d = zt[:, :].bitcast(BF16)[:, SEQ:2 * SEQ]
    xn = sbt("xn", [128, DM], BF16)
    e0 = sbt("e0", [128, SEQ], BF16)
    e1 = sbt("e1", [128, SEQ], BF16)
    sall = qTh[:, :].bitcast(F32)
    snapt = sbt("snapt", [128, SEQ], F32)
    qrw = sbt("qrw", [128, 1024], BF16)
    qrt = sbt("qrt", [128, 1024], BF16)
    uneg = qrw
    dchk = qrt
    attsc = sbt("attsc", [128, 640], BF16)
    qro = attsc[:, 0:512]
    ST = snapt[0:16, :]
    lqk = [snapt[0:16, i * 128:(i + 1) * 128] for i in range(4)]
    sown = sbt("sown", [128, 64], F32)
    stat = sbt("stat", [128, 64], F32)
    srch = sbt("srch", [128, 32], F32)
    srchu = sbt("srchu", [128, 1], U32)
    srow = sbt("srow", [1, 16], F32)
    wcol = sbt("wcol", [128, 16], F32)
    sstat = sbt("sstat", [128, 96], F32)
    lamc = sbt("lamc", [128, 16], F32)
    lsc = sbt("lsc", [16, 8], F32)
    c_id_bf = sbt("c_id_bf", [128, 128], BF16)
    c_id_f = sbt("c_id_f", [128, 128], F32)
    c_emask = sbt("c_emask", [128, 128], BF16)
    c_maskt = sbt("c_maskt", [128, 128], BF16)
    c_mcnt = sbt("c_mcnt", [128, 1], F32)
    c_mgrp = sbt("c_mgrp", [128, 128], F32)
    c_repl = sbt("c_repl", [16, 128], F32)
    c_ones = sbt("c_ones", [1, 128], F32)
    c_selr = sbt("c_selr", [128, 1], F32)
    c_rb = sbt("c_rb", [1, 16], F32)
    c_ebbf = e1[0:1, 0:DM]
    jnk_bf = snapt[:, 0:1024].bitcast(BF16)
    jnk_bf2 = snapt[:, 1024:2048].bitcast(BF16)
    c_rmsw = sbt("c_rmsw", [128, 1], F32)
    c_misc = sbt("c_misc", [128, 4], F32)
    rwsb = sbt("rwsb", [128, ND * 16], BF16)
    c_onesb = c_maskt[0:1, :]
    c_rope_c = SCR[:, 0:NT * 256]
    c_rope_s = SCR[:, NT * 256:2 * NT * 256]

    pE0 = pst("pE0", [128, 1024], F32)
    pE1 = pst("pE1", [128, 1024], F32)
    pA = pst("pA", [128, 512], F32)
    pB = pst("pB", [128, 512], F32)
    pTR = pst("pTR", [128, 2048], BF16)

    pS_cnt = pE0[:, 0:1]
    pS_thx = pE0[:, 1:2]
    pS_lam = pE0[:, 2:18]
    pS_w = pE0[:, 18:34]
    pS_st = [pE0[0:16, 34:162], pE0[0:16, 162:290]]
    pS_r0 = pE0[0:1, 290:418]
    pS_r16 = pE0[0:16, 418:419]
    pS_lr = pE0[0:1, 420:436]

    dma = {n: Ctr(ctx, nc, n) for n in (
        "smc", "sx0", "sx1", "ssl0", "ssl1", "gmc", "geb", "gst", "gsall",
        "grope", "gwk", "gwv", "gwq0", "gwq1", "gzs", "gat0", "gat1")}
    SX = [dma["sx0"], dma["sx1"]]
    SSL = [dma["ssl0"], dma["ssl1"]]
    GWQ = [dma["gwq0"], dma["gwq1"]]
    GAT = [dma["gat0"], dma["gat1"]]
    ccs = Ctr(ctx, nc, "ccs")
    tmm = Ctr(ctx, nc, "tmm")
    pml = Ctr(ctx, nc, "pml")
    vml = Ctr(ctx, nc, "vml")
    sml = Ctr(ctx, nc, "sml")

    prog = {k: [] for k in ("sync", "gpsimd", "tensor", "vector", "scalar")}
    S_ = prog["sync"].append
    G_ = prog["gpsimd"].append
    T_ = prog["tensor"].append
    V_ = prog["vector"].append
    C_ = prog["scalar"].append

    guard = {}

    def need(e, g):
        if g is None:
            return
        if isinstance(g, list):
            for gg in g:
                need(e, gg)
        else:
            e.wait_ge(g[0], g[1])

    def gget(key):
        return guard.get(key)

    state_tr = [0]

    def _c0(e):
        e.dma_start(out=c_id_f[:], in_=ident).then_inc(dma["smc"].h, 16)
        e.dma_start(out=c_mcnt[:], in_=maskcnt).then_inc(dma["smc"].h, 16)
        e.dma_start(out=c_mgrp[:], in_=mgrp).then_inc(dma["smc"].h, 16)
        e.dma_start(out=c_repl[:], in_=repl16).then_inc(dma["smc"].h, 16)
        e.dma_start(out=c_ones[:], in_=ones_row).then_inc(dma["smc"].h, 16)
        e.dma_start(out=c_rb[:], in_=rb).then_inc(dma["smc"].h, 16)
        e.dma_start(out=c_selr[:], in_=selr).then_inc(dma["smc"].h, 16)
        e.dma_start(out=c_rmsw[:], in_=rmsw).then_inc(dma["smc"].h, 16)
        e.dma_start(out=c_misc[:], in_=cmisc).then_inc(dma["smc"].h, 16)
    S_(_c0)
    dma["smc"].n += 9 * 16
    SL_CONST = dma["smc"].n

    def _c1(e):
        e.dma_start(out=c_id_bf[:], in_=ident).then_inc(dma["gmc"].h, 16)
        e.dma_start(out=c_emask[:], in_=emask).then_inc(dma["gmc"].h, 16)
        e.dma_start(out=c_maskt[:], in_=maskt).then_inc(dma["gmc"].h, 16)
        e.dma_start(out=rwsb[:].rearrange("p (t c) -> p t c", c=16),
                        in_=rw.rearrange("(t p) c -> p t c", p=128)).then_inc(dma["gmc"].h, 16)
        e.dma_start(out=WBUF[:].rearrange("p (t c) -> p t c", c=2048),
                        in_=ew.rearrange("(t p) c -> p t c", p=128)).then_inc(dma["gmc"].h, 16)
    G_(_c1)
    dma["gmc"].n += 5 * 16
    GL_CONST = dma["gmc"].n

    def ln_stats(xtile, waits, par):
        b = par * 16

        def f(e, xtile=xtile, waits=list(waits) + [gget(("stats", par))], b=b):
            for g in waits:
                need(e, g)
            e.activation(out=jnk_bf, in_=xtile[:], func=AF.Identity,
                         accum_out=stat[:, b + 0:b + 1])
            e.activation(out=jnk_bf2, in_=xtile[:], func=AF.Square,
                         accum_out=stat[:, b + 1:b + 2])
            e.copy(out=stat[:, b + 10:b + 11],
                   in_=stat[:, b + 1:b + 2]).then_inc(sml.h, 1)
        C_(f)
        sml.n += 1
        return sml.n

    def ln_normalize(xtile, s_stats, par):
        b = par * 16

        def fA(e, s_stats=s_stats, b=b, g=gget(("statA", par))):
            e.wait_ge(sml.h, s_stats)
            need(e, g)
            e.activation(out=stat[:, b + 2:b + 3], in_=stat[:, b + 0:b + 1],
                         func=AF.Identity, scale=1.0 / DM)
            e.activation(out=stat[:, b + 3:b + 4], in_=stat[:, b + 1:b + 2],
                         func=AF.Identity, scale=1.0 / DM).then_inc(sml.h, 1)
        C_(fA)
        sml.n += 1
        s_A = sml.n

        def fB(e, s_A=s_A, b=b, g=gget(("statB", par))):
            e.wait_ge(sml.h, s_A)
            need(e, g)
            e.tensor_scalar(out=stat[:, b + 4:b + 5], in0=stat[:, b + 2:b + 3],
                            scalar1=stat[:, b + 2:b + 3],
                            scalar2=stat[:, b + 3:b + 4],
                            op0=ALU.mult, op1=ALU.subtract).then_inc(vml.h, 1)
        V_(fB)
        vml.n += 1
        v_B = vml.n
        guard[("stats", par)] = (vml.h, v_B)

        def fC(e, v_B=v_B, b=b):
            e.wait_ge(vml.h, v_B)
            e.activation(out=stat[:, b + 5:b + 6], in_=stat[:, b + 4:b + 5],
                         func=AF.Ln, scale=-1.0,
                         bias=c_misc[:, 0:1]).then_inc(sml.h, 1)
        C_(fC)
        sml.n += 1
        s_C = sml.n

        def fD(e, s_C=s_C, b=b):
            e.wait_ge(sml.h, s_C)
            e.tensor_copy(out=stat[:, b + 7:b + 8],
                          in_=stat[:, b + 5:b + 6]).then_inc(vml.h, 1)
        V_(fD)
        vml.n += 1
        v_D = vml.n

        def fE(e, v_D=v_D, b=b):
            e.wait_ge(vml.h, v_D)
            e.activation(out=stat[:, b + 6:b + 7], in_=stat[:, b + 7:b + 8],
                         func=AF.Exp, scale=-0.5).then_inc(sml.h, 1)
        C_(fE)
        sml.n += 1
        s_E = sml.n
        guard[("statB", par)] = (sml.h, s_E)

        def fF(e, xtile=xtile, s_E=s_E, b=b, g=gget("xn")):
            e.wait_ge(sml.h, s_E)
            need(e, g)
            e.tensor_scalar(out=xn[:], in0=xtile[:], scalar1=stat[:, b + 2:b + 3],
                            scalar2=stat[:, b + 6:b + 7], op0=ALU.subtract,
                            op1=ALU.mult).then_inc(vml.h, 1)
        V_(fF)
        vml.n += 1
        guard[("statA", par)] = (vml.h, vml.n)
        return vml.n

    def transpose_to_big(i, v_xn):
        for bq in range(4):
            half = state_tr[0] % 2
            state_tr[0] += 1

            def ft(e, bq=bq, half=half, v_xn=v_xn, g=gget(("pTR", half))):
                e.wait_ge(vml.h, v_xn)
                need(e, g)
                for j in range(4):
                    dt = bq * 4 + j
                    ins = e.transpose(pTR[:, half * 1024 + j * 128:half * 1024 + (j + 1) * 128],
                                      xn[:, dt * 128:(dt + 1) * 128], c_id_bf[:])
                ins.then_inc(tmm.h, 1)
            T_(ft)
            tmm.n += 1
            t_tr = tmm.n

            def fe(e, bq=bq, half=half, t_tr=t_tr, i=i):
                e.wait_ge(tmm.h, t_tr)
                out_ap = BIG[:, :].rearrange("p (t q) -> p t q", q=SEQ)[
                    :, bq * 4:(bq + 1) * 4, i * 128:(i + 1) * 128]
                e.activation(out=out_ap, in_=pTR[:, half * 1024:half * 1024 + 512],
                             func=AF.Copy).then_inc(sml.h, 1)
            C_(fe)
            sml.n += 1
            guard[("pTR", half)] = (sml.h, sml.n)
        guard["xn"] = (tmm.h, tmm.n)
        return sml.n

    def barrier():
        t1, v1, s1, p1 = tmm.n, vml.n, sml.n, pml.n
        dsnap = [(d.h, d.n) for d in dma.values() if d.n > 0]

        def wt(e, t1=t1, v1=v1, s1=s1, p1=p1, dsnap=list(dsnap)):
            e.wait_ge(tmm.h, t1)
            e.wait_ge(vml.h, v1)
            e.wait_ge(sml.h, s1)
            if p1 > 0:
                e.wait_ge(pml.h, p1)
            for h, n in dsnap:
                e.wait_ge(h, n)
        for EM in (T_, V_, C_, G_, S_):
            EM(wt)
        guard.clear()

    def ffn_phase(call, src_res, dst_res, fuse_next_ln, big_tile_s, xt_free):
        cin, cout = cc_in[call], cc_out[call]

        geb0 = dma["geb"].n

        def feb(e, call=call):
            e.dma_start(out=c_ebbf, in_=eb).then_inc(dma["geb"].h, 16)
            if call == 1:
                for dt in range(ND):
                    e.dma_start(out=WBUF[:, dt * DM:(dt + 1) * DM],
                                in_=ew[dt * 128:(dt + 1) * 128, :]
                                ).then_inc(dma["geb"].h, 16)
        G_(feb)
        dma["geb"].n += 16 * (1 + (ND if call == 1 else 0))
        g_eb = dma["geb"].n

        st_prev = {}
        for i in range(NT):
            pR = pA if i % 2 == 0 else pB
            par = i % 2
            sb = par * 16

            def fr(e, i=i, pR=pR, bs=big_tile_s[i], g=gget(id(pR))):
                e.wait_ge(sml.h, bs)
                if i == 0:
                    e.wait_ge(dma["gmc"].h, GL_CONST)
                    e.wait_ge(dma["smc"].h, SL_CONST)
                need(e, g)
                for dt in range(ND):
                    e.matmul(pR[:, 0:16],
                             BIG[:, dt * SEQ + i * 128:dt * SEQ + (i + 1) * 128],
                             rwsb[:, dt * 16:(dt + 1) * 16],
                             start=(dt == 0), stop=False)
                e.matmul(pR[:, 0:16], c_ones[:], c_rb[:], start=False,
                         stop=True).then_inc(tmm.h, 1)
            T_(fr)
            tmm.n += 1
            t_lg = tmm.n

            def fx(e, pR=pR, t_lg=t_lg, sb=sb, par=par,
                   g=[gget(("rtacc", par))]):
                e.wait_ge(tmm.h, t_lg)
                need(e, g)
                e.activation(out=sown[:, sb:sb + 16], in_=pR[:, 0:16], func=AF.Exp,
                             accum_out=stat[:, 32 + par:33 + par])
                e.copy(out=stat[:, 40 + par:41 + par],
                       in_=stat[:, 32 + par:33 + par]).then_inc(sml.h, 1)
            C_(fx)
            sml.n += 1
            s_exp = sml.n
            guard[id(pR)] = (sml.h, sml.n)

            def fn(e, s_exp=s_exp, par=par):
                e.wait_ge(sml.h, s_exp)
                e.reciprocal(out=stat[:, 34 + par:35 + par],
                             in_=stat[:, 32 + par:33 + par]).then_inc(vml.h, 1)
            V_(fn)
            vml.n += 1
            v_sm = vml.n
            guard[("rtacc", par)] = (vml.h, v_sm)

            def fnm(e, v_sm=v_sm, sb=sb, par=par, g=gget(("sownn", par))):
                e.wait_ge(vml.h, v_sm)
                need(e, g)
                e.activation(out=sown[:, 32 + sb:32 + sb + 16],
                             in_=sown[:, sb:sb + 16], func=AF.Identity,
                             scale=stat[:, 34 + par:35 + par]).then_inc(sml.h, 1)
            C_(fnm)
            sml.n += 1
            s_nm = sml.n

            def fst(e, par=par, sb=sb, s_nm=s_nm, g=gget("bankS")):
                e.wait_ge(sml.h, s_nm)
                need(e, g)
                e.transpose(pS_st[par], sown[:, 32 + sb:32 + sb + 16],
                            c_id_f[:]).then_inc(tmm.h, 1)
            T_(fst)
            tmm.n += 1
            t_st = tmm.n

            def fse(e, i=i, par=par, t_st=t_st):
                e.wait_ge(tmm.h, t_st)
                e.activation(out=ST[0:16, i * 128:(i + 1) * 128], in_=pS_st[par],
                             func=AF.Copy).then_inc(sml.h, 1)
            C_(fse)
            sml.n += 1
            guard[("sownn", par)] = (tmm.h, t_st)
            guard["bankS"] = (sml.h, sml.n)
        st_all = sml.n

        def fcc(e, st_all=st_all, cin=cin, cout=cout, gs0=dma["gst"].n, cc0=ccs.n):
            e.wait_ge(sml.h, st_all)
            e.dma_start(out=cin[:, :], in_=ST[0:16, :]).then_inc(dma["gst"].h, 16)
            e.wait_ge(dma["gst"].h, gs0 + 16)
            e.collective_compute(
                "AllGather", ALU.bypass,
                replica_groups=[list(range(NCORES))],
                ins=[cin.ap()], outs=[cout.ap()],
            ).then_inc(ccs.h, 1)
            e.wait_ge(ccs.h, cc0 + 1)
            e.dma_start(out=sall[:], in_=cout[:, :]).then_inc(dma["gsall"].h, 16)
        G_(fcc)
        dma["gst"].n += 16
        ccs.n += 1
        dma["gsall"].n += 16
        g_sall = dma["gsall"].n

        def fs0(e, g_sall=g_sall):
            e.wait_ge(dma["gsall"].h, g_sall)
            e.memset(srch[:, :], 0.0)
            e.memset(srch[:, 2:3], 1.0)
            e.memset(srch[:, 4:5], 0.5).then_inc(vml.h, 1)
        V_(fs0)
        vml.n += 1

        for it in range(SEARCH_ITERS):
            par = it % 2
            nxt = 1 - par
            v_mid = vml.n

            def fmb(e, v_mid=v_mid, par=par):
                e.wait_ge(vml.h, v_mid)
                e.copy(out=srch[:, 12:13], in_=srch[:, 4 + par:5 + par]).then_inc(sml.h, 1)
            C_(fmb)
            sml.n += 1
            s_mid = sml.n

            def fsa(e, s_mid=s_mid):
                e.wait_ge(sml.h, s_mid)
                e.tensor_scalar(out=e0[:], in0=sall[:], scalar1=srch[:, 12:13],
                                scalar2=0.0, op0=ALU.is_ge, op1=ALU.add,
                                accum_out=srch[:, 13:14])
                e.tensor_copy(out=srch[:, 14:15], in_=srch[:, 13:14]).then_inc(vml.h, 1)
            V_(fsa)
            vml.n += 1
            v_cnt = vml.n

            def fsb(e, v_cnt=v_cnt, g=gget("bankS")):
                e.wait_ge(vml.h, v_cnt)
                need(e, g)
                e.matmul(pS_cnt, c_mgrp[:], srch[:, 13:14], start=True,
                         stop=True).then_inc(tmm.h, 1)
            T_(fsb)
            tmm.n += 1
            t_cnt = tmm.n

            def fsA(e, t_cnt=t_cnt, par=par):
                e.wait_ge(tmm.h, t_cnt)
                e.tensor_scalar(out=srch[:, 8:9], in0=pS_cnt, scalar1=float(TOPK),
                                scalar2=None, op0=ALU.is_ge)
                e.tensor_tensor(out=srch[:, 6:7], in0=srch[:, 4 + par:5 + par],
                                in1=srch[:, 0 + par:1 + par], op=ALU.subtract)
                e.tensor_tensor(out=srch[:, 7:8], in0=srch[:, 2 + par:3 + par],
                                in1=srch[:, 4 + par:5 + par],
                                op=ALU.subtract).then_inc(vml.h, 1)
            V_(fsA)
            vml.n += 1
            v_A = vml.n
            guard["bankS"] = (vml.h, v_A)

            def fsB(e, v_A=v_A):
                e.wait_ge(vml.h, v_A)
                e.copy(out=srch[:, 9:10], in_=srch[:, 8:9])
                e.copy(out=srch[:, 10:11], in_=srch[:, 6:7])
                e.copy(out=srch[:, 11:12], in_=srch[:, 7:8]).then_inc(sml.h, 1)
            C_(fsB)
            sml.n += 1
            s_B = sml.n

            def fsC(e, s_B=s_B, par=par, nxt=nxt):
                e.wait_ge(sml.h, s_B)
                e.scalar_tensor_tensor(out=srch[:, 0 + nxt:1 + nxt],
                                       in0=srch[:, 10:11], scalar=srch[:, 9:10],
                                       in1=srch[:, 0 + par:1 + par],
                                       op0=ALU.mult, op1=ALU.add)
                e.scalar_tensor_tensor(out=srch[:, 2 + nxt:3 + nxt],
                                       in0=srch[:, 11:12], scalar=srch[:, 9:10],
                                       in1=srch[:, 4 + par:5 + par],
                                       op0=ALU.mult, op1=ALU.add).then_inc(vml.h, 1)
            V_(fsC)
            vml.n += 1
            v_C = vml.n

            def fsD(e, v_C=v_C, nxt=nxt):
                e.wait_ge(vml.h, v_C)
                e.activation(out=srch[:, 10:11], in_=srch[:, 0 + nxt:1 + nxt],
                             func=AF.Identity, scale=0.5)
                e.activation(out=srch[:, 11:12], in_=srch[:, 2 + nxt:3 + nxt],
                             func=AF.Identity, scale=0.5).then_inc(sml.h, 1)
            C_(fsD)
            sml.n += 1
            s_D = sml.n

            def fsE(e, s_D=s_D, nxt=nxt):
                e.wait_ge(sml.h, s_D)
                e.tensor_tensor(out=srch[:, 4 + nxt:5 + nxt], in0=srch[:, 10:11],
                                in1=srch[:, 11:12], op=ALU.add).then_inc(vml.h, 1)
            V_(fsE)
            vml.n += 1
        LOF = SEARCH_ITERS % 2

        v_lo = vml.n

        def flb(e, v_lo=v_lo, LOF=LOF):
            e.wait_ge(vml.h, v_lo)
            e.copy(out=srch[:, 15:16], in_=srch[:, LOF:LOF + 1]).then_inc(sml.h, 1)
        C_(flb)
        sml.n += 1
        s_lo = sml.n

        def fsn1(e, s_lo=s_lo):
            e.wait_ge(sml.h, s_lo)
            e.scalar_tensor_tensor(out=e0[:], in0=sall[:], scalar=srch[:, 15:16],
                                   in1=sall[:], op0=ALU.is_ge,
                                   op1=ALU.bypass).then_inc(vml.h, 1)
        V_(fsn1)
        vml.n += 1
        v_n1 = vml.n

        def fsn2(e, v_n1=v_n1):
            e.wait_ge(vml.h, v_n1)
            e.activation(out=xn[:], in_=e0[:], func=AF.Identity, scale=-2.0,
                         bias=c_misc[:, 3:4]).then_inc(sml.h, 1)
        C_(fsn2)
        sml.n += 1
        s_n2 = sml.n

        def fsn3(e, s_n2=s_n2):
            e.wait_ge(sml.h, s_n2)
            e.tensor_tensor(out=snapt[:], in0=sall[:], in1=xn[:], op=ALU.add)
            e.tensor_reduce(out=srch[:, 6:7], in_=snapt[:], axis=AX.X,
                            op=ALU.min).then_inc(vml.h, 1)
        V_(fsn3)
        vml.n += 1
        v_cm = vml.n

        def fst2(e, v_cm=v_cm, g=gget("bankS")):
            e.wait_ge(vml.h, v_cm)
            need(e, g)
            e.transpose(pS_r0, srch[:, 6:7], c_id_f[:]).then_inc(tmm.h, 1)
        T_(fst2)
        tmm.n += 1
        t_cm = tmm.n

        def fmx(e, t_cm=t_cm):
            e.wait_ge(tmm.h, t_cm)
            e.tensor_reduce(out=srow[:], in_=pS_r0.rearrange("p (r e) -> p e r", r=8),
                            axis=AX.X, op=ALU.min).then_inc(vml.h, 1)
        V_(fmx)
        vml.n += 1
        guard["bankS"] = (vml.h, vml.n)
        v_th = vml.n

        def fth(e, v_th=v_th, g=gget("bankS")):
            e.wait_ge(vml.h, v_th)
            need(e, g)
            e.transpose(pS_r16, srow[:], c_id_f[0:1, 0:1]).then_inc(tmm.h, 1)
        T_(fth)
        tmm.n += 1
        t_th = tmm.n

        def fthc(e, t_th=t_th):
            e.wait_ge(tmm.h, t_th)
            e.tensor_copy(out=srch[0:16, 7:8], in_=pS_r16).then_inc(vml.h, 1)
        V_(fthc)
        vml.n += 1
        guard["bankS"] = (vml.h, vml.n)
        v_thc = vml.n

        def fthx(e, v_thc=v_thc, g=gget("bankS")):
            e.wait_ge(vml.h, v_thc)
            need(e, g)
            e.matmul(pS_thx, c_repl[:], srch[0:16, 7:8], start=True,
                     stop=True).then_inc(tmm.h, 1)
        T_(fthx)
        tmm.n += 1
        t_thx = tmm.n

        def fw0(e, t_thx=t_thx):
            e.wait_ge(tmm.h, t_thx)
            e.activation(out=srch[:, 8:9], in_=pS_thx, func=AF.Copy).then_inc(sml.h, 1)
        C_(fw0)
        sml.n += 1
        s_th = sml.n
        guard["bankS"] = (sml.h, sml.n)

        def fw1(e, s_th=s_th):
            e.wait_ge(sml.h, s_th)
            e.scalar_tensor_tensor(out=snapt[:], in0=sall[:], scalar=srch[:, 8:9],
                                   in1=sall[:], op0=ALU.is_ge,
                                   op1=ALU.mult).then_inc(vml.h, 1)
        V_(fw1)
        vml.n += 1
        v_msk = vml.n

        def fw2(e, v_msk=v_msk, g=gget("bankS")):
            e.wait_ge(vml.h, v_msk)
            need(e, g)
            for i in range(NT):
                ins = e.matmul(pS_w[:, i:i + 1], snapt[:, i * 128:(i + 1) * 128],
                               c_selr[:], start=True, stop=True)
            ins.then_inc(tmm.h, 1)
        T_(fw2)
        tmm.n += 1
        t_w = tmm.n

        def fw3(e, t_w=t_w):
            e.wait_ge(tmm.h, t_w)
            e.activation(out=wcol[:], in_=pS_w, func=AF.Identity,
                         scale=0.5).then_inc(sml.h, 1)
        C_(fw3)
        sml.n += 1
        guard["bankS"] = (sml.h, sml.n)
        v_w = None
        s_w = sml.n

        y_evac = {}
        apply_v = dict(xt_free)
        store_g = {}
        big_next = [None] * NT
        zt_guard = [gget("zt"), gget("zt_store")]

        def emit_y(i):
            slot = (i % 4) * 2048
            for cch in range(4):
                pY = pA if (i * 4 + cch) % 2 == 0 else pB

                def fy(e, i=i, cch=cch, pY=pY, bs=big_tile_s[i], g=gget(id(pY)),
                       geb=(g_eb if i == 0 and cch == 0 else None)):
                    e.wait_ge(sml.h, bs)
                    if geb is not None:
                        e.wait_ge(dma["geb"].h, geb)
                    need(e, g)
                    for dt in range(ND):
                        e.matmul(pY[:],
                                 BIG[:, dt * SEQ + i * 128:dt * SEQ + (i + 1) * 128],
                                 WBUF[:, dt * DM + cch * 512:dt * DM + (cch + 1) * 512],
                                 start=(dt == 0), stop=False)
                    e.matmul(pY[:], c_onesb, c_ebbf[:, cch * 512:(cch + 1) * 512],
                             start=False, stop=True).then_inc(tmm.h, 1)
                T_(fy)
                tmm.n += 1
                t_y = tmm.n

                def fyev(e, i=i, cch=cch, pY=pY, t_y=t_y, slot=slot,
                         w=apply_v.get(i - 4)):
                    e.wait_ge(tmm.h, t_y)
                    if w is not None:
                        e.wait_ge(vml.h, w)
                    e.activation(out=SCR[:, slot + cch * 512:slot + (cch + 1) * 512],
                                 in_=pY[:], func=AF.Copy).then_inc(sml.h, 1)
                C_(fyev)
                sml.n += 1
                guard[id(pY)] = (sml.h, sml.n)
            y_evac[i] = sml.n

        def emit_apply(i):
            slot = (i % 4) * 2048
            xb = xt[i % 2]

            def fxl(e, i=i, xb=xb, w=apply_v.get(i - 2), sh=SX[i % 2].h):
                if w is not None:
                    e.wait_ge(vml.h, w)
                e.dma_start(out=xb[:], in_=src_res[i * 128:(i + 1) * 128, :]
                            ).then_inc(sh, 16)
            S_(fxl)
            SX[i % 2].n += 16
            sl_x = (SX[i % 2].h, SX[i % 2].n)

            def fap(e, i=i, xb=xb, slot=slot, sl_x=sl_x, sy=max(y_evac[i], s_w),
                    zg=list(zt_guard)):
                e.wait_ge(sml.h, sy)
                e.wait_ge(sl_x[0], sl_x[1])
                for g in zg:
                    need(e, g)
                e.scalar_tensor_tensor(out=zt[:], in0=SCR[:, slot:slot + 2048],
                                       scalar=wcol[:, i:i + 1], in1=xb[:],
                                       op0=ALU.mult, op1=ALU.add).then_inc(vml.h, 1)
            V_(fap)
            vml.n += 1
            v_z = vml.n
            apply_v[i] = v_z

            def fzs(e, i=i, v_z=v_z):
                e.wait_ge(vml.h, v_z)
                e.dma_start(out=dst_res[i * 128:(i + 1) * 128, :], in_=zt[:]
                            ).then_inc(dma["gzs"].h, 16)
            G_(fzs)
            dma["gzs"].n += 16
            g_z = dma["gzs"].n
            store_g[i] = g_z

            if fuse_next_ln:
                s_st = ln_stats(zt, [(vml.h, v_z)], i % 2)
                v_xn = ln_normalize(zt, s_st, i % 2)
                s_big = transpose_to_big(i, v_xn)
                big_next[i] = s_big
                zt_guard[0] = (vml.h, v_xn)
                zt_guard[1] = (dma["gzs"].h, g_z)
            else:
                zt_guard[0] = (dma["gzs"].h, g_z)
                zt_guard[1] = None

        for i in range(NT):
            emit_y(i)
            if i >= 3:
                emit_apply(i - 3)
        for i in range(NT - 3, NT):
            emit_apply(i)
        guard["zt"] = zt_guard[0]
        guard["zt_store"] = zt_guard[1]
        return big_next

    def initial_ln():
        big_tile_s = [None] * NT
        xn_done = {}
        for i in range(NT):
            xb = xt[i % 2]

            def fl(e, i=i, xb=xb, w=xn_done.get(i - 2), sh=SX[i % 2].h):
                if w is not None:
                    e.wait_ge(vml.h, w)
                e.dma_start(out=xb[:], in_=x_in[i * 128:(i + 1) * 128, :]
                            ).then_inc(sh, 16)
            S_(fl)
            SX[i % 2].n += 16
            s_st = ln_stats(xb, [(SX[i % 2].h, SX[i % 2].n)], i % 2)
            if sub < 2:
                xn_done[i] = vml.n
                big_tile_s[i] = sml.n
                continue
            v_xn = ln_normalize(xb, s_st, i % 2)
            xn_done[i] = v_xn
            if sub < 3:
                big_tile_s[i] = sml.n
                continue
            big_tile_s[i] = transpose_to_big(i, v_xn)
        xt_free = {-2: xn_done[NT - 2], -1: xn_done[NT - 1]}
        return big_tile_s, xt_free

    def kq_rope_group(i, pP, colw, dst_col_fn, t_mm, rope_state):
        ngr = colw // 256

        def fr1(e, i=i, ngr=ngr, pP=pP, t_mm=t_mm, gr=rope_state.get("g"),
                w=rope_state.get("v2")):
            e.wait_ge(tmm.h, t_mm)
            e.wait_ge(dma["grope"].h, gr)
            if w is not None:
                e.wait_ge(pml.h, w)
            CF = c_rope_c[:, i * 256:(i + 1) * 256]
            SF3 = c_rope_s[:, i * 256:(i + 1) * 256].rearrange(
                "p (j t) -> p j t", t=2)
            for g2 in range(ngr):
                b = g2 * 256
                pg3 = pP[:, b:b + 256].rearrange("p (j t) -> p j t", t=2)
                q3 = qrt[:, 512 + b:512 + b + 256].rearrange("p (j t) -> p j t", t=2)
                e.tensor_tensor(out=qrt[:, b:b + 256], in0=pP[:, b:b + 256],
                                in1=CF, op=ALU.mult)
                e.tensor_tensor(out=q3[:, :, 0:1], in0=pg3[:, :, 1:2],
                                in1=SF3[:, :, 0:1], op=ALU.mult)
                ins = e.tensor_tensor(out=q3[:, :, 1:2], in0=pg3[:, :, 0:1],
                                      in1=SF3[:, :, 1:2], op=ALU.mult)
            ins.then_inc(vml.h, 1)
        V_(fr1)
        vml.n += 1
        v_r1 = vml.n

        def fr2(e, colw=colw, v_r1=v_r1, w=rope_state.get("t")):
            e.wait_ge(vml.h, v_r1)
            if w is not None:
                e.wait_ge(tmm.h, w)
            e.tensor_tensor(out=qro[:, 0:colw], in0=qrt[:, 0:colw],
                            in1=qrt[:, 512:512 + colw],
                            op=ALU.add).then_inc(pml.h, 1)
        G_(fr2)
        pml.n += 1
        p_r2 = pml.n
        rope_state["v2"] = p_r2

        ntr = colw // 128
        half = state_tr[0] % 2
        state_tr[0] += 1

        def ftr(e, ntr=ntr, half=half, p_r2=p_r2, g=gget(("pTR", half))):
            e.wait_ge(pml.h, p_r2)
            need(e, g)
            for j in range(ntr):
                ins = e.transpose(pTR[:, half * 1024 + j * 128:half * 1024 + (j + 1) * 128],
                                  qro[:, j * 128:(j + 1) * 128], c_id_bf[:])
            ins.then_inc(tmm.h, 1)
        T_(ftr)
        tmm.n += 1
        t_tr = tmm.n

        def fte(e, ntr=ntr, half=half, t_tr=t_tr, dst=dst_col_fn):
            e.wait_ge(tmm.h, t_tr)
            for j in range(ntr):
                e.activation(out=dst(j), in_=pTR[:, half * 1024 + j * 128:
                                               half * 1024 + (j + 1) * 128],
                             func=AF.Copy).then_inc(sml.h, 1)
        C_(fte)
        sml.n += ntr
        guard[("pTR", half)] = (sml.h, sml.n)
        return sml.n, v_r1

    def attention_phase(nheads=H, nqi=NT):
        rope_state = {}
        state_u = {}
        state_d = {}
        state_dr = {}
        state_att = {}
        state_att2 = {}
        state_pe = [0]
        state_uh = [0]

        def flq(e):
            e.dma_start(out=lqk[0], in_=lq1).then_inc(dma["smc"].h, 16)
            e.dma_start(out=lqk[1], in_=lq2).then_inc(dma["smc"].h, 16)
            e.dma_start(out=lqk[2], in_=lk1).then_inc(dma["smc"].h, 16)
            e.dma_start(out=lqk[3], in_=lk2).then_inc(dma["smc"].h, 16)
        S_(flq)
        dma["smc"].n += 64
        sl_lqk = dma["smc"].n

        def fl1(e, sl_lqk=sl_lqk):
            e.wait_ge(dma["smc"].h, sl_lqk)
            e.scalar_tensor_tensor(out=lqk[0], in0=lqk[0], scalar=1.0,
                                   in1=lqk[2], op0=ALU.mult, op1=ALU.mult,
                                   accum_out=lsc[:, 0:1])
            e.scalar_tensor_tensor(out=lqk[1], in0=lqk[1], scalar=1.0,
                                   in1=lqk[3], op0=ALU.mult, op1=ALU.mult,
                                   accum_out=lsc[:, 1:2])
            e.tensor_copy(out=lsc[:, 6:7], in_=lsc[:, 1:2]).then_inc(vml.h, 1)
        V_(fl1)
        vml.n += 1
        v_l1 = vml.n

        def fl2(e, v_l1=v_l1):
            e.wait_ge(vml.h, v_l1)
            e.activation(out=lsc[:, 2:4], in_=lsc[:, 0:2], func=AF.Exp).then_inc(sml.h, 1)
        C_(fl2)
        sml.n += 1
        s_l2 = sml.n

        def fl3(e, s_l2=s_l2):
            e.wait_ge(sml.h, s_l2)
            e.tensor_tensor(out=lsc[:, 4:5], in0=lsc[:, 2:3], in1=lsc[:, 3:4],
                            op=ALU.subtract).then_inc(vml.h, 1)
        V_(fl3)
        vml.n += 1
        v_l3a = vml.n

        def fl3b(e, v_l3a=v_l3a):
            e.wait_ge(vml.h, v_l3a)
            e.activation(out=lsc[:, 5:6], in_=lsc[:, 4:5], func=AF.Identity,
                         bias=c_misc[0:16, 1:2]).then_inc(sml.h, 1)
        C_(fl3b)
        sml.n += 1
        v_l3 = sml.n

        def fl4(e, v_l3=v_l3):
            e.wait_ge(sml.h, v_l3)
            e.transpose(pS_lr, lsc[:, 5:6], c_id_f[0:16, 0:16]).then_inc(tmm.h, 1)
        T_(fl4)
        tmm.n += 1
        t_l4 = tmm.n

        def fl5(e, t_l4=t_l4):
            e.wait_ge(tmm.h, t_l4)
            e.tensor_copy(out=srow[:], in_=pS_lr).then_inc(vml.h, 1)
        V_(fl5)
        vml.n += 1
        v_l5 = vml.n

        def fl6(e, v_l5=v_l5):
            e.wait_ge(vml.h, v_l5)
            e.matmul(pS_lam, c_ones[:], srow[:], start=True,
                     stop=True).then_inc(tmm.h, 1)
        T_(fl6)
        tmm.n += 1
        t_l6 = tmm.n

        def fl7(e, t_l6=t_l6):
            e.wait_ge(tmm.h, t_l6)
            e.tensor_copy(out=lamc[:], in_=pS_lam).then_inc(vml.h, 1)
        V_(fl7)
        vml.n += 1
        v_lam = vml.n

        def frt(e):
            e.dma_start(out=SCR[:, 0:NT * 256].rearrange("p (t c) -> p t c", c=256),
                        in_=ropec.rearrange("(t p) c -> p t c", p=128)
                        ).then_inc(dma["grope"].h, 16)
            e.dma_start(out=SCR[:, NT * 256:2 * NT * 256].rearrange("p (t c) -> p t c", c=256),
                        in_=ropes.rearrange("(t p) c -> p t c", p=128)
                        ).then_inc(dma["grope"].h, 16)
        G_(frt)
        dma["grope"].n += 32
        rope_state["g"] = dma["grope"].n

        def fwk(e):
            e.dma_start(out=WBUF[:, WSLOT:WSLOT + ND * 1024].rearrange("p (t c) -> p t c", c=1024),
                        in_=wk.rearrange("(t p) c -> p t c", p=128)
                        ).then_inc(dma["gwk"].h, 16)
        G_(fwk)
        dma["gwk"].n += 16
        g_wk = dma["gwk"].n

        pend_k = None
        kq_steps = [(i, cch) for i in range(NT) for cch in range(2)] + [None]
        for stp in kq_steps:
            if stp is not None:
                i, cch = stp
                pP = pA if (i * 2 + cch) % 2 == 0 else pB

                def fkm(e, i=i, cch=cch, pP=pP, g_wk=g_wk, g=gget(id(pP))):
                    e.wait_ge(dma["gwk"].h, g_wk)
                    need(e, g)
                    for dt in range(ND):
                        ins = e.matmul(pP[:],
                                       BIG[:, dt * SEQ + i * 128:dt * SEQ + (i + 1) * 128],
                                       WBUF[:, WSLOT + dt * 1024 + cch * 512:
                                            WSLOT + dt * 1024 + (cch + 1) * 512],
                                       start=(dt == 0), stop=(dt == ND - 1))
                    ins.then_inc(tmm.h, 1)
                T_(fkm)
                tmm.n += 1
                cur_k = (i, cch, pP, tmm.n)
            else:
                cur_k = None
            if pend_k is None:
                pend_k = cur_k
                continue
            i, cch, pP, t_mm = pend_k
            pend_k = cur_k

            def dstf(j, i=i, cch=cch):
                kc = cch * 4 + j
                return WBUF[:, KT0 + kc * SEQ + i * 128:KT0 + kc * SEQ + (i + 1) * 128]
            s_done, v_ev = kq_rope_group(i, pP, 512, dstf, t_mm, rope_state)
            guard[id(pP)] = (vml.h, v_ev)
        kT_ready = sml.n

        def fwv(e, kT_ready=kT_ready):
            e.wait_ge(sml.h, kT_ready)
            e.dma_start(out=WBUF[:, WSLOT:WSLOT + ND * 512].rearrange("p (t c) -> p t c", c=512),
                        in_=wv.rearrange("(t p) c -> p t c", p=128)
                        ).then_inc(dma["gwv"].h, 16)
        G_(fwv)
        dma["gwv"].n += 16
        g_wv = dma["gwv"].n

        for i in range(NT):
            pP = pA if i % 2 == 0 else pB

            def fvm(e, i=i, pP=pP, g_wv=g_wv, g=gget(id(pP))):
                e.wait_ge(dma["gwv"].h, g_wv)
                need(e, g)
                for dt in range(ND):
                    ins = e.matmul(pP[:], BIG[:, dt * SEQ + i * 128:dt * SEQ + (i + 1) * 128],
                                   WBUF[:, WSLOT + dt * 512:WSLOT + (dt + 1) * 512],
                                   start=(dt == 0), stop=(dt == ND - 1))
                ins.then_inc(tmm.h, 1)
            T_(fvm)
            tmm.n += 1
            t_v = tmm.n

            def fve(e, i=i, pP=pP, t_v=t_v):
                e.wait_ge(tmm.h, t_v)
                e.activation(out=WBUF[:, VS0 + i * 512:VS0 + (i + 1) * 512],
                             in_=pP[:], func=AF.Copy).then_inc(sml.h, 1)
            C_(fve)
            sml.n += 1
            guard[id(pP)] = (sml.h, sml.n)
        v_ready = sml.n

        WQS = [WSLOT, WSLOT + ND * 256]
        wq_user = {}
        wq_g = {}

        def emit_wq(hh):
            def fwq(e, hh=hh, slot=WQS[hh % 2], w=wq_user.get(hh - 2),
                    vr=v_ready):
                if hh < 2:
                    e.wait_ge(sml.h, vr)
                if w is not None:
                    e.wait_ge(tmm.h, w)
                e.dma_start(out=WBUF[:, WQS[hh % 2]:WQS[hh % 2] + ND * 256]
                            .rearrange("p (t c) -> p t c", c=256),
                            in_=wq[:, hh * 256:(hh + 1) * 256].rearrange(
                                "(t p) c -> p t c", p=128)
                            ).then_inc(GWQ[hh % 2].h, 16)
            G_(fwq)
            GWQ[hh % 2].n += 16
            wq_g[hh] = (GWQ[hh % 2].h, GWQ[hh % 2].n)

        for h in range(nheads):
            slot = WQS[h % 2]
            kv = h // 4
            if h < 2:
                emit_wq(h)
            g_wq = wq_g[h]

            pend_q = None
            qev = {}
            for i in list(range(NT)) + [None]:
                if i is not None:
                    pP = pA if i % 2 == 0 else pB

                    def fqm(e, i=i, pP=pP, slot=slot, g_wq=g_wq, g=gget(id(pP)),
                            qg=gget("qTh") if i == 0 else None):
                        e.wait_ge(g_wq[0], g_wq[1])
                        need(e, g)
                        need(e, qg)
                        for dt in range(ND):
                            ins = e.matmul(pP[:, 0:256],
                                           BIG[:, dt * SEQ + i * 128:dt * SEQ + (i + 1) * 128],
                                           WBUF[:, slot + dt * 256:slot + (dt + 1) * 256],
                                           start=(dt == 0), stop=(dt == ND - 1))
                        ins.then_inc(tmm.h, 1)
                    T_(fqm)
                    tmm.n += 1
                    cur_q = (i, pP, tmm.n)
                else:
                    cur_q = None
                if pend_q is None:
                    pend_q = cur_q
                    continue
                i_p, pP, t_mm = pend_q
                pend_q = cur_q

                def dstq(j, i=i_p):
                    return qTh[:, j * SEQ + i * 128:j * SEQ + (i + 1) * 128]
                s_done, v_ev = kq_rope_group(i_p, pP, 256, dstq, t_mm, rope_state)
                qev[i_p] = s_done
                guard[id(pP)] = (vml.h, v_ev)
            wq_user[h] = tmm.n
            if h + 2 < nheads:
                emit_wq(h + 2)
            q_ready = sml.n

            EB0 = [e0, e0b, e0c, e0d]
            EB1 = [e1, e1b, e1c, e1d]
            for grp in range(nqi // 4):
                qis = [4 * grp + j for j in range(4)]
                nchs = [((qi + 1) * 128 + 511) // 512 for qi in qis]

                for j, qi in enumerate(qis):
                    klen = (qi + 1) * 128
                    for kc in range(nchs[j]):
                        wch = min(512, klen - kc * 512)
                        isdiag = (kc == nchs[j] - 1)
                        half = state_pe[0] % 2
                        state_pe[0] += 1
                        hb = half * 512
                        hE0 = pE0[:, hb:hb + wch]
                        hE1 = pE1[:, hb:hb + wch]

                        def fem(e, qi=qi, kc=kc, wch=wch, isdiag=isdiag,
                                hE0=hE0, hE1=hE1, kv=kv,
                                g0=gget(("pE", half)),
                                kr=(qev.get(qi) if kc == 0 else None),
                                lr=(v_lam if (h == 0 and qi == 0) else None)):
                            if kr is not None:
                                e.wait_ge(sml.h, kr)
                            if lr is not None:
                                e.wait_ge(vml.h, lr)
                            need(e, g0)
                            for a, hE in ((0, hE0), (1, hE1)):
                                kcg = kv * 2 + a
                                ins = e.matmul(hE,
                                               qTh[:, a * SEQ + qi * 128:
                                                   a * SEQ + (qi + 1) * 128],
                                               WBUF[:, KT0 + kcg * SEQ + kc * 512:
                                                    KT0 + kcg * SEQ + kc * 512 + wch],
                                               start=True, stop=not isdiag)
                                if isdiag:
                                    ins = e.matmul(hE[:, wch - 128:wch], c_id_bf[:],
                                                   c_emask[:], start=False, stop=True)
                            ins.then_inc(tmm.h, 1)
                        T_(fem)
                        tmm.n += 1
                        t_e = tmm.n

                        def fex(e, j=j, kc=kc, wch=wch, hE0=hE0, hE1=hE1, t_e=t_e,
                                g=(gget(("eb", j)) if kc == 0 else None)):
                            e.wait_ge(tmm.h, t_e)
                            need(e, g)
                            e.activation(out=EB0[j][:, kc * 512:kc * 512 + wch],
                                         in_=hE0, func=AF.Exp, scale=CEXP)
                            e.activation(out=EB1[j][:, kc * 512:kc * 512 + wch],
                                         in_=hE1, func=AF.Exp,
                                         scale=CEXP).then_inc(sml.h, 1)
                        C_(fex)
                        sml.n += 1
                        guard[("pE", half)] = (sml.h, sml.n)
                        s_fex = sml.n

                        def fexr(e, j=j, kc=kc, wch=wch, s_fex=s_fex):
                            e.wait_ge(sml.h, s_fex)
                            e.tensor_reduce(out=sstat[:, 32 + 8 * j + kc:
                                                      33 + 8 * j + kc],
                                            in_=EB0[j][:, kc * 512:kc * 512 + wch],
                                            axis=AX.X, op=ALU.add)
                            e.tensor_reduce(out=sstat[:, 36 + 8 * j + kc:
                                                      37 + 8 * j + kc],
                                            in_=EB1[j][:, kc * 512:kc * 512 + wch],
                                            axis=AX.X, op=ALU.add).then_inc(vml.h, 1)
                        V_(fexr)
                        vml.n += 1
                v_stageA = vml.n

                def fg1(e, nchs=nchs, v_stageA=v_stageA):
                    e.wait_ge(vml.h, v_stageA)
                    for j in range(4):
                        nch = nchs[j]
                        e.activation(out=srch[:, 12:12 + nch],
                                     in_=sstat[:, 32 + 8 * j:32 + 8 * j + nch],
                                     func=AF.Identity,
                                     accum_out=srch[:, 0 + j:1 + j])
                        e.activation(out=srch[:, 12:12 + nch],
                                     in_=sstat[:, 32 + 8 * j:32 + 8 * j + nch],
                                     func=AF.Identity, scale=-1.0,
                                     accum_out=srch[:, 4 + j:5 + j])
                        e.activation(out=srch[:, 12:12 + nch],
                                     in_=sstat[:, 36 + 8 * j:36 + 8 * j + nch],
                                     func=AF.Identity,
                                     accum_out=srch[:, 8 + j:9 + j])
                    e.copy(out=srch[:, 12:13], in_=srch[:, 0:1]).then_inc(sml.h, 1)
                C_(fg1)
                sml.n += 1
                s_g1 = sml.n

                def fg2(e, s_g1=s_g1):
                    e.wait_ge(sml.h, s_g1)
                    e.reciprocal(out=srch[:, 16:20], in_=srch[:, 4:8])
                    e.reciprocal(out=srch[:, 20:24],
                                 in_=srch[:, 8:12]).then_inc(vml.h, 1)
                V_(fg2)
                vml.n += 1
                v_g2 = vml.n

                def fg3(e, v_g2=v_g2):
                    e.wait_ge(vml.h, v_g2)
                    e.copy(out=srch[:, 24:28], in_=srch[:, 20:24]).then_inc(sml.h, 1)
                C_(fg3)
                sml.n += 1
                s_g3 = sml.n

                def fg4(e, s_g3=s_g3, h=h):
                    e.wait_ge(sml.h, s_g3)
                    for j in range(4):
                        ins = e.tensor_scalar(out=wcol[:, j:j + 1],
                                              in0=lamc[:, h:h + 1],
                                              scalar1=srch[:, 0 + j:1 + j],
                                              scalar2=srch[:, 24 + j:25 + j],
                                              op0=ALU.mult, op1=ALU.mult)
                    ins.then_inc(vml.h, 1)
                V_(fg4)
                vml.n += 1
                v_g4 = vml.n

                def fg5(e, v_g4=v_g4):
                    e.wait_ge(vml.h, v_g4)
                    e.copy(out=wcol[:, 4:8], in_=wcol[:, 0:4]).then_inc(sml.h, 1)
                C_(fg5)
                sml.n += 1
                s_gb = sml.n

                pend_pe = []
                steps = []
                for j, qi in enumerate(qis):
                    klen = (qi + 1) * 128
                    for kc in range(nchs[j]):
                        steps.append((j, qi, kc, min(512, klen - kc * 512),
                                      kc == nchs[j] - 1))
                pend = None
                for st in steps + [None]:
                    if st is not None:
                        j, qi, kc, wch, isdiag = st
                        half_u = state_uh[0] % 2
                        state_uh[0] += 1
                        ub = half_u * 512

                        def fcb(e, j=j, kc=kc, wch=wch, ub=ub, s_gb=s_gb,
                                w=state_u.get(half_u)):
                            e.wait_ge(sml.h, max(s_gb, w if w is not None else 0))
                            e.scalar_tensor_tensor(out=uneg[:, ub:ub + wch],
                                                   in0=EB1[j][:, kc * 512:
                                                              kc * 512 + wch],
                                                   scalar=wcol[:, 4 + j:5 + j],
                                                   in1=EB0[j][:, kc * 512:
                                                              kc * 512 + wch],
                                                   op0=ALU.mult,
                                                   op1=ALU.subtract
                                                   ).then_inc(vml.h, 1)
                        V_(fcb)
                        vml.n += 1
                        if kc == nchs[j] - 1:
                            guard[("eb", j)] = (vml.h, vml.n)
                        cur = (st, half_u, ub, vml.n)
                    else:
                        cur = None
                    if pend is None:
                        pend = cur
                        continue
                    (j, qi, kc, wch, isdiag), half_u, ub, v_u = pend
                    pend = cur

                    def fed(e, j=j, kc=kc, wch=wch, ub=ub, v_u=v_u,
                            w=state_d.get(half_u), w2=state_dr.get(half_u)):
                        e.wait_ge(vml.h, v_u)
                        if w is not None:
                            e.wait_ge(tmm.h, w)
                        if w2 is not None:
                            e.wait_ge(vml.h, w2)
                        e.activation(out=dchk[:, ub:ub + wch],
                                     in_=uneg[:, ub:ub + wch],
                                     func=AF.Exp, scale=srch[:, 16 + j:17 + j]
                                     ).then_inc(sml.h, 1)
                    C_(fed)
                    sml.n += 1
                    state_u[half_u] = sml.n
                    s_d = sml.n

                    def fedr(e, j=j, kc=kc, wch=wch, ub=ub, s_d=s_d):
                        e.wait_ge(sml.h, s_d)
                        e.tensor_reduce(out=sstat[:, 64 + 8 * j + kc:
                                                  65 + 8 * j + kc],
                                        in_=dchk[:, ub:ub + wch],
                                        axis=AX.X, op=ALU.add).then_inc(vml.h, 1)
                    V_(fedr)
                    vml.n += 1
                    state_dr[half_u] = vml.n

                    ntr = wch // 128
                    half = state_tr[0] % 2
                    state_tr[0] += 1

                    def ftd(e, ntr=ntr, ub=ub, half=half, s_d=s_d,
                            g=gget(("pTR", half))):
                        e.wait_ge(sml.h, s_d)
                        need(e, g)
                        for jt in range(ntr):
                            ins = e.transpose(pTR[:, half * 1024 + jt * 128:
                                                  half * 1024 + (jt + 1) * 128],
                                              dchk[:, ub + jt * 128:
                                                   ub + (jt + 1) * 128],
                                              c_id_bf[:])
                        ins.then_inc(tmm.h, 1)
                    T_(ftd)
                    tmm.n += 1
                    t_d = tmm.n
                    state_d[half_u] = t_d

                    ncopy = ntr - 1 if isdiag else ntr
                    if ncopy > 0:
                        def fde(e, kc=kc, ncopy=ncopy, half=half, t_d=t_d):
                            e.wait_ge(tmm.h, t_d)
                            e.tensor_copy(out=xn[:, kc * 512:
                                                 kc * 512 + ncopy * 128],
                                          in_=pTR[:, half * 1024:
                                                   half * 1024 + ncopy * 128]
                                          ).then_inc(vml.h, 1)
                        V_(fde)
                        vml.n += 1
                    if isdiag:
                        def fdm(e, ntr=ntr, half=half, t_d=t_d, qi=qi):
                            e.wait_ge(tmm.h, t_d)
                            e.tensor_tensor(out=xn[:, qi * 128:(qi + 1) * 128],
                                            in0=pTR[:, half * 1024 +
                                                    (ntr - 1) * 128:
                                                    half * 1024 + ntr * 128],
                                            in1=c_maskt[:],
                                            op=ALU.mult).then_inc(vml.h, 1)
                        V_(fdm)
                        vml.n += 1
                    guard[("pTR", half)] = (vml.h, vml.n)
                    if not isdiag:
                        continue
                    s_dT_s, s_dT_v = sml.n, vml.n
                    pV = pA if qi % 2 == 0 else pB

                    def fpv(e, qi=qi, kv=kv, pV=pV, s_dT_s=s_dT_s, s_dT_v=s_dT_v,
                            g=gget(id(pV)),
                            vr=(v_ready if (h == 0 and qi == 0) else None)):
                        e.wait_ge(sml.h, s_dT_s)
                        e.wait_ge(vml.h, s_dT_v)
                        if vr is not None:
                            e.wait_ge(sml.h, vr)
                        need(e, g)
                        for kt in range(qi + 1):
                            ins = e.matmul(pV[:, 0:128],
                                           xn[:, kt * 128:(kt + 1) * 128],
                                           WBUF[:, VS0 + kt * 512 + kv * 128:
                                                VS0 + kt * 512 + (kv + 1) * 128],
                                           start=(kt == 0), stop=(kt == qi))
                        ins.then_inc(tmm.h, 1)
                    T_(fpv)
                    tmm.n += 1
                    pend_pe.append((qi, pV, tmm.n))

                    if len(pend_pe) > 1:
                        qi_p, pV_p, t_pv_p = pend_pe.pop(0)

                        def fpe(e, qi=qi_p, pV=pV_p, t_pv=t_pv_p,
                                g=gget(("attA", qi_p))):
                            e.wait_ge(tmm.h, t_pv)
                            need(e, g)
                            e.activation(out=attsc[:, 256:384], in_=pV[:, 0:128],
                                         func=AF.Square,
                                         accum_out=sstat[:, qi:qi + 1])
                            e.activation(out=snapt[:, qi * 128:(qi + 1) * 128],
                                         in_=pV[:, 0:128],
                                         func=AF.Copy).then_inc(sml.h, 1)
                        C_(fpe)
                        sml.n += 1
                        guard[id(pV_p)] = (sml.h, sml.n)

                while pend_pe:
                    qi_p, pV_p, t_pv_p = pend_pe.pop(0)

                    def fpe(e, qi=qi_p, pV=pV_p, t_pv=t_pv_p,
                            g=gget(("attA", qi_p))):
                        e.wait_ge(tmm.h, t_pv)
                        need(e, g)
                        e.activation(out=attsc[:, 256:384], in_=pV[:, 0:128],
                                     func=AF.Square,
                                     accum_out=sstat[:, qi:qi + 1])
                        e.activation(out=snapt[:, qi * 128:(qi + 1) * 128],
                                     in_=pV[:, 0:128],
                                     func=AF.Copy).then_inc(sml.h, 1)
                    C_(fpe)
                    sml.n += 1
                    guard[id(pV_p)] = (sml.h, sml.n)
                s_pe_last = sml.n

                def fsd(e, qis=qis, nchs=nchs, s_pe_last=s_pe_last):
                    e.wait_ge(sml.h, s_pe_last)
                    for j, qi in enumerate(qis):
                        nch = nchs[j]
                        if nch > 1:
                            ins = e.tensor_reduce(out=sstat[:, 16 + qi:17 + qi],
                                                  in_=sstat[:, 64 + 8 * j:
                                                            64 + 8 * j + nch],
                                                  axis=AX.X, op=ALU.add)
                        else:
                            ins = e.tensor_copy(out=sstat[:, 16 + qi:17 + qi],
                                                in_=sstat[:, 64 + 8 * j:
                                                          65 + 8 * j])
                    ins.then_inc(vml.h, 1)
                V_(fsd)
                vml.n += 1
                guard["sacc"] = (vml.h, vml.n)
            v_head = vml.n
            s_head = sml.n

            if "pv" in KSKIP or "comb" in KSKIP or "head" in KSKIP:
                continue
            def fscA(e, v_head=v_head, s_head=s_head):
                e.wait_ge(vml.h, v_head)
                e.wait_ge(sml.h, s_head)
                e.copy(out=sstat[:, 32:48], in_=sstat[:, 16:32]).then_inc(sml.h, 1)
            C_(fscA)
            sml.n += 1
            sA = sml.n

            def fscB(e, sA=sA):
                e.wait_ge(sml.h, sA)
                e.tensor_scalar(out=sstat[:, 16:32], in0=sstat[:, 32:48],
                                scalar1=c_mcnt[:], scalar2=None,
                                op0=ALU.subtract).then_inc(vml.h, 1)
            V_(fscB)
            vml.n += 1
            vB = vml.n

            def fscC(e, vB=vB):
                e.wait_ge(vml.h, vB)
                e.copy(out=sstat[:, 32:48], in_=sstat[:, 16:32]).then_inc(sml.h, 1)
            C_(fscC)
            sml.n += 1
            sC = sml.n

            def fscD(e, sC=sC):
                e.wait_ge(sml.h, sC)
                e.tensor_tensor(out=sstat[:, 48:64], in0=sstat[:, 32:48],
                                in1=sstat[:, 32:48], op=ALU.mult).then_inc(vml.h, 1)
            V_(fscD)
            vml.n += 1
            vD = vml.n

            def fscE(e, vD=vD):
                e.wait_ge(vml.h, vD)
                e.activation(out=sstat[:, 32:48], in_=sstat[:, 48:64],
                             func=AF.Identity, scale=128.0e-8).then_inc(sml.h, 1)
            C_(fscE)
            sml.n += 1
            sE = sml.n

            def fscF(e, sE=sE):
                e.wait_ge(sml.h, sE)
                e.tensor_tensor(out=sstat[:, 48:64], in0=sstat[:, 32:48],
                                in1=sstat[:, 0:16], op=ALU.add).then_inc(vml.h, 1)
            V_(fscF)
            vml.n += 1
            vF = vml.n

            def fscG(e, vF=vF):
                e.wait_ge(vml.h, vF)
                e.activation(out=sstat[:, 32:48], in_=sstat[:, 48:64],
                             func=AF.Ln).then_inc(sml.h, 1)
            C_(fscG)
            sml.n += 1
            sG = sml.n

            def fscH(e, sG=sG):
                e.wait_ge(sml.h, sG)
                e.tensor_copy(out=sstat[:, 48:64],
                              in_=sstat[:, 32:48]).then_inc(vml.h, 1)
            V_(fscH)
            vml.n += 1
            vH = vml.n

            def fscI(e, vH=vH):
                e.wait_ge(vml.h, vH)
                e.activation(out=sstat[:, 0:16], in_=sstat[:, 48:64], func=AF.Exp,
                             scale=-0.5, bias=c_misc[:, 2:3]).then_inc(sml.h, 1)
            C_(fscI)
            sml.n += 1
            v_sc3 = sml.n
            guard["sstat_e"] = (sml.h, v_sc3)

            pend_ae = None
            for qi in (list(range(nqi)) + [None] if "evac" not in KSKIP
                       and "pv" not in KSKIP and "comb" not in KSKIP else []):
                if qi is not None:
                    sl = (qi % 2) * 128
                    so = 384 + (qi % 2) * 128

                    def fas(e, qi=qi, sl=sl, v_sc3=v_sc3,
                            w=state_att.get(qi % 2)):
                        e.wait_ge(sml.h, v_sc3)
                        if w is not None:
                            e.wait_ge(tmm.h, w)
                        e.tensor_scalar(out=attsc[:, sl:sl + 128],
                                        in0=snapt[:, qi * 128:(qi + 1) * 128],
                                        scalar1=sstat[:, 0 + qi:1 + qi],
                                        scalar2=None,
                                        op0=ALU.mult).then_inc(vml.h, 1)
                    V_(fas)
                    vml.n += 1
                    v_as = vml.n
                    guard[("attA", qi)] = (vml.h, v_as)

                    half = state_tr[0] % 2
                    state_tr[0] += 1

                    def fat(e, sl=sl, half=half, v_as=v_as,
                            g=gget(("pTR", half))):
                        e.wait_ge(vml.h, v_as)
                        need(e, g)
                        e.transpose(pTR[:, half * 1024:half * 1024 + 128],
                                    attsc[:, sl:sl + 128],
                                    c_id_bf[:]).then_inc(tmm.h, 1)
                    T_(fat)
                    tmm.n += 1
                    state_att[qi % 2] = tmm.n
                    cur_ae = (qi, so, half, tmm.n)
                else:
                    cur_ae = None
                if pend_ae is None:
                    pend_ae = cur_ae
                    continue
                qi_p, so, half, t_at = pend_ae
                pend_ae = cur_ae

                def fae(e, so=so, half=half, t_at=t_at,
                        w=state_att2.get(qi_p % 2)):
                    e.wait_ge(tmm.h, t_at)
                    if w is not None:
                        e.wait_ge(w[0], w[1])
                    e.tensor_copy(out=attsc[:, so:so + 128],
                                  in_=pTR[:, half * 1024:half * 1024 + 128]
                                  ).then_inc(vml.h, 1)
                V_(fae)
                vml.n += 1
                s_ae = vml.n
                guard[("pTR", half)] = (vml.h, s_ae)

                def fad(e, h=h, qi=qi_p, so=so, s_ae=s_ae):
                    e.wait_ge(vml.h, s_ae)
                    e.dma_start(out=attr[h * 128:(h + 1) * 128,
                                         qi * 128:(qi + 1) * 128],
                                in_=attsc[:, so:so + 128]).then_inc(GAT[qi % 2].h, 16)
                G_(fad)
                GAT[qi_p % 2].n += 16
                state_att2[qi_p % 2] = (GAT[qi_p % 2].h, GAT[qi_p % 2].n)
            guard["qTh"] = (tmm.h, tmm.n)
            rope_state["t"] = tmm.n

    def wo_phase():
        g_wo0 = dma["gwk"].n

        def fwo(e):
            for dt in range(ND):
                e.dma_start(out=WBUF[:, dt * DM:(dt + 1) * DM],
                            in_=wo[dt * 128:(dt + 1) * 128, :]
                            ).then_inc(dma["gwk"].h, 16)
        G_(fwo)
        dma["gwk"].n += 16 * ND

        def fws(e, gw=g_wo0 + 16 * ND):
            e.wait_ge(dma["gwk"].h, gw)
            for dt in range(ND):
                ins = e.tensor_scalar(out=WBUF[:, dt * DM:(dt + 1) * DM],
                                      in0=WBUF[:, dt * DM:(dt + 1) * DM],
                                      scalar1=c_rmsw[:], scalar2=None,
                                      op0=ALU.mult)
            ins.then_inc(vml.h, 1)
        V_(fws)
        vml.n += 1
        v_wo = vml.n

        apply_v = {}
        big_tile_s = [None] * NT
        zt_guard = [None, None]
        for i in range(NT):
            sbase = (i % 2) * 2048

            def fsl(e, i=i, sbase=sbase, w=apply_v.get(i - 2)):
                if w is not None:
                    e.wait_ge(vml.h, w)
                e.dma_start(out=SCR[:, sbase:sbase + 2048].rearrange(
                                "p (t c) -> p t c", c=128),
                            in_=attr[:, i * 128:(i + 1) * 128].rearrange(
                                "(t p) c -> p t c", p=128)).then_inc(SSL[i % 2].h, 16)
            S_(fsl)
            SSL[i % 2].n += 16

            xb = xt[i % 2]

            def fx1(e, i=i, xb=xb, w=apply_v.get(i - 2)):
                if w is not None:
                    e.wait_ge(vml.h, w)
                e.dma_start(out=xb[:], in_=x1r[i * 128:(i + 1) * 128, :]
                            ).then_inc(SX[i % 2].h, 16)
            S_(fx1)
            SX[i % 2].n += 16
            sl_i = [(SSL[i % 2].h, SSL[i % 2].n), (SX[i % 2].h, SX[i % 2].n)]

            for cch in range(4):
                pW = pA if (i * 4 + cch) % 2 == 0 else pB

                def fwm(e, i=i, cch=cch, pW=pW, sbase=sbase, sl_i=sl_i,
                        v_wo=v_wo, g=gget(id(pW))):
                    for hh, nn in sl_i:
                        e.wait_ge(hh, nn)
                    if i == 0 and cch == 0:
                        e.wait_ge(vml.h, v_wo)
                    need(e, g)
                    for dt in range(ND):
                        ins = e.matmul(pW[:], SCR[:, sbase + dt * 128:sbase + (dt + 1) * 128],
                                       WBUF[:, dt * DM + cch * 512:dt * DM + (cch + 1) * 512],
                                       start=(dt == 0), stop=(dt == ND - 1))
                    ins.then_inc(tmm.h, 1)
                T_(fwm)
                tmm.n += 1
                t_w = tmm.n

                def fadd(e, cch=cch, pW=pW, xb=xb, t_w=t_w,
                         zg=(list(zt_guard) if cch == 0 else None)):
                    e.wait_ge(tmm.h, t_w)
                    if zg is not None:
                        for g in zg:
                            need(e, g)
                    e.tensor_tensor(out=zt[:, cch * 512:(cch + 1) * 512],
                                    in0=pW[:], in1=xb[:, cch * 512:(cch + 1) * 512],
                                    op=ALU.add).then_inc(vml.h, 1)
                V_(fadd)
                vml.n += 1
                guard[id(pW)] = (vml.h, vml.n)
            v_z = vml.n
            apply_v[i] = v_z

            def fzs(e, i=i, v_z=v_z):
                e.wait_ge(vml.h, v_z)
                e.dma_start(out=x2r[i * 128:(i + 1) * 128, :], in_=zt[:]
                            ).then_inc(dma["gzs"].h, 16)
            G_(fzs)
            dma["gzs"].n += 16
            g_z = dma["gzs"].n

            s_st = ln_stats(zt, [(vml.h, v_z)], i % 2)
            v_xn = ln_normalize(zt, s_st, i % 2)
            big_tile_s[i] = transpose_to_big(i, v_xn)
            zt_guard = [(vml.h, v_xn), (dma["gzs"].h, g_z)]
        guard["zt"] = zt_guard[0]
        guard["zt_store"] = zt_guard[1]
        xt_free = {-2: apply_v[NT - 2], -1: apply_v[NT - 1]}
        return big_tile_s, xt_free

    big_s, xt_free = initial_ln()
    if dbg:
        d_big1 = nc.dram_tensor("d_big1", [128, ND * SEQ], BF16,
                                kind="ExternalOutput").ap()
        _bs = big_s[NT - 1]

        def fdump1(e, _bs=_bs):
            e.wait_ge(sml.h, _bs)
            e.dma_start(out=d_big1, in_=BIG[:]).then_inc(dma["smc"].h, 16)
        S_(fdump1)
        dma["smc"].n += 16
    if stop >= 2:
        ffn_phase(0, x_in, x1r, fuse_next_ln=True, big_tile_s=big_s, xt_free=xt_free)
        barrier()
    if stop >= 3:
        nh_ = H
        nq_ = NT
        if stop == 3:
            if sub == 17:
                nh_, nq_ = 1, 0
            elif sub == 18:
                nh_, nq_ = 1, 1
            elif sub == 19:
                nh_, nq_ = 1, 4
            elif sub == 20:
                nh_, nq_ = 1, 2
            elif sub == 21:
                nh_, nq_ = 1, 3
            elif sub <= 16:
                nh_, nq_ = sub, NT
        attention_phase(nheads=nh_, nqi=nq_)
        barrier()
    if stop >= 4:
        big_s, xt_free = wo_phase()
        barrier()
    if stop >= 5:
        ffn_phase(1, x2r, y_out, fuse_next_ln=False, big_tile_s=big_s, xt_free=xt_free)

    if dbg:
        ddump = {}
        for nm, src in (("d_sall", sall), ("d_srch", srch), ("d_wcol", wcol),
                        ("d_stat", stat), ("d_snapt", snapt), ("d_sown", sown)):
            dt_ = nc.dram_tensor(nm, list(src.shape), F32, kind="ExternalOutput").ap()
            ddump[nm] = (dt_, src)
        d_big = nc.dram_tensor("d_big", [128, ND * SEQ], BF16, kind="ExternalOutput").ap()
        d_ccin = nc.dram_tensor("d_ccin", [16, SEQ], F32, kind="ExternalOutput").ap()
        d_ccout = nc.dram_tensor("d_ccout", [128, SEQ], F32, kind="ExternalOutput").ap()
        t9, v9, s9 = tmm.n, vml.n, sml.n
        dsn = [(d.h, d.n) for d in dma.values() if d.n > 0]

        def fdump(e, t9=t9, v9=v9, s9=s9, dsn=list(dsn)):
            e.wait_ge(tmm.h, t9)
            e.wait_ge(vml.h, v9)
            e.wait_ge(sml.h, s9)
            for h, n in dsn:
                e.wait_ge(h, n)
            for nm, (dt_, src) in ddump.items():
                if nm == "d_srch":
                    e.dma_start(out=dt_[:, 0:12], in_=src[:, 0:12]).then_inc(dma["smc"].h, 16)
                elif nm == "d_stat":
                    e.dma_start(out=dt_[:, 0:8], in_=src[:, 0:8]).then_inc(dma["smc"].h, 16)
                else:
                    e.dma_start(out=dt_, in_=src[:, :]).then_inc(dma["smc"].h, 16)
            e.dma_start(out=d_big, in_=BIG[:]).then_inc(dma["smc"].h, 16)
            e.dma_start(out=d_ccin, in_=cc_in[0].ap()).then_inc(dma["smc"].h, 16)
            e.dma_start(out=d_ccout, in_=cc_out[0].ap()).then_inc(dma["smc"].h, 16)
        S_(fdump)
        dma["smc"].n += 16 * 9

    dsnap_fin = [(d.h, d.n) for d in dma.values() if d.n > 0]

    def ffin(e, dsnap=list(dsnap_fin)):
        for h, n in dsnap:
            e.wait_ge(h, n)
    G_(ffin)

    with nc.Block() as blk:
        @blk.sync
        def _(e):
            for f in prog["sync"]:
                f(e)

        @blk.gpsimd
        def _(e):
            for f in prog["gpsimd"]:
                f(e)

        @blk.tensor
        def _(e):
            for f in prog["tensor"]:
                f(e)

        @blk.vector
        def _(e):
            for f in prog["vector"]:
                f(e)

        @blk.scalar
        def _(e):
            for f in prog["scalar"]:
                f(e)

    ctx.close()
    return nc


_CONSTS = make_consts()


def kernel(L0_attn_x, L0_attn_Wq, L0_attn_Wk, L0_attn_Wv, L0_attn_Wo,
           L0_attn_lq1, L0_attn_lq2, L0_attn_lk1, L0_attn_lk2, L0_attn_rms_w,
           L0_ffn_router_w, L0_ffn_router_b, L0_ffn_expert_w, L0_ffn_expert_b):
    nc = build_program()
    x = np.ascontiguousarray(np.asarray(L0_attn_x, dtype=np.float32))
    base = dict(
        wq=np.ascontiguousarray(np.asarray(L0_attn_Wq, np.float32)),
        wk=np.ascontiguousarray(np.asarray(L0_attn_Wk, np.float32)),
        wv=np.ascontiguousarray(np.asarray(L0_attn_Wv, np.float32)),
        wo=np.ascontiguousarray(np.asarray(L0_attn_Wo, np.float32)),
        lq1=np.ascontiguousarray(np.asarray(L0_attn_lq1, np.float32)),
        lq2=np.ascontiguousarray(np.asarray(L0_attn_lq2, np.float32)),
        lk1=np.ascontiguousarray(np.asarray(L0_attn_lk1, np.float32)),
        lk2=np.ascontiguousarray(np.asarray(L0_attn_lk2, np.float32)),
        rmsw=np.ascontiguousarray(np.asarray(L0_attn_rms_w, np.float32).reshape(128, 1)),
        rw=np.ascontiguousarray(np.asarray(L0_ffn_router_w, np.float32)),
        rb=np.ascontiguousarray(np.asarray(L0_ffn_router_b, np.float32).reshape(1, 16)),
        ew=np.ascontiguousarray(np.asarray(L0_ffn_expert_w, np.float32)),
        eb=np.ascontiguousarray(np.asarray(L0_ffn_expert_b, np.float32).reshape(1, DM)),
        **_CONSTS,
    )
    pidx = np.arange(128)
    in_maps = []
    for r in range(NCORES):
        m = dict(base)
        m["x_in"] = np.ascontiguousarray(x[r])
        m["selr"] = (pidx // 16 == r).astype(np.float32).reshape(128, 1)
        in_maps.append(m)
    res = run_bass_kernel_spmd(nc, in_maps, core_ids=list(range(NCORES)))
    return np.stack([res.results[r]["y_out"] for r in range(NCORES)], axis=0)
